# version 26
# baseline (speedup 1.0000x reference)
import contextlib
import numpy as np
import ml_dtypes
import concourse.bass as bass
import concourse.mybir as mybir
from concourse.bass_utils import run_bass_kernel_spmd

F32 = mybir.dt.float32
BF16 = mybir.dt.bfloat16
U8 = mybir.dt.uint8
AF = mybir.ActivationFunctionType
ALU = mybir.AluOpType
AX = mybir.AxisListType

B, T, C = 8, 2048, 1024
DEBUG = False
DEBUG_MIN = False
DEBUG_NB = 1
DEBUG_NOMM = True
L1A_PHASES = 6
L1A_P4SUB = 3
L1A_P4HEADS = 8
L1A_P4TT = 16
L1A_P4KR = 1
CH = 8
NEG = -1e38


def dram_reg(ap, t0=0, t1=1 << 30):
    return (ap.tensor.name, 0, 128, t0 * 4, t1 * 4)


def reg(h, p0=0, p1=128, b0=0, b1=None):
    name = h if isinstance(h, str) else h.name
    if b1 is None:
        b1 = 1 << 40
    return (name, p0, p1, b0, b1)


def _isect(a, b):
    return (a[0] == b[0] and a[1] < b[2] and b[1] < a[2]
            and a[3] < b[4] and b[3] < a[4])


class Prog:
    COMPUTE = ('PE', 'ACT', 'DVE', 'GP')

    SELF_EDGE = True

    def __init__(self, nc, serialize=False):
        self.nc = nc
        self.serialize = serialize
        self.ops = []
        self.events = {}
        self.deps = []

    def _record(self, engine, fn, reads, writes, stream=None):
        idx = len(self.ops)
        deps = set()
        for r in reads:
            for (i, kind, rr) in self.events.get(r[0], ()):
                if kind == 'w' and _isect(r, rr):
                    deps.add(i)
        for w in writes:
            for (i, kind, rr) in self.events.get(w[0], ()):
                if i != idx and _isect(w, rr):
                    deps.add(i)
        for r in reads:
            self.events.setdefault(r[0], []).append((idx, 'r', r))
        for w in writes:
            lst = self.events.setdefault(w[0], [])
            keep = []
            for (i, kind, rr) in lst:
                covered = (rr[1] >= w[1] and rr[2] <= w[2]
                           and rr[3] >= w[3] and rr[4] <= w[4])
                if not covered:
                    keep.append((i, kind, rr))
            keep.append((idx, 'w', w))
            self.events[w[0]] = keep
        if self.serialize and idx > 0:
            deps.add(idx - 1)
        self.ops.append(dict(engine=engine, fn=fn, stream=stream))
        self.deps.append(deps)
        return idx

    def pe(self, fn, reads, writes):
        return self._record('PE', fn, reads, writes)

    def act(self, fn, reads, writes):
        return self._record('ACT', fn, reads, writes)

    def dve(self, fn, reads, writes):
        return self._record('DVE', fn, reads, writes)

    def gpc(self, fn, reads, writes):
        return self._record('GP', fn, reads, writes)

    def dump(self, nc, name, carve, shape=None):
        shape = shape or carve.shape
        d = nc.dram_tensor(name, list(shape), carve.dtype, kind="ExternalOutput").ap()
        self.dma('SYNC', 'dbg', lambda nc_, c=carve, d=d: nc_.sync.dma_start(out=d, in_=c.ap),
                 [carve.r()], [(name, 0, 128, 0, 1 << 40)])

    def dma(self, queue, stream, fn, reads, writes):
        return self._record(queue, fn, reads, writes, stream=stream)

    def build(self):
        nc = self.nc
        n = len(self.ops)
        stream_pos = {}
        dma_index = [None] * n
        stream_before = [None] * n
        running = {}
        for i, op in enumerate(self.ops):
            stream_before[i] = dict(running)
            if op['stream'] is not None:
                k = stream_pos.setdefault(op['stream'], 0)
                dma_index[i] = k
                stream_pos[op['stream']] = k + 1
                running[op['stream']] = k + 1
        eng_pos = {}
        pos_of = [None] * n
        for i, op in enumerate(self.ops):
            e = op['engine']
            k = eng_pos.setdefault(e, 0)
            pos_of[i] = k
            eng_pos[e] = k + 1
        targets = {e: set() for e in self.COMPUTE}
        for i, op in enumerate(self.ops):
            for d in self.deps[i]:
                dop = self.ops[d]
                if dop['stream'] is not None:
                    continue
                if dop['engine'] == op['engine'] and (
                        not self.SELF_EDGE or dop['engine'] == 'PE'):
                    continue
                targets[dop['engine']].add(d)
        inc_no = {}
        for e, tset in targets.items():
            for rank, d in enumerate(sorted(tset)):
                inc_no[d] = rank + 1
        self._stack = contextlib.ExitStack()
        esem = {e: self._stack.enter_context(nc.semaphore(f"sem_{e}"))
                for e in self.COMPUTE}
        ssem = {s: self._stack.enter_context(nc.semaphore(f"sem_{s}"))
                for s in stream_pos}
        eng_obj = {'PE': nc.tensor, 'ACT': nc.scalar, 'DVE': nc.vector,
                   'SYNC': nc.sync, 'GP': nc.gpsimd}
        hiwater = {}
        for i, op in enumerate(self.ops):
            e = op['engine']
            eo = eng_obj[e]
            waits = {}
            for d in self.deps[i]:
                dop = self.ops[d]
                if dop['engine'] == e and dop['stream'] is None and (
                        not self.SELF_EDGE or e == 'PE'):
                    continue
                if dop['stream'] is not None:
                    sem = ssem[dop['stream']]
                    val = 16 * (dma_index[d] + 1)
                    key = ('s', dop['stream'])
                else:
                    sem, val = esem[dop['engine']], inc_no[d]
                    key = ('e', dop['engine'])
                if key not in waits or val > waits[key][1]:
                    waits[key] = (sem, val)
            if op['stream'] is not None and dma_index[i] > 0:
                key = ('s', op['stream'])
                v = 16 * dma_index[i]
                if (key not in waits) or waits[key][1] < v:
                    waits[key] = (ssem[op['stream']], v)
            for key, (sem, val) in sorted(waits.items()):
                hk = (e, key)
                if hiwater.get(hk, 0) >= val:
                    continue
                hiwater[hk] = val
                eo.wait_ge(sem, val)
            inst = op['fn'](nc)
            if op['stream'] is not None:
                inst.then_inc(ssem[op['stream']], 16)
            elif i in inc_no:
                inst.then_inc(esem[e], 1)
        for s, cnt in stream_pos.items():
            nc.sync.wait_ge(ssem[s], 16 * cnt)
        return self

    def close(self):
        self._stack.close()


SBUF_BYTES = 195136


class Arena:
    def __init__(self, nc, name="arena", nbytes=SBUF_BYTES):
        self.cm = nc.sbuf_tensor(name, [128, nbytes], U8)
        self.t = None
        self.nbytes = nbytes
        self.free_off = 0
        self.marks = []

    def __enter__(self):
        self.t = self.cm.__enter__()
        return self

    def __exit__(self, *a):
        return self.cm.__exit__(*a)

    def alloc(self, name, shape, dtype, align=64):
        esz = {F32: 4, BF16: 2, U8: 1, mybir.dt.uint32: 4}[dtype]
        nel = int(np.prod(shape[1:]))
        nb = ((nel * esz + align - 1) // align) * align
        off = ((self.free_off + align - 1) // align) * align
        assert off + nb <= self.nbytes, f"arena overflow at {name}: {off+nb} > {self.nbytes}"
        self.free_off = off + nb
        view = self.t[:].rearrange("p (a b c) -> p a b c", a=1, b=1)[:, 0, 0, off:off + nel * esz]
        view = view.bitcast(dtype)
        if len(shape) > 2:
            inner = " ".join(f"d{i}" for i in range(len(shape) - 1))
            view = view.rearrange(f"p ({inner}) -> p {inner}",
                                  **{f"d{i}": shape[1 + i] for i in range(len(shape) - 2)})
        return Carve(name, view, off, nb, shape, dtype)

    def mark(self):
        self.marks.append(self.free_off)

    def release(self):
        self.free_off = self.marks.pop()


class Carve:
    def __init__(self, name, ap, off, nb, shape, dtype):
        self.name = name
        self.ap = ap
        self.off = off
        self.nb = nb
        self.shape = shape
        self.dtype = dtype

    def __getitem__(self, idx):
        return self.ap[idx]

    def r(self, p0=0, p1=128, b0=0, b1=None):
        if b1 is None:
            b1 = self.nb
        return ("arena", p0, p1, self.off + b0, self.off + b1)




def host_prep_l0ffn(W_r, W_k, W_v, mu_r, mu_k):
    d = {}
    d["l0f_WrT"] = np.ascontiguousarray(W_r.T)
    d["l0f_WkT"] = np.ascontiguousarray(W_k.T)
    d["l0f_WvT"] = np.ascontiguousarray(W_v.T)
    d["l0f_mu"] = np.ascontiguousarray(
        np.stack([mu_r.reshape(CH, 128).T, mu_k.reshape(CH, 128).T], 1))
    return d


CONST_ONES_SCALED = "c_ones1024"


def host_consts():
    return {CONST_ONES_SCALED: np.full((128, 128), 1.0 / 1024, dtype=ml_dtypes.bfloat16)}


class WeightDram:

    def __init__(self, nc, name, rows, cols):
        self.name = name
        self.rows, self.cols = rows, cols
        self.ap = nc.dram_tensor(name, [rows, cols], F32, kind="ExternalInput").ap()

    def view(self):
        return self.ap.rearrange("(i p) n -> p i n", p=128)


def load_weight(P, wd: WeightDram, dst: Carve, stream="wt", nsplit=1):
    ni = wd.rows // 128
    dv = dst.ap if len(dst.shape) == 3 else dst.ap.rearrange("p (i n) -> p i n", i=ni)
    sv = wd.view()
    per = max(1, ni // nsplit)
    idxs = []
    i = 0
    while i < ni:
        j = min(ni, i + per)
        i0, j0 = i, j
        idxs.append(P.dma('GP', stream,
                          lambda nc, i0=i0, j0=j0: nc.gpsimd.dma_start(
                              out=dv[:, i0:j0, :], in_=sv[:, i0:j0, :]),
                          [reg(wd.name)],
                          [dst.r(b0=i0 * wd.cols * 2, b1=j0 * wd.cols * 2)]))
        i = j
    return idxs



def emit_ln(P, src, ydst, xbf, x2, m2buf, rstd, pm, pq, ones_bf, tb, y_bf_out=None):
    CHtb2 = reg
    P.act(lambda nc: nc.scalar.copy(xbf.ap, src.ap), [src.r()], [xbf.r()])
    P.act(lambda nc: nc.scalar.activation(x2.ap, xbf.ap, AF.Square),
          [xbf.r()], [x2.r()])
    for (ps, srcb) in ((pm, xbf), (pq, x2)):
        for h in range(CH):
            P.pe(lambda nc, h=h, ps=ps, srcb=srcb: nc.tensor.matmul(
                ps.ap, ones_bf.ap, srcb.ap[:, h, :],
                start=(h == 0), stop=(h == CH - 1)),
                [srcb.r(b0=h * tb * 2, b1=(h + 1) * tb * 2), ones_bf.r()],
                [ps.r(b0=0, b1=tb * 4)])
    P.act(lambda nc: nc.scalar.activation(m2buf.ap, pm.ap[:, 0:tb], AF.Square),
          [pm.r(b0=0, b1=tb * 4)], [m2buf.r()])
    P.dve(lambda nc: nc.vector.scalar_tensor_tensor(
        rstd.ap, pq.ap[:, 0:tb], 1e-5, m2buf.ap, op0=ALU.add, op1=ALU.subtract),
        [pq.r(b0=0, b1=tb * 4), m2buf.r()], [rstd.r()])
    P.act(lambda nc: nc.scalar.activation(rstd.ap, rstd.ap, AF.Ln),
          [rstd.r()], [rstd.r()])
    P.act(lambda nc: nc.scalar.activation(rstd.ap, rstd.ap, AF.Exp, scale=-0.5),
          [rstd.r()], [rstd.r()])
    ysl = ydst.ap[:, :, 1:tb + 1]
    P.dve(lambda nc: nc.vector.tensor_tensor(
        ysl, src.ap, pm.ap[:, 0:tb].unsqueeze(1).broadcast_to([128, CH, tb]),
        op=ALU.subtract),
        [src.r(), pm.r(b0=0, b1=tb * 4)], [ydst.r(b0=4)])
    P.dve(lambda nc: nc.vector.tensor_tensor(
        ysl, ysl, rstd.ap.unsqueeze(1).broadcast_to([128, CH, tb]),
        op=ALU.mult),
        [ydst.r(b0=4), rstd.r()], [ydst.r(b0=4)])
    if y_bf_out is not None:
        P.act(lambda nc: nc.scalar.copy(y_bf_out.ap, ysl),
              [ydst.r(b0=4)], [y_bf_out.r()])



def emit_l0ffn(P, A, nc, x_in, x_out, wr_bf, wk_bf, wv_bf, mu_sb, ones_bf, psum, first_call):
    TB = 512
    NB = T // TB
    A.mark()
    s_x = [A.alloc(f"l0f_x{i}", [128, CH, TB], F32) for i in range(2)]
    s_y = [A.alloc(f"l0f_y{i}", [128, CH, TB + 2], BF16) for i in range(2)]
    s_xbf = [A.alloc(f"l0f_xbf{i}", [128, CH, TB], BF16) for i in range(2)]
    s_m2 = A.alloc("l0f_m2", [128, TB], F32)
    s_rstd = A.alloc("l0f_rstd", [128, TB], F32)
    s_mbf = A.alloc("l0f_mbf", [128, 2 * TB], BF16)
    s_xr = A.alloc("l0f_xr", [128, CH, TB], BF16)
    s_xk = A.alloc("l0f_xk", [128, CH, TB], BF16)
    s_d = A.alloc("l0f_d", [128, CH, TB], F32)
    s_sr = A.alloc("l0f_sr", [128, CH, TB], BF16)
    s_ka = A.alloc("l0f_ka", [128, CH, TB], BF16)
    s_t = A.alloc("l0f_t", [128, TB], F32)

    xin_v = x_in.rearrange("(h p) t -> p h t", p=128)
    xout_v = x_out.rearrange("(h p) t -> p h t", p=128)

    for b in range(NB):
        xb = s_x[b % 2]
        yb = s_y[b % 2]
        yprev = s_y[(b + 1) % 2]
        xbf = s_xbf[b % 2]
        t0 = b * TB
        P.dma('SYNC', 'xin', lambda nc, xb=xb, t0=t0: nc.sync.dma_start(
            out=xb.ap, in_=xin_v[:, :, t0:t0 + TB]),
            [dram_reg(x_in, t0, t0 + TB)], [xb.r()])
        P.dma('GP', 'xbf', lambda nc, xbf=xbf, t0=t0: nc.gpsimd.dma_start(
            out=xbf.ap, in_=xin_v[:, :, t0:t0 + TB]),
            [dram_reg(x_in, t0, t0 + TB)], [xbf.r()])
        if b == 0:
            P.dve(lambda nc, yb=yb: nc.vector.memset(yb.ap[:, :, 0:2], 0.0),
                  [], [yb.r()])
        else:
            P.dve(lambda nc, yb=yb, yprev=yprev: nc.vector.tensor_copy(
                yb.ap[:, :, 1:2], yprev.ap[:, :, TB + 1:TB + 2]),
                [yprev.r(b0=(TB + 1) * 2)], [yb.r()])
        emit_ln_into(P, None, yb, 2, xbf, s_m2, s_rstd, psum[0], psum[1],
                     ones_bf, TB, TB + 2, mbf=s_mbf)
        ysh = yb.ap[:, :, 1:TB + 1]
        ycur = yb.ap[:, :, 2:TB + 2]
        P.dve(lambda nc, ycur=ycur, ysh=ysh: nc.vector.tensor_tensor(
            s_d.ap, ycur, ysh, op=ALU.subtract),
            [yb.r()], [s_d.r()])
        for h in range(CH):
            P.dve(lambda nc, h=h, yb=yb: nc.vector.scalar_tensor_tensor(
                s_xr.ap[:, h, :], s_d.ap[:, h, :], mu_sb.ap[:, 0, h:h + 1],
                yb.ap[:, h, 1:TB + 1], op0=ALU.mult, op1=ALU.add),
                [s_d.r(b0=h * TB * 4, b1=(h + 1) * TB * 4), mu_sb.r(), yb.r()],
                [s_xr.r(b0=h * TB * 2, b1=(h + 1) * TB * 2)])
            P.dve(lambda nc, h=h, yb=yb: nc.vector.scalar_tensor_tensor(
                s_xk.ap[:, h, :], s_d.ap[:, h, :], mu_sb.ap[:, 1, h:h + 1],
                yb.ap[:, h, 1:TB + 1], op0=ALU.mult, op1=ALU.add),
                [s_d.r(b0=h * TB * 4, b1=(h + 1) * TB * 4), mu_sb.r(), yb.r()],
                [s_xk.r(b0=h * TB * 2, b1=(h + 1) * TB * 2)])
        for (wbf, src, dst, bank0) in ((wr_bf, s_xr, s_sr, 2), (wk_bf, s_xk, s_ka, 4)):
            for dh in range(CH):
                pb = (psum[2], psum[3], psum[4], psum[7])[dh % 4]
                for h in range(CH):
                    P.pe(lambda nc, h=h, dh=dh, pb=pb, wbf=wbf, src=src: nc.tensor.matmul(
                        pb.ap, wbf.ap[:, h, dh * 128:(dh + 1) * 128], src.ap[:, h, :],
                        start=(h == 0), stop=(h == CH - 1)),
                        [wbf.r(b0=(h * 1024 + dh * 128) * 2, b1=(h * 1024 + dh * 128 + 128) * 2),
                         src.r(b0=h * TB * 2, b1=(h + 1) * TB * 2)],
                        [pb.r(b0=0, b1=TB * 4)])
                if dst is s_sr:
                    P.act(lambda nc, dh=dh, pb=pb, dst=dst: nc.scalar.activation(
                        dst.ap[:, dh, :], pb.ap[:, 0:TB], AF.Sigmoid),
                        [pb.r(b0=0, b1=TB * 4)],
                        [dst.r(b0=dh * TB * 2, b1=(dh + 1) * TB * 2)])
                else:
                    P.act(lambda nc, dh=dh, pb=pb, dst=dst: nc.scalar.activation(
                        dst.ap[:, dh, :], pb.ap[:, 0:TB], AF.Relu),
                        [pb.r(b0=0, b1=TB * 4)],
                        [dst.r(b0=dh * TB * 2, b1=(dh + 1) * TB * 2)])
        P.dve(lambda nc: nc.vector.tensor_tensor(s_ka.ap, s_ka.ap, s_ka.ap, op=ALU.mult),
              [s_ka.r()], [s_ka.r()])
        for dh in range(CH):
            pb = (psum[2], psum[3], psum[4], psum[7])[dh % 4]
            for h in range(CH):
                P.pe(lambda nc, h=h, dh=dh, pb=pb: nc.tensor.matmul(
                    pb.ap, wv_bf.ap[:, h, dh * 128:(dh + 1) * 128], s_ka.ap[:, h, :],
                    start=(h == 0), stop=(h == CH - 1)),
                    [wv_bf.r(b0=(h * 1024 + dh * 128) * 2, b1=(h * 1024 + dh * 128 + 128) * 2),
                     s_ka.r(b0=h * TB * 2, b1=(h + 1) * TB * 2)],
                    [pb.r(b0=0, b1=TB * 4)])
            P.dve(lambda nc, dh=dh, pb=pb: nc.vector.tensor_tensor(
                s_t.ap, pb.ap[:, 0:TB], s_sr.ap[:, dh, :], op=ALU.mult),
                [pb.r(b0=0, b1=TB * 4), s_sr.r(b0=dh * TB * 2, b1=(dh + 1) * TB * 2)],
                [s_t.r()])
            P.dve(lambda nc, dh=dh, xb=xb: nc.vector.scalar_tensor_tensor(
                xb.ap[:, dh, :], s_t.ap, 0.5, xb.ap[:, dh, :],
                op0=ALU.mult, op1=ALU.add),
                [s_t.r(), xb.r(b0=dh * TB * 4, b1=(dh + 1) * TB * 4)],
                [xb.r(b0=dh * TB * 4, b1=(dh + 1) * TB * 4)])
        P.dma('SYNC', 'xout', lambda nc, xb=xb, t0=t0: nc.sync.dma_start(
            out=xout_v[:, :, t0:t0 + TB], in_=xb.ap),
            [xb.r()], [dram_reg(x_out, t0, t0 + TB)])
    A.release()



def build_l0ffn_only(serialize=False):
    nc = bass.Bass('TRN2', target_bir_lowering=False)
    x_in = nc.dram_tensor("x_in", [C, T], F32, kind="ExternalInput").ap()
    x_out = nc.dram_tensor("x_out", [C, T], F32, kind="ExternalOutput").ap()
    wr = WeightDram(nc, "l0f_WrT", C, C)
    wk = WeightDram(nc, "l0f_WkT", C, C)
    wv = WeightDram(nc, "l0f_WvT", C, C)
    mu = nc.dram_tensor("l0f_mu", [128, 2, CH], F32, kind="ExternalInput").ap()
    ones = nc.dram_tensor(CONST_ONES_SCALED, [128, 128], BF16, kind="ExternalInput").ap()
    with contextlib.ExitStack() as stack:
        A = stack.enter_context(Arena(nc))
        psum = [stack.enter_context(PsumTile(nc, i)) for i in range(5)] + [None, None]
        psum.append(stack.enter_context(PsumTile(nc, 7)))
        P = Prog(nc, serialize=serialize)
        wr_bf = A.alloc("wrbf", [128, CH, C], BF16)
        wk_bf = A.alloc("wkbf", [128, CH, C], BF16)
        wv_bf = A.alloc("wvbf", [128, CH, C], BF16)
        mu_sb = A.alloc("musb", [128, 2, CH], F32)
        ones_bf = A.alloc("ones", [128, 128], BF16)
        load_weight(P, wr, wr_bf)
        load_weight(P, wk, wk_bf)
        load_weight(P, wv, wv_bf)
        P.dma('SYNC', 'xin', lambda nc: nc.sync.dma_start(out=mu_sb.ap, in_=mu),
              [reg("l0f_mu")], [mu_sb.r()])
        P.dma('SYNC', 'xin', lambda nc: nc.sync.dma_start(out=ones_bf.ap, in_=ones),
              [reg(CONST_ONES_SCALED)], [ones_bf.r()])
        emit_l0ffn(P, A, nc, x_in, x_out, wr_bf, wk_bf, wv_bf, mu_sb, ones_bf,
                   psum, True)
        P.build()
        P.close()
    return nc


class PsumTile:
    def __init__(self, nc, i):
        self.cm = nc.psum_tensor(f"pb{i}", [128, 512], F32)
        self.name = f"pb{i}"

    def __enter__(self):
        t = self.cm.__enter__()
        self.ap = t[:]
        return self

    def __exit__(self, *a):
        return self.cm.__exit__(*a)

    def r(self, p0=0, p1=128, b0=0, b1=None):
        if b1 is None:
            b1 = 2048
        return (self.name, p0, p1, b0, b1)





def emit_l0attn(P, A, nc, x_in, x_out, wk_bf, wv_bf, wr_bf, wo_bf, tm_sb,
                td_sb, tf_sb, ones_bf, psum):
    TB = 256
    NB = T // TB
    A.mark()
    s_x = [A.alloc(f"l0a_x{i}", [128, CH, TB], F32) for i in range(2)]
    s_y = A.alloc("l0a_y", [128, CH, TB + 2], BF16)
    s_mbf = A.alloc("l0a_mbf", [128, 2 * TB], BF16)
    s_ycar = A.alloc("l0a_ycar", [128, CH, 1], BF16)
    s_xbf = A.alloc("l0a_xbf", [128, CH, TB], BF16)
    s_m2 = A.alloc("l0a_m2", [128, TB], F32)
    s_rstd = A.alloc("l0a_rstd", [128, TB], F32)
    s_dm = A.alloc("l0a_dm", [128, TB], F32)
    s_mx = [A.alloc(f"l0a_mx{i}", [128, CH, TB], BF16) for i in range(3)]
    s_k = [A.alloc(f"l0a_k{i}", [128, CH, TB], BF16) for i in range(2)]
    s_v = [A.alloc(f"l0a_v{i}", [128, CH, TB], BF16) for i in range(2)]
    s_sr = [A.alloc(f"l0a_sr{i}", [128, CH, TB], BF16) for i in range(2)]
    s_sy = [A.alloc(f"l0a_sy{i}", [128, CH, TB], BF16) for i in range(2)]
    s_P = A.alloc("l0a_P", [128, CH, TB + 1], F32)
    s_A = A.alloc("l0a_A", [128, CH, TB + 1], BF16)
    s_B = A.alloc("l0a_B", [128, CH, TB + 1], BF16)
    s_F12 = A.alloc("l0a_F12", [128, CH, 2 * TB], BF16)
    s_G = A.alloc("l0a_G", [128, CH, TB], BF16)
    s_N = A.alloc("l0a_N", [128, CH, TB], BF16)
    s_T12 = A.alloc("l0a_T12", [128, CH, 2 * TB], F32)
    s_w = A.alloc("l0a_w", [128, CH], F32)

    xin_v = x_in.rearrange("(h p) t -> p h t", p=128)
    xout_v = x_out.rearrange("(h p) t -> p h t", p=128)

    P.act(lambda nc: nc.scalar.activation(s_w.ap, td_sb.ap, AF.Exp),
          [td_sb.r()], [s_w.r()])
    P.dve(lambda nc: nc.vector.tensor_scalar(s_w.ap, s_w.ap, -1.0, None, op0=ALU.mult),
          [s_w.r()], [s_w.r()])
    P.dve(lambda nc: nc.vector.memset(s_P.ap, NEG), [], [s_P.r()])
    P.dve(lambda nc: nc.vector.memset(s_A.ap, 0.0), [], [s_A.r()])
    P.dve(lambda nc: nc.vector.memset(s_B.ap, 0.0), [], [s_B.r()])
    P.dve(lambda nc: nc.vector.memset(s_ycar.ap, 0.0), [], [s_ycar.r()])

    def stage1(b):
        t0 = b * TB
        P.dma('GP', 'xbf', lambda nc, t0=t0: nc.gpsimd.dma_start(
            out=s_xbf.ap, in_=xin_v[:, :, t0:t0 + TB]),
            [dram_reg(x_in, t0, t0 + TB)], [s_xbf.r()])
        P.dve(lambda nc: nc.vector.tensor_copy(s_y.ap[:, :, 1:2], s_ycar.ap),
              [s_ycar.r()], [s_y.r()])
        emit_ln_into(P, None, s_y, 2, s_xbf, s_m2, s_rstd, psum[0], psum[1],
                     ones_bf, TB, TB + 2, mbf=s_mbf)
        P.dve(lambda nc: nc.vector.tensor_copy(s_ycar.ap, s_y.ap[:, :, TB + 1:TB + 2]),
              [s_y.r(b0=(TB + 1) * 2)], [s_ycar.r()])
        for h in range(CH):
            P.dve(lambda nc, h=h: nc.vector.tensor_tensor(
                s_dm.ap, s_y.ap[:, h, 2:TB + 2], s_y.ap[:, h, 1:TB + 1], op=ALU.subtract),
                [s_y.r(b0=h * (TB + 2) * 2, b1=(h + 1) * (TB + 2) * 2)], [s_dm.r()])
            for j in range(3):
                P.dve(lambda nc, h=h, j=j: nc.vector.scalar_tensor_tensor(
                    s_mx[j].ap[:, h, :], s_dm.ap, tm_sb.ap[:, j, h:h + 1],
                    s_y.ap[:, h, 1:TB + 1], op0=ALU.mult, op1=ALU.add),
                    [s_dm.r(), tm_sb.r(),
                     s_y.r(b0=h * (TB + 2) * 2, b1=(h + 1) * (TB + 2) * 2)],
                    [s_mx[j].r(b0=h * TB * 2, b1=(h + 1) * TB * 2)])
        for (wbf, mxi, dst, act, bank0) in ((wk_bf, 0, s_k[b % 2], AF.Identity, 2),
                                            (wv_bf, 1, s_v[b % 2], AF.Identity, 4),
                                            (wr_bf, 2, s_sr[b % 2], AF.Sigmoid, 2)):
            src = s_mx[mxi]
            for dh in range(CH):
                pb = (psum[2], psum[3], psum[4], psum[7])[dh % 4]
                for h in range(CH):
                    P.pe(lambda nc, h=h, dh=dh, pb=pb, wbf=wbf, src=src: nc.tensor.matmul(
                        pb.ap[:, 0:TB], wbf.ap[:, h, dh * 128:(dh + 1) * 128],
                        src.ap[:, h, :],
                        start=(h == 0), stop=(h == CH - 1)),
                        [wbf.r(b0=(h * 1024 + dh * 128) * 2, b1=(h * 1024 + dh * 128 + 128) * 2),
                         src.r(b0=h * TB * 2, b1=(h + 1) * TB * 2)],
                        [pb.r(b0=0, b1=TB * 4)])
                P.act(lambda nc, dh=dh, pb=pb, dst=dst, act=act: nc.scalar.activation(
                    dst.ap[:, dh, :], pb.ap[:, 0:TB], act),
                    [pb.r(b0=0, b1=TB * 4)],
                    [dst.r(b0=dh * TB * 2, b1=(dh + 1) * TB * 2)])

    def stage2(b):
        kb, vb, srb = s_k[b % 2], s_v[b % 2], s_sr[b % 2]
        xb = s_x[b % 2]
        P.dma('SYNC', 'xin', lambda nc, xb=xb, t0=b * TB: nc.sync.dma_start(
            out=xb.ap, in_=xin_v[:, :, t0:t0 + TB]),
            [dram_reg(x_in, b * TB, (b + 1) * TB)], [xb.r()])
        if b > 0:
            for s in (s_P, s_A, s_B):
                P.dve(lambda nc, s=s: nc.vector.tensor_copy(
                    s.ap[:, :, 0:1], s.ap[:, :, TB:TB + 1]),
                    [s.r(b0=TB * 4)], [s.r()])
        for h in range(CH):
            P.dve(lambda nc, h=h: nc.vector.tensor_tensor_scan(
                s_P.ap[:, h, 1:TB + 1],
                s_w.ap[:, h:h + 1].broadcast_to([128, TB]),
                kb.ap[:, h, :], s_P.ap[:, h, 0:1], op0=ALU.add, op1=ALU.max),
                [s_w.r(), kb.r(b0=h * TB * 2, b1=(h + 1) * TB * 2),
                 s_P.r(b0=h * (TB + 1) * 4, b1=h * (TB + 1) * 4 + 4)],
                [s_P.r(b0=(h * (TB + 1) + 1) * 4, b1=(h + 1) * (TB + 1) * 4)])
        psh = s_P.ap[:, :, 0:TB]
        pcur = s_P.ap[:, :, 1:TB + 1]
        F1 = s_F12.ap[:, :, 0:TB]
        F2 = s_F12.ap[:, :, TB:2 * TB]
        T1 = s_T12.ap[:, :, 0:TB]
        T2 = s_T12.ap[:, :, TB:2 * TB]
        for h in range(CH):
            P.dve(lambda nc, h=h: nc.vector.scalar_tensor_tensor(
                s_T12.ap[:, h, 0:TB], s_P.ap[:, h, 0:TB], s_w.ap[:, h:h + 1],
                s_P.ap[:, h, 1:TB + 1], op0=ALU.add, op1=ALU.subtract),
                [s_P.r(b0=h * (TB + 1) * 4, b1=(h + 1) * (TB + 1) * 4), s_w.r()],
                [s_T12.r(b0=h * 2 * TB * 4, b1=(h * 2 + 1) * TB * 4)])
        P.dve(lambda nc, pcur=pcur, T2=T2: nc.vector.tensor_tensor(
            T2, kb.ap, pcur, op=ALU.subtract),
            [kb.r(), s_P.r()], [s_T12.r()])
        P.act(lambda nc: nc.scalar.activation(s_F12.ap, s_T12.ap, AF.Exp),
              [s_T12.r()], [s_F12.r()])
        P.dve(lambda nc, F2=F2: nc.vector.tensor_tensor(
            s_G.ap, F2, vb.ap, op=ALU.mult),
            [s_F12.r(), vb.r()], [s_G.r()])
        for h in range(CH):
            P.dve(lambda nc, h=h: nc.vector.tensor_tensor_scan(
                s_A.ap[:, h, 1:TB + 1], s_F12.ap[:, h, 0:TB], s_G.ap[:, h, :],
                s_A.ap[:, h, 0:1], op0=ALU.mult, op1=ALU.add),
                [s_F12.r(), s_G.r(b0=h * TB * 2, b1=(h + 1) * TB * 2),
                 s_A.r(b0=h * (TB + 1) * 2, b1=h * (TB + 1) * 2 + 2)],
                [s_A.r(b0=(h * (TB + 1) + 1) * 2, b1=(h + 1) * (TB + 1) * 2)])
            P.dve(lambda nc, h=h: nc.vector.tensor_tensor_scan(
                s_B.ap[:, h, 1:TB + 1], s_F12.ap[:, h, 0:TB], s_F12.ap[:, h, TB:2 * TB],
                s_B.ap[:, h, 0:1], op0=ALU.mult, op1=ALU.add),
                [s_F12.r(),
                 s_B.r(b0=h * (TB + 1) * 2, b1=h * (TB + 1) * 2 + 2)],
                [s_B.r(b0=(h * (TB + 1) + 1) * 2, b1=(h + 1) * (TB + 1) * 2)])
        for h in range(CH):
            P.act(lambda nc, h=h: nc.scalar.activation(
                s_T12.ap[:, h, TB:2 * TB], kb.ap[:, h, :], AF.Identity,
                bias=tf_sb.ap[:, h:h + 1]),
                [kb.r(b0=h * TB * 2, b1=(h + 1) * TB * 2), tf_sb.r()],
                [s_T12.r(b0=(h * 2 + 1) * TB * 4, b1=(h + 1) * 2 * TB * 4)])
        P.dve(lambda nc, psh=psh, T1=T1, T2=T2: nc.vector.tensor_tensor(
            T1, psh, T2, op=ALU.subtract),
            [s_P.r(), s_T12.r()], [s_T12.r()])
        P.dve(lambda nc, T1=T1, T2=T2: nc.vector.tensor_scalar(
            T2, T1, -1.0, 0.0, op0=ALU.mult, op1=ALU.min),
            [s_T12.r()], [s_T12.r()])
        P.dve(lambda nc, T1=T1: nc.vector.tensor_scalar(
            T1, T1, 0.0, None, op0=ALU.min),
            [s_T12.r()], [s_T12.r()])
        P.act(lambda nc: nc.scalar.activation(s_F12.ap, s_T12.ap, AF.Exp),
              [s_T12.r()], [s_F12.r()])
        P.dve(lambda nc, F1=F1: nc.vector.tensor_tensor(
            s_G.ap, F1, s_B.ap[:, :, 0:TB], op=ALU.mult),
            [s_F12.r(), s_B.r()], [s_G.r()])
        P.dve(lambda nc, F2=F2: nc.vector.tensor_tensor(
            s_G.ap, s_G.ap, F2, op=ALU.add),
            [s_G.r(), s_F12.r()], [s_G.r()])
        P.act(lambda nc: nc.scalar.activation(s_G.ap, s_G.ap, AF.Ln),
              [s_G.r()], [s_G.r()])
        P.act(lambda nc: nc.scalar.activation(s_G.ap, s_G.ap, AF.Exp, scale=-1.0),
              [s_G.r()], [s_G.r()])
        P.dve(lambda nc, F1=F1: nc.vector.tensor_tensor(
            s_N.ap, F1, s_A.ap[:, :, 0:TB], op=ALU.mult),
            [s_F12.r(), s_A.r()], [s_N.r()])
        P.dve(lambda nc, F1=F1, F2=F2: nc.vector.tensor_tensor(
            F1, F2, vb.ap, op=ALU.mult),
            [s_F12.r(), vb.r()], [s_F12.r()])
        P.dve(lambda nc, F1=F1: nc.vector.tensor_tensor(
            s_N.ap, s_N.ap, F1, op=ALU.add),
            [s_N.r(), s_F12.r()], [s_N.r()])
        P.dve(lambda nc: nc.vector.tensor_tensor(
            s_N.ap, s_N.ap, s_G.ap, op=ALU.mult),
            [s_N.r(), s_G.r()], [s_N.r()])
        syb = s_sy[b % 2]
        P.dve(lambda nc, syb=syb: nc.vector.tensor_tensor(
            syb.ap, s_N.ap, srb.ap, op=ALU.mult),
            [s_N.r(), srb.r()], [syb.r()])

    def stage3(b):
        xb = s_x[b % 2]
        syb = s_sy[b % 2]
        t0 = b * TB
        for dh in range(CH):
            pb = (psum[2], psum[3], psum[4], psum[7])[dh % 4]
            for h in range(CH):
                P.pe(lambda nc, h=h, dh=dh, pb=pb, syb=syb: nc.tensor.matmul(
                    pb.ap[:, 0:TB], wo_bf.ap[:, h, dh * 128:(dh + 1) * 128],
                    syb.ap[:, h, :],
                    start=(h == 0), stop=(h == CH - 1)),
                    [wo_bf.r(b0=(h * 1024 + dh * 128) * 2, b1=(h * 1024 + dh * 128 + 128) * 2),
                     syb.r(b0=h * TB * 2, b1=(h + 1) * TB * 2)],
                    [pb.r(b0=0, b1=TB * 4)])
            P.dve(lambda nc, dh=dh, pb=pb, xb=xb: nc.vector.tensor_tensor(
                xb.ap[:, dh, :], pb.ap[:, 0:TB], xb.ap[:, dh, :], op=ALU.add),
                [pb.r(b0=0, b1=TB * 4), xb.r(b0=dh * TB * 4, b1=(dh + 1) * TB * 4)],
                [xb.r(b0=dh * TB * 4, b1=(dh + 1) * TB * 4)])
        P.dma('SYNC', 'xout', lambda nc, xb=xb, t0=t0: nc.sync.dma_start(
            out=xout_v[:, :, t0:t0 + TB], in_=xb.ap),
            [xb.r()], [dram_reg(x_out, t0, t0 + TB)])

    stage1(0)
    for b in range(NB):
        if b + 1 < NB:
            stage1(b + 1)
        stage2(b)
        if b > 0:
            stage3(b - 1)
    stage3(NB - 1)
    A.release()


def emit_ln_bf(P, src, ydst, xbf, m2buf, rstd, dm_unused, pm, pq, ones_bf, tb):
    P.act(lambda nc: nc.scalar.copy(xbf.ap, src.ap), [src.r()], [xbf.r()])
    for h in range(CH):
        P.pe(lambda nc, h=h: nc.tensor.matmul(
            pm.ap[:, 0:tb], ones_bf.ap, xbf.ap[:, h, :],
            start=(h == 0), stop=(h == CH - 1)),
            [xbf.r(b0=h * tb * 2, b1=(h + 1) * tb * 2), ones_bf.r()],
            [pm.r(b0=0, b1=tb * 4)])
    P.act(lambda nc: nc.scalar.activation(xbf.ap, xbf.ap, AF.Square),
          [xbf.r()], [xbf.r()])
    for h in range(CH):
        P.pe(lambda nc, h=h: nc.tensor.matmul(
            pq.ap[:, 0:tb], ones_bf.ap, xbf.ap[:, h, :],
            start=(h == 0), stop=(h == CH - 1)),
            [xbf.r(b0=h * tb * 2, b1=(h + 1) * tb * 2), ones_bf.r()],
            [pq.r(b0=0, b1=tb * 4)])
    P.act(lambda nc: nc.scalar.activation(m2buf.ap[:, 0:tb], pm.ap[:, 0:tb], AF.Square),
          [pm.r(b0=0, b1=tb * 4)], [m2buf.r(b1=tb * 4)])
    P.dve(lambda nc: nc.vector.scalar_tensor_tensor(
        rstd.ap[:, 0:tb], pq.ap[:, 0:tb], 1e-5, m2buf.ap[:, 0:tb],
        op0=ALU.add, op1=ALU.subtract),
        [pq.r(b0=0, b1=tb * 4), m2buf.r(b1=tb * 4)], [rstd.r(b1=tb * 4)])
    P.act(lambda nc: nc.scalar.activation(rstd.ap[:, 0:tb], rstd.ap[:, 0:tb], AF.Ln),
          [rstd.r(b1=tb * 4)], [rstd.r(b1=tb * 4)])
    P.act(lambda nc: nc.scalar.activation(rstd.ap[:, 0:tb], rstd.ap[:, 0:tb], AF.Exp,
                                          scale=-0.5),
          [rstd.r(b1=tb * 4)], [rstd.r(b1=tb * 4)])
    ysl = ydst.ap[:, :, 1:tb + 1]
    P.dve(lambda nc, ysl=ysl: nc.vector.tensor_tensor(
        ysl, src.ap, pm.ap[:, 0:tb].unsqueeze(1).broadcast_to([128, CH, tb]),
        op=ALU.subtract),
        [src.r(), pm.r(b0=0, b1=tb * 4)], [ydst.r(b0=2)])
    P.dve(lambda nc, ysl=ysl: nc.vector.tensor_tensor(
        ysl, ysl, rstd.ap[:, 0:tb].unsqueeze(1).broadcast_to([128, CH, tb]),
        op=ALU.mult),
        [ydst.r(b0=2), rstd.r(b1=tb * 4)], [ydst.r(b0=2)])


def host_prep_l0attn(Wk, Wv, Wr, Wo, td, tf, tmk, tmv, tmr):
    d = {}
    d["l0a_WkT"] = np.ascontiguousarray(Wk.T)
    d["l0a_WvT"] = np.ascontiguousarray(Wv.T)
    d["l0a_WrT"] = np.ascontiguousarray(Wr.T)
    d["l0a_WoT"] = np.ascontiguousarray(Wo.T)
    d["l0a_td"] = np.ascontiguousarray(td.reshape(CH, 128).T)
    d["l0a_tf"] = np.ascontiguousarray(tf.reshape(CH, 128).T)
    d["l0a_tm"] = np.ascontiguousarray(
        np.stack([tmk.reshape(CH, 128).T, tmv.reshape(CH, 128).T,
                  tmr.reshape(CH, 128).T], 1))
    return d


def build_l0attn_only():
    nc = bass.Bass('TRN2', target_bir_lowering=False)
    x_in = nc.dram_tensor("x_in", [C, T], F32, kind="ExternalInput").ap()
    x_out = nc.dram_tensor("x_out", [C, T], F32, kind="ExternalOutput").ap()
    wk = WeightDram(nc, "l0a_WkT", C, C)
    wv = WeightDram(nc, "l0a_WvT", C, C)
    wr = WeightDram(nc, "l0a_WrT", C, C)
    wo = WeightDram(nc, "l0a_WoT", C, C)
    td = nc.dram_tensor("l0a_td", [128, CH], F32, kind="ExternalInput").ap()
    tf = nc.dram_tensor("l0a_tf", [128, CH], F32, kind="ExternalInput").ap()
    tm = nc.dram_tensor("l0a_tm", [128, 3, CH], F32, kind="ExternalInput").ap()
    ones = nc.dram_tensor(CONST_ONES_SCALED, [128, 128], BF16, kind="ExternalInput").ap()
    with contextlib.ExitStack() as stack:
        A = stack.enter_context(Arena(nc))
        psum = [stack.enter_context(PsumTile(nc, i)) for i in range(5)] + [None, None]
        psum.append(stack.enter_context(PsumTile(nc, 7)))
        P = Prog(nc)
        wk_bf = A.alloc("wkbf", [128, CH, C], BF16)
        wv_bf = A.alloc("wvbf", [128, CH, C], BF16)
        wr_bf = A.alloc("wrbf", [128, CH, C], BF16)
        wo_bf = A.alloc("wobf", [128, CH, C], BF16)
        td_sb = A.alloc("tdsb", [128, CH], F32)
        tf_sb = A.alloc("tfsb", [128, CH], F32)
        tm_sb = A.alloc("tmsb", [128, 3, CH], F32)
        ones_bf = A.alloc("ones", [128, 128], BF16)
        load_weight(P, wk, wk_bf)
        load_weight(P, wv, wv_bf)
        load_weight(P, wr, wr_bf)
        load_weight(P, wo, wo_bf)
        P.dma('SYNC', 'xin', lambda nc: nc.sync.dma_start(out=td_sb.ap, in_=td),
              [reg("l0a_td")], [td_sb.r()])
        P.dma('SYNC', 'xin', lambda nc: nc.sync.dma_start(out=tf_sb.ap, in_=tf),
              [reg("l0a_tf")], [tf_sb.r()])
        P.dma('SYNC', 'xin', lambda nc: nc.sync.dma_start(out=tm_sb.ap, in_=tm),
              [reg("l0a_tm")], [tm_sb.r()])
        P.dma('SYNC', 'xin', lambda nc: nc.sync.dma_start(out=ones_bf.ap, in_=ones),
              [reg(CONST_ONES_SCALED)], [ones_bf.r()])
        emit_l0attn(P, A, nc, x_in, x_out, wk_bf, wv_bf, wr_bf, wo_bf, tm_sb,
                    td_sb, tf_sb, ones_bf, psum)
        P.build()
        P.close()
    return nc






def emit_l1attn(P, A, nc, x_in, x_out, wq_d, wk_d, wlep_d, consts, psum, psum_bfs):
    TB = 512
    NB = T // TB
    TP = T + 4
    qb_sb, kb_sb, lepb_sb, ones_bf, ident_bf, cos_d, sin_d, sel_bf = consts
    A.mark()
    s_x0 = A.alloc("l1a_x0", [128, CH, TB], F32)
    s_x = [s_x0, s_x0]
    s_y = A.alloc("l1a_y", [128, CH, TP], BF16)
    s_xbf1 = A.alloc("l1a_xbf0", [128, CH, TB], BF16)
    s_xbf = [s_xbf1, s_xbf1]
    s_m2 = A.alloc("l1a_m2", [128, TB], F32)
    s_rstd = A.alloc("l1a_rstd", [128, TB], F32)
    s_mbf = A.alloc("l1a_mbf", [128, 2 * TB], BF16)
    s_q = A.alloc("l1a_q", [128, CH, T], BF16)
    s_z1 = A.alloc("l1a_z1", [128, TB], F32)
    s_z2 = A.alloc("l1a_z2", [128, TB], F32)
    s_z3 = s_m2
    s_z4 = s_rstd
    s_cs = [A.alloc(f"l1a_cs{i}", [128, TB], BF16) for i in range(4)]
    s_km = A.alloc("l1a_km", [128, CH], F32)
    s_kml = A.alloc("l1a_kml", [128, CH, CH], BF16)
    s_zbf = A.alloc("l1a_zbf", [128, T], BF16)
    s_vh = A.alloc("l1a_vh", [128, 16, 128], BF16)
    s_kr = A.alloc("l1a_kr", [128, 16, 128], BF16)
    s_kv = A.alloc("l1a_kv", [128, CH, 128], BF16)
    s_zr = [A.alloc(f"l1a_zr{i}", [128, TB], BF16) for i in range(2)]
    s_qs = [A.alloc(f"l1a_qs{i}", [128, TB], BF16) for i in range(2)]
    A.mark()
    s_k = A.alloc("l1a_k", [128, CH, T], BF16)
    wq_bf = A.alloc("l1a_wq", [128, CH, C], BF16)
    wk_bf = A.alloc("l1a_wk", [128, CH, C], BF16)
    load_weight(P, wq_d, wq_bf)
    load_weight(P, wk_d, wk_bf)

    xin_v = x_in.rearrange("(h p) t -> p h t", p=128)
    xout_v = x_out.rearrange("(h p) t -> p h t", p=128)

    P.dve(lambda nc: nc.vector.memset(s_y.ap[:, :, 0:2], 0.0), [], [s_y.r()])
    P.dve(lambda nc: nc.vector.memset(s_zbf.ap, 0.0), [], [s_zbf.r()])
    P.dve(lambda nc: nc.vector.memset(s_y.ap[:, :, TP - 2:TP], 0.0), [], [s_y.r()])

    def qk_block(wqk, bias, dst, b):
        t0 = b * TB
        for dh in range(CH):
            pb = (psum[2], psum[3], psum[4], psum[7])[dh % 4]
            for h in range(CH):
                P.pe(lambda nc, h=h, dh=dh, pb=pb, wqk=wqk, t0=t0: nc.tensor.matmul(
                    pb.ap, wqk.ap[:, h, dh * 128:(dh + 1) * 128],
                    s_y.ap[:, h, 2 + t0:2 + t0 + TB],
                    start=(h == 0), stop=(h == CH - 1)),
                    [wqk.r(b0=(h * 1024 + dh * 128) * 2, b1=(h * 1024 + dh * 128 + 128) * 2),
                     s_y.r(b0=h * TP * 2, b1=(h + 1) * TP * 2)],
                    [pb.r()])
            P.act(lambda nc, pb=pb, bias=bias, dh=dh: nc.scalar.activation(
                s_z1.ap, pb.ap, AF.Identity, bias=bias.ap[:, dh:dh + 1]),
                [pb.r(), bias.r()], [s_z1.r()])
            P.dve(lambda nc: nc.vector.tensor_scalar(
                s_z2.ap, s_z1.ap, 0.0, None, op0=ALU.min),
                [s_z1.r()], [s_z2.r()])
            P.act(lambda nc: nc.scalar.activation(s_z2.ap, s_z2.ap, AF.Exp),
                  [s_z2.r()], [s_z2.r()])
            P.dve(lambda nc: nc.vector.tensor_scalar(
                s_z1.ap, s_z1.ap, 0.0, None, op0=ALU.max),
                [s_z1.r()], [s_z1.r()])
            P.dve(lambda nc, dst=dst, dh=dh, t0=t0: nc.vector.tensor_tensor(
                dst.ap[:, dh, t0:t0 + TB], s_z1.ap, s_z2.ap, op=ALU.add),
                [s_z1.r(), s_z2.r()],
                [dst.r(b0=(dh * T + t0) * 2, b1=(dh * T + t0 + TB) * 2)])

    for b in range(NB):
        t0 = b * TB
        xbf = s_xbf[b % 2]
        P.dma('GP', 'xbf', lambda nc, xbf=xbf, t0=t0: nc.gpsimd.dma_start(
            out=xbf.ap, in_=xin_v[:, :, t0:t0 + TB]),
            [dram_reg(x_in, t0, t0 + TB)], [xbf.r()])
        emit_ln_into(P, None, s_y, 2 + t0, xbf, s_m2, s_rstd, psum[0], psum[1],
                     ones_bf, TB, TP, mbf=s_mbf)
        qk_block(wq_bf, qb_sb, s_q, b)
        qk_block(wk_bf, kb_sb, s_k, b)

    P.dve(lambda nc: nc.vector.tensor_reduce(
        s_km.ap.unsqueeze(2), s_k.ap, axis=AX.X, op=ALU.add),
        [s_k.r()], [s_km.r()])
    P.dve(lambda nc: nc.vector.tensor_scalar(
        s_km.ap, s_km.ap, 1.0 / T, None, op0=ALU.mult),
        [s_km.r()], [s_km.r()])
    P.dve(lambda nc: nc.vector.memset(s_kml.ap, 0.0), [], [s_kml.r()])
    for j in range(CH):
        hcol = 2 * (j % 4)
        P.dve(lambda nc, j=j, hcol=hcol: nc.vector.tensor_copy(
            s_kml.ap[0:64, j, hcol:hcol + 1], s_km.ap[0:64, j:j + 1]),
            [s_km.r(p1=64)], [s_kml.r(p1=64)])
        P.dve(lambda nc, j=j, hcol=hcol: nc.vector.tensor_copy(
            s_kml.ap[64:128, j, hcol + 1:hcol + 2], s_km.ap[64:128, j:j + 1]),
            [s_km.r(p0=64)], [s_kml.r(p0=64)])
    for b in range(NB):
        t0 = b * TB
        pb = psum[4]
        for j in range(CH):
            P.pe(lambda nc, j=j, t0=t0, pb=pb: nc.tensor.matmul(
                pb.ap[0:8, :], s_kml.ap[:, j, :], s_q.ap[:, j, t0:t0 + TB],
                start=(j == 0), stop=(j == CH - 1)),
                [s_kml.r(), s_q.r(b0=(j * T + t0) * 2, b1=(j * T + t0 + TB) * 2)],
                [pb.r(p1=8)])
        P.dve(lambda nc, pb=pb: nc.vector.tensor_scalar(
            s_z1.ap[0:8, :], pb.ap[0:8, :], 1e-6, None, op0=ALU.add),
            [pb.r(p1=8)], [s_z1.r(p1=8)])
        P.dve(lambda nc: nc.vector.reciprocal(s_z1.ap[0:8, :], s_z1.ap[0:8, :]),
              [s_z1.r(p1=8)], [s_z1.r(p1=8)])
        P.act(lambda nc, t0=t0: nc.scalar.copy(s_zbf.ap[0:8, t0:t0 + TB], s_z1.ap[0:8, :]),
              [s_z1.r(p1=8)], [s_zbf.r(p1=8, b0=t0 * 2, b1=(t0 + TB) * 2)])

    if L1A_PHASES < 3:
        A.release(); A.release()
        return
    if DEBUG:
        P.dump(nc, "dbg_zbf", s_zbf)
        P.dump(nc, "dbg_km", s_km)
    zb1 = s_z1.ap.bitcast(BF16)[:, 0:TB]
    zb2 = s_z1.ap.bitcast(BF16)[:, TB:2 * TB]
    zb3 = s_z2.ap.bitcast(BF16)[:, 0:TB]
    zb4 = s_z2.ap.bitcast(BF16)[:, TB:2 * TB]
    csi = [0]

    def rope_src(src_):
        for j in range(4):
            for b in range(NB):
                t0 = b * TB
                cs_c = s_cs[csi[0] % 2]
                cs_s = s_cs[2 + csi[0] % 2]
                csi[0] += 1
                P.dma('SYNC', 'xin', lambda nc, j=j, t0=t0, cs_c=cs_c: nc.sync.dma_start(
                    out=cs_c.ap, in_=cos_d[:, j, t0:t0 + TB]),
                    [reg("c_cos")], [cs_c.r()])
                P.dma('SYNC', 'xin', lambda nc, j=j, t0=t0, cs_s=cs_s: nc.sync.dma_start(
                    out=cs_s.ap, in_=sin_d[:, j, t0:t0 + TB]),
                    [reg("c_sin")], [cs_s.r()])
                P.dve(lambda nc, src_=src_, j=j, t0=t0, cs_c=cs_c: nc.vector.tensor_tensor(
                    zb1, cs_c.ap, src_.ap[:, j, t0:t0 + TB], op=ALU.mult),
                    [cs_c.r(), src_.r(b0=(j * T + t0) * 2, b1=(j * T + t0 + TB) * 2)],
                    [s_z1.r(b1=TB * 2)])
                P.dve(lambda nc, src_=src_, j=j, t0=t0, cs_s=cs_s: nc.vector.tensor_tensor(
                    zb2, cs_s.ap, src_.ap[:, 4 + j, t0:t0 + TB], op=ALU.mult),
                    [cs_s.r(), src_.r(b0=((4 + j) * T + t0) * 2, b1=((4 + j) * T + t0 + TB) * 2)],
                    [s_z1.r(b0=TB * 2)])
                P.dve(lambda nc, src_=src_, j=j, t0=t0, cs_s=cs_s: nc.vector.tensor_tensor(
                    zb3, cs_s.ap, src_.ap[:, j, t0:t0 + TB], op=ALU.mult),
                    [cs_s.r(), src_.r(b0=(j * T + t0) * 2, b1=(j * T + t0 + TB) * 2)],
                    [s_z2.r(b1=TB * 2)])
                P.dve(lambda nc, src_=src_, j=j, t0=t0, cs_c=cs_c: nc.vector.tensor_tensor(
                    zb4, cs_c.ap, src_.ap[:, 4 + j, t0:t0 + TB], op=ALU.mult),
                    [cs_c.r(), src_.r(b0=((4 + j) * T + t0) * 2, b1=((4 + j) * T + t0 + TB) * 2)],
                    [s_z2.r(b0=TB * 2)])
                P.dve(lambda nc, src_=src_, j=j, t0=t0: nc.vector.tensor_tensor(
                    src_.ap[:, 4 + j, t0:t0 + TB], zb1, zb2, op=ALU.subtract),
                    [s_z1.r()],
                    [src_.r(b0=((4 + j) * T + t0) * 2, b1=((4 + j) * T + t0 + TB) * 2)])
                P.dve(lambda nc, src_=src_, j=j, t0=t0: nc.vector.tensor_tensor(
                    src_.ap[:, j, t0:t0 + TB], zb3, zb4, op=ALU.add),
                    [s_z2.r()],
                    [src_.r(b0=(j * T + t0) * 2, b1=(j * T + t0 + TB) * 2)])

    rope_src(s_k)
    for h in range(min(CH, L1A_P4HEADS)):
        jsrc = 4 + h if h < 4 else h - 4
        for tt in range(min(16, L1A_P4TT)):
            pA, pB = psum_bfs
            P.pe(lambda nc, h=h, tt=tt, pA=pA: nc.tensor.transpose(
                pA.ap[:, 0:128], s_y.ap[:, h, 2 + tt * 128:2 + (tt + 1) * 128],
                ident_bf.ap[:, 0:128]),
                [s_y.r(b0=h * TP * 2, b1=(h + 1) * TP * 2), ident_bf.r()],
                [pA.r(b1=256)])
            P.act(lambda nc, tt=tt, pA=pA: nc.scalar.copy(
                s_vh.ap[:, tt, :], pA.ap[:, 0:128]),
                [pA.r(b1=256)], [s_vh.r(b0=tt * 256, b1=(tt + 1) * 256)])
            if not L1A_P4KR:
                continue
            P.pe(lambda nc, jsrc=jsrc, tt=tt, pB=pB: nc.tensor.transpose(
                pB.ap[:, 0:128],
                s_k.ap[:, jsrc, tt * 128:(tt + 1) * 128],
                ident_bf.ap[:, 0:128]),
                [s_k.r(b0=jsrc * T * 2, b1=(jsrc + 1) * T * 2), ident_bf.r()],
                [pB.r(b1=256)])
            P.act(lambda nc, tt=tt, pB=pB: nc.scalar.copy(
                s_kr.ap[:, tt, :], pB.ap[:, 0:128]),
                [pB.r(b1=256)], [s_kr.r(b0=tt * 256, b1=(tt + 1) * 256)])
        if L1A_P4SUB >= 2:
            pk = psum[7]
            for tt in range(16):
                P.pe(lambda nc, tt=tt, pk=pk: nc.tensor.matmul(
                    pk.ap[:, 0:128], s_kr.ap[:, tt, :], s_vh.ap[:, tt, :],
                    start=(tt == 0), stop=(tt == 15)),
                    [s_kr.r(b0=tt * 256, b1=(tt + 1) * 256),
                     s_vh.r(b0=tt * 256, b1=(tt + 1) * 256)],
                    [pk.r(b1=512)])
            P.act(lambda nc, h=h, pk=pk: nc.scalar.copy(s_kv.ap[:, h, :], pk.ap[:, 0:128]),
                  [pk.r(b1=512)], [s_kv.r(b0=h * 256, b1=(h + 1) * 256)])

    if L1A_PHASES < 5:
        A.release(); A.release()
        return
    rope_src(s_q)

    A.release()
    wlep_bf = A.alloc("l1a_wlep", [128, 3 * CH, C], BF16)
    load_weight(P, wlep_d, wlep_bf, nsplit=3)
    for b in range(NB):
        t0 = b * TB
        xb = s_x[b % 2]
        P.dma('SYNC', 'xin', lambda nc, xb=xb, t0=t0: nc.sync.dma_start(
            out=xb.ap, in_=xin_v[:, :, t0:t0 + TB]),
            [dram_reg(x_in, t0, t0 + TB)], [xb.r()])
        for dh in range(CH):
            pb = (psum[2], psum[3], psum[4], psum[7])[dh % 4]
            pz = psum[4]
            P.pe(lambda nc, dh=dh, t0=t0, pz=pz: nc.tensor.matmul(
                pz.ap, sel_bf.ap[0:8, dh * 128:(dh + 1) * 128],
                s_zbf.ap[0:8, t0:t0 + TB], start=True, stop=True),
                [sel_bf.r(p1=8), s_zbf.r(p1=8, b0=t0 * 2, b1=(t0 + TB) * 2)],
                [pz.r()])
            zr = s_zr[dh % 2]
            qs = s_qs[dh % 2]
            P.act(lambda nc, pz=pz, zr=zr: nc.scalar.copy(zr.ap, pz.ap),
                  [pz.r()], [zr.r()])
            jsrc = 4 + dh if dh < 4 else dh - 4
            P.dve(lambda nc, jsrc=jsrc, t0=t0, zr=zr, qs=qs: nc.vector.tensor_tensor(
                qs.ap, s_q.ap[:, jsrc, t0:t0 + TB],
                zr.ap, op=ALU.mult),
                [s_q.r(b0=jsrc * T * 2, b1=(jsrc + 1) * T * 2), zr.r()],
                [qs.r()])
            for dt in range(3):
                for h in range(CH):
                    P.pe(lambda nc, dt=dt, h=h, dh=dh, t0=t0, pb=pb: nc.tensor.matmul(
                        pb.ap, wlep_bf.ap[:, dt * CH + h, dh * 128:(dh + 1) * 128],
                        s_y.ap[:, h, t0 + dt + 1:t0 + dt + 1 + TB],
                        start=(dt == 0 and h == 0), stop=False),
                        [wlep_bf.r(b0=((dt * CH + h) * 1024 + dh * 128) * 2,
                                   b1=((dt * CH + h) * 1024 + dh * 128 + 128) * 2),
                         s_y.r(b0=h * TP * 2, b1=(h + 1) * TP * 2)],
                        [pb.r()])
            P.pe(lambda nc, dh=dh, pb=pb, qs=qs: nc.tensor.matmul(
                pb.ap, s_kv.ap[:, dh, :], qs.ap,
                start=False, stop=True),
                [s_kv.r(b0=dh * 256, b1=(dh + 1) * 256), qs.r()],
                [pb.r()])
            P.dve(lambda nc, dh=dh, pb=pb, xb=xb: nc.vector.scalar_tensor_tensor(
                xb.ap[:, dh, :], pb.ap, lepb_sb.ap[:, dh:dh + 1], xb.ap[:, dh, :],
                op0=ALU.add, op1=ALU.add),
                [pb.r(), lepb_sb.r(), xb.r(b0=dh * TB * 4, b1=(dh + 1) * TB * 4)],
                [xb.r(b0=dh * TB * 4, b1=(dh + 1) * TB * 4)])
        P.dma('SYNC', 'xout', lambda nc, xb=xb, t0=t0: nc.sync.dma_start(
            out=xout_v[:, :, t0:t0 + TB], in_=xb.ap),
            [xb.r()], [dram_reg(x_out, t0, t0 + TB)])
    A.release()


def emit_ln_into(P, src, ydst, col0, xbf, m2buf, rstd, pm, pq, ones_bf, tb, tot,
                 mbf=None):
    for h in range(CH):
        P.pe(lambda nc, h=h: nc.tensor.matmul(
            pm.ap[:, 0:tb], ones_bf.ap, xbf.ap[:, h, :],
            start=(h == 0), stop=(h == CH - 1)),
            [xbf.r(b0=h * tb * 2, b1=(h + 1) * tb * 2), ones_bf.r()],
            [pm.r(b0=0, b1=tb * 4)])
    P.act(lambda nc: nc.scalar.activation(m2buf.ap[:, 0:tb], pm.ap[:, 0:tb], AF.Square),
          [pm.r(b0=0, b1=tb * 4)], [m2buf.r(b1=tb * 4)])
    if mbf is not None:
        P.act(lambda nc: nc.scalar.copy(mbf.ap[:, 0:tb], pm.ap[:, 0:tb]),
              [pm.r(b0=0, b1=tb * 4)], [mbf.r(b1=tb * 2)])
        for h in range(CH):
            ysl = ydst.ap[:, h, col0:col0 + tb]
            P.dve(lambda nc, h=h, ysl=ysl: nc.vector.tensor_tensor(
                ysl, xbf.ap[:, h, :], mbf.ap[:, 0:tb], op=ALU.subtract),
                [xbf.r(b0=h * tb * 2, b1=(h + 1) * tb * 2), mbf.r(b1=tb * 2)],
                [ydst.r()])
    P.act(lambda nc: nc.scalar.activation(xbf.ap, xbf.ap, AF.Square),
          [xbf.r()], [xbf.r()])
    for h in range(CH):
        P.pe(lambda nc, h=h: nc.tensor.matmul(
            pq.ap[:, 0:tb], ones_bf.ap, xbf.ap[:, h, :],
            start=(h == 0), stop=(h == CH - 1)),
            [xbf.r(b0=h * tb * 2, b1=(h + 1) * tb * 2), ones_bf.r()],
            [pq.r(b0=0, b1=tb * 4)])
    P.dve(lambda nc: nc.vector.scalar_tensor_tensor(
        rstd.ap[:, 0:tb], pq.ap[:, 0:tb], 1e-5, m2buf.ap[:, 0:tb],
        op0=ALU.add, op1=ALU.subtract),
        [pq.r(b0=0, b1=tb * 4), m2buf.r(b1=tb * 4)], [rstd.r(b1=tb * 4)])
    P.act(lambda nc: nc.scalar.activation(rstd.ap[:, 0:tb], rstd.ap[:, 0:tb], AF.Ln),
          [rstd.r(b1=tb * 4)], [rstd.r(b1=tb * 4)])
    if mbf is not None:
        P.act(lambda nc: nc.scalar.activation(
            mbf.ap[:, tb:2 * tb], rstd.ap[:, 0:tb], AF.Exp, scale=-0.5),
            [rstd.r(b1=tb * 4)], [mbf.r(b0=tb * 2)])
        for h in range(CH):
            ysl = ydst.ap[:, h, col0:col0 + tb]
            P.dve(lambda nc, h=h, ysl=ysl: nc.vector.tensor_tensor(
                ysl, ysl, mbf.ap[:, tb:2 * tb], op=ALU.mult),
                [ydst.r(), mbf.r(b0=tb * 2)], [ydst.r()])
    else:
        P.act(lambda nc: nc.scalar.activation(rstd.ap[:, 0:tb], rstd.ap[:, 0:tb], AF.Exp,
                                              scale=-0.5),
              [rstd.r(b1=tb * 4)], [rstd.r(b1=tb * 4)])
        ysl = ydst.ap[:, :, col0:col0 + tb]
        P.dve(lambda nc, ysl=ysl: nc.vector.tensor_tensor(
            ysl, src.ap, pm.ap[:, 0:tb].unsqueeze(1).broadcast_to([128, CH, tb]),
            op=ALU.subtract),
            [src.r(), pm.r(b0=0, b1=tb * 4)], [ydst.r()])
        P.dve(lambda nc, ysl=ysl: nc.vector.tensor_tensor(
            ysl, ysl, rstd.ap[:, 0:tb].unsqueeze(1).broadcast_to([128, CH, tb]),
            op=ALU.mult),
            [ydst.r(), rstd.r(b1=tb * 4)], [ydst.r()])


def host_prep_l1attn(qk_w, qk_b, lepe_w, lepe_b):
    perm = np.concatenate([np.arange(0, C, 2), np.arange(1, C, 2)])
    qw = qk_w[:, 0:C][:, perm]
    kw = qk_w[:, C:2 * C][:, perm]
    d = {}
    d["l1a_wqT"] = np.ascontiguousarray(qw)
    d["l1a_wkT"] = np.ascontiguousarray(kw)
    d["l1a_qb"] = np.ascontiguousarray(qk_b[0:C][perm].reshape(CH, 128).T)
    d["l1a_kb"] = np.ascontiguousarray(qk_b[C:2 * C][perm].reshape(CH, 128).T)
    d["l1a_wlep"] = np.ascontiguousarray(lepe_w.reshape(3 * C, C))
    d["l1a_lepb"] = np.ascontiguousarray(lepe_b.reshape(CH, 128).T)
    return d


def host_consts_l1attn():
    kmax = C // 2
    theta = 1.0 / (10000.0 ** (np.arange(kmax, dtype=np.float64) / kmax))
    ang = np.arange(T, dtype=np.float64)[:, None] * theta[None, :]
    cos = np.cos(ang).T.astype(np.float32)
    sin = np.sin(ang).T.astype(np.float32)
    def to_ct(a):
        return np.ascontiguousarray(a.reshape(4, 128, T).transpose(1, 0, 2)).astype(ml_dtypes.bfloat16)
    sel = np.zeros((8, C), dtype=np.float64)
    for e in range(8):
        sel[e, e * 128:(e + 1) * 128] = 1.0 / np.sqrt(T)
    return {"c_cos": to_ct(cos), "c_sin": to_ct(sin),
            "c_ident": np.concatenate([np.eye(128), np.vstack([np.eye(64), np.eye(64)])], 1).astype(ml_dtypes.bfloat16),
            "c_sel": sel.astype(ml_dtypes.bfloat16)}


def build_l1attn_only():
    nc = bass.Bass('TRN2', target_bir_lowering=False)
    x_in = nc.dram_tensor("x_in", [C, T], F32, kind="ExternalInput").ap()
    x_out = nc.dram_tensor("x_out", [C, T], F32, kind="ExternalOutput").ap()
    wq_d = WeightDram(nc, "l1a_wqT", C, C)
    wk_d = WeightDram(nc, "l1a_wkT", C, C)
    wlep_d = WeightDram(nc, "l1a_wlep", 3 * C, C)
    qb = nc.dram_tensor("l1a_qb", [128, CH], F32, kind="ExternalInput").ap()
    kb = nc.dram_tensor("l1a_kb", [128, CH], F32, kind="ExternalInput").ap()
    lepb = nc.dram_tensor("l1a_lepb", [128, CH], F32, kind="ExternalInput").ap()
    ones = nc.dram_tensor(CONST_ONES_SCALED, [128, 128], BF16, kind="ExternalInput").ap()
    ident = nc.dram_tensor("c_ident", [128, 192], BF16, kind="ExternalInput").ap()
    cosd = nc.dram_tensor("c_cos", [128, 4, T], BF16, kind="ExternalInput").ap()
    sind = nc.dram_tensor("c_sin", [128, 4, T], BF16, kind="ExternalInput").ap()
    seld = nc.dram_tensor("c_sel", [8, C], BF16, kind="ExternalInput").ap()
    with contextlib.ExitStack() as stack:
        A = stack.enter_context(Arena(nc))
        psum = [stack.enter_context(PsumTile(nc, i)) for i in range(5)] + [None, None]
        psum_bfs = [stack.enter_context(PsumTileBF(nc, 5)),
                    stack.enter_context(PsumTileBF(nc, 6))]
        psum.append(stack.enter_context(PsumTile(nc, 7)))
        P = Prog(nc)
        qb_sb = A.alloc("qbsb", [128, CH], F32)
        kb_sb = A.alloc("kbsb", [128, CH], F32)
        lepb_sb = A.alloc("lepbsb", [128, CH], F32)
        ones_bf = A.alloc("ones", [128, 128], BF16)
        ident_bf = A.alloc("ident", [128, 192], BF16)
        sel_bf = A.alloc("sel", [128, C], BF16)
        P.dma('SYNC', 'xin', lambda nc: nc.sync.dma_start(out=qb_sb.ap, in_=qb),
              [reg("l1a_qb")], [qb_sb.r()])
        P.dma('SYNC', 'xin', lambda nc: nc.sync.dma_start(out=kb_sb.ap, in_=kb),
              [reg("l1a_kb")], [kb_sb.r()])
        P.dma('SYNC', 'xin', lambda nc: nc.sync.dma_start(out=lepb_sb.ap, in_=lepb),
              [reg("l1a_lepb")], [lepb_sb.r()])
        P.dma('SYNC', 'xin', lambda nc: nc.sync.dma_start(out=ones_bf.ap, in_=ones),
              [reg(CONST_ONES_SCALED)], [ones_bf.r()])
        P.dma('SYNC', 'xin', lambda nc: nc.sync.dma_start(out=ident_bf.ap, in_=ident),
              [reg("c_ident")], [ident_bf.r()])
        P.dma('SYNC', 'xin', lambda nc: nc.sync.dma_start(out=sel_bf.ap[0:8, :], in_=seld),
              [reg("c_sel")], [sel_bf.r()])
        consts = (qb_sb, kb_sb, lepb_sb, ones_bf, ident_bf, cosd, sind, sel_bf)
        emit_l1attn(P, A, nc, x_in, x_out, wq_d, wk_d, wlep_d, consts, psum, psum_bfs)
        P.build()
        P.close()
    return nc


class PsumTileBF:
    def __init__(self, nc, i):
        self.cm = nc.psum_tensor(f"pbf{i}", [128, 1024], BF16)
        self.name = f"pbf{i}"

    def __enter__(self):
        t = self.cm.__enter__()
        self.ap = t[:]
        return self

    def __exit__(self, *a):
        return self.cm.__exit__(*a)

    def r(self, p0=0, p1=128, b0=0, b1=None):
        if b1 is None:
            b1 = 2048
        return (self.name, p0, p1, b0, b1)



HALF = 1536
HK = HALF // 128


def emit_l1ffn(P, A, nc, x_in, x_out, consts, psum, psum_bfs, acc_d, y_d, tag):
    TB = 512
    NB = T // TB
    (ones_bf, identf_d, ident_bf, sel1_d, gT_d,
     wg_d, wu_d, wd_d, sgg_d, sgu_d, sgd_d) = consts
    acc_v = acc_d.rearrange("p (h t) -> p h t", h=CH)
    y_v = y_d.rearrange("p (h t) -> p h t", h=CH)
    xin_v = x_in.rearrange("(h p) t -> p h t", p=128)
    xout_v = x_out.rearrange("(h p) t -> p h t", p=128)
    A.mark()
    s_identf = A.alloc(f"{tag}_idf", [128, 8], F32)
    s_sel1 = A.alloc(f"{tag}_sel1", [8, C], BF16)
    s_combT = A.alloc(f"{tag}_combT", [128, T], BF16)
    s_ys = [A.alloc(f"{tag}_ys{i}", [128, CH, TB], BF16) for i in range(2)]
    s_wg = [None, None]
    s_wu = [None, None]
    s_wg[0] = A.alloc(f"{tag}_wg0", [128, CH, HALF], BF16)
    s_wu[0] = A.alloc(f"{tag}_wu0", [128, CH, HALF], BF16)
    s_wd = A.alloc(f"{tag}_wd", [128, HK, C], BF16)
    s_h = A.alloc(f"{tag}_h", [128, HK, TB], BF16)
    s_sg = [A.alloc(f"{tag}_sg{i}", [128, TB], BF16) for i in range(4)]
    s_su = [A.alloc(f"{tag}_su{i}", [128, TB], BF16) for i in range(4)]
    s_cr = [A.alloc(f"{tag}_cr{i}", [128, TB], BF16) for i in range(2)]
    s_ab = [A.alloc(f"{tag}_ab{i}", [128, TB], F32) for i in range(4)]
    P.dma('SYNC', 'xin', lambda nc: nc.sync.dma_start(out=s_identf.ap, in_=identf_d),
          [reg("c_identf8")], [s_identf.r()])
    P.dma('SYNC', 'xin', lambda nc: nc.sync.dma_start(out=s_sel1.ap[0:8, :], in_=sel1_d),
          [reg("c_sel1")], [s_sel1.r()])

    def load_gu(e, par):
        if e == 0:
            load_weight(P, sgg_d, s_wg[par], stream="wt")
            load_weight(P, sgu_d, s_wu[par], stream="wtu")
        else:
            ex = e - 1
            for (wd_, dst, strm) in ((wg_d, s_wg[par], "wt"), (wu_d, s_wu[par], "wtu")):
                sv = wd_.view()
                P.dma('GP', strm, lambda nc, sv=sv, dst=dst, ex=ex: nc.gpsimd.dma_start(
                    out=dst.ap, in_=sv[:, ex * CH:(ex + 1) * CH, :]),
                    [reg(wd_.name)], [dst.r()])

    def load_wd(e):
        if e == 0:
            load_weight(P, sgd_d, s_wd, stream="wtd")
        else:
            ex = e - 1
            sv = wd_d.view()
            P.dma('GP', 'wtd', lambda nc, sv=sv, ex=ex: nc.gpsimd.dma_start(
                out=s_wd.ap, in_=sv[:, ex * HK:(ex + 1) * HK, :]),
                [reg(wd_d.name)], [s_wd.r()])

    def expert_body(e):
        par = e % 2
        wg, wu = s_wg[par], s_wu[par]
        for b in range(NB):
            t0 = b * TB
            ys = s_ys[(e * NB + b) % 2]
            P.dma('SYNC', 'xin', lambda nc, t0=t0, ys=ys: nc.sync.dma_start(
                out=ys.ap, in_=y_v[:, :, t0:t0 + TB]),
                [(y_d.tensor.name, 0, 128, 0, 1 << 40)], [ys.r()])
            cr = s_cr[e % 2]
            if e > 0:
                pz = psum[4]
                P.pe(lambda nc, e=e, t0=t0, pz=pz: nc.tensor.matmul(
                    pz.ap, s_sel1.ap[0:8, (e - 1) * 128:e * 128],
                    s_combT.ap[0:8, t0:t0 + TB], start=True, stop=True),
                    [s_sel1.r(p1=8), s_combT.r(p1=8, b0=t0 * 2, b1=(t0 + TB) * 2)],
                    [pz.r()])
                P.act(lambda nc, pz=pz, cr=cr: nc.scalar.copy(cr.ap, pz.ap),
                      [pz.r()], [cr.r()])
            for mt in range(HK):
                pg, pu = psum[2], psum[3]
                for h in range(CH):
                    P.pe(lambda nc, h=h, mt=mt, pg=pg, wg=wg, ys=ys: nc.tensor.matmul(
                        pg.ap, wg.ap[:, h, mt * 128:(mt + 1) * 128],
                        ys.ap[:, h, :],
                        start=(h == 0), stop=(h == CH - 1)),
                        [wg.r(b0=(h * HALF + mt * 128) * 2, b1=(h * HALF + mt * 128 + 128) * 2),
                         ys.r(b0=h * TB * 2, b1=(h + 1) * TB * 2)],
                        [pg.r()])
                for h in range(CH):
                    P.pe(lambda nc, h=h, mt=mt, pu=pu, wu=wu, ys=ys: nc.tensor.matmul(
                        pu.ap, wu.ap[:, h, mt * 128:(mt + 1) * 128],
                        ys.ap[:, h, :],
                        start=(h == 0), stop=(h == CH - 1)),
                        [wu.r(b0=(h * HALF + mt * 128) * 2, b1=(h * HALF + mt * 128 + 128) * 2),
                         ys.r(b0=h * TB * 2, b1=(h + 1) * TB * 2)],
                        [pu.r()])
                sg = s_sg[mt % 4]
                su = s_su[mt % 4]
                P.act(lambda nc, pg=pg, sg=sg: nc.scalar.activation(sg.ap, pg.ap, AF.Silu),
                      [pg.r()], [sg.r()])
                if e > 0:
                    P.dve(lambda nc, pu=pu, su=su, cr=cr: nc.vector.tensor_tensor(
                        su.ap, pu.ap, cr.ap, op=ALU.mult),
                        [pu.r(), cr.r()], [su.r()])
                else:
                    P.act(lambda nc, pu=pu, su=su: nc.scalar.copy(su.ap, pu.ap),
                          [pu.r()], [su.r()])
                P.dve(lambda nc, mt=mt, sg=sg, su=su: nc.vector.tensor_tensor(
                    s_h.ap[:, mt, :], sg.ap, su.ap, op=ALU.mult),
                    [sg.r(), su.r()],
                    [s_h.r(b0=mt * TB * 2, b1=(mt + 1) * TB * 2)])
            for dg in range(4):
                for (di, pd) in ((0, psum[4]), (1, psum[7])):
                    dh = dg * 2 + di
                    for hk in range(HK):
                        P.pe(lambda nc, hk=hk, dh=dh, pd=pd: nc.tensor.matmul(
                            pd.ap, s_wd.ap[:, hk, dh * 128:(dh + 1) * 128],
                            s_h.ap[:, hk, :],
                            start=(hk == 0), stop=(hk == HK - 1)),
                            [s_wd.r(b0=(hk * C + dh * 128) * 2, b1=(hk * C + dh * 128 + 128) * 2),
                             s_h.r(b0=hk * TB * 2, b1=(hk + 1) * TB * 2)],
                            [pd.r()])
                    ab = s_ab[dh % 4]
                    P.act(lambda nc, pd=pd, ab=ab: nc.scalar.copy(ab.ap, pd.ap),
                          [pd.r()], [ab.r()])
                    P.dma('GP', 'acc', lambda nc, e=e, dh=dh, t0=t0, ab=ab: nc.gpsimd.dma_start(
                        out=acc_v[:, dh, t0:t0 + TB], in_=ab.ap,
                        accum_op=(ALU.bypass if e == 0 else ALU.add)),
                        [ab.r()],
                        [(acc_d.tensor.name, 0, 128, (dh * T + t0) * 4, (dh * T + t0 + TB) * 4)])

    load_gu(0, 0)
    load_wd(0)
    A.mark()
    s_y = A.alloc(f"{tag}_y", [128, CH, T], BF16)
    A.mark()
    s_xbf = [A.alloc(f"{tag}_xbf{i}", [128, CH, TB], BF16) for i in range(2)]
    s_m2 = A.alloc(f"{tag}_m2", [128, TB], F32)
    s_rstd = A.alloc(f"{tag}_rstd", [128, TB], F32)
    s_mbf = A.alloc(f"{tag}_mbf", [128, 2 * TB], BF16)
    for b in range(NB):
        t0 = b * TB
        xbf = s_xbf[b % 2]
        P.dma('GP', 'xbf', lambda nc, t0=t0, xbf=xbf: nc.gpsimd.dma_start(
            out=xbf.ap, in_=xin_v[:, :, t0:t0 + TB]),
            [dram_reg(x_in, t0, t0 + TB)], [xbf.r()])
        emit_ln_into(P, None, s_y, t0, xbf, s_m2, s_rstd, psum[0], psum[1],
                     ones_bf, TB, T, mbf=s_mbf)
        P.dma('SYNC', 'yd', lambda nc, t0=t0: nc.sync.dma_start(
            out=y_v[:, :, t0:t0 + TB], in_=s_y.ap[:, :, t0:t0 + TB]),
            [s_y.r(b0=t0 * 2)], [(y_d.tensor.name, 0, 128, 0, 1 << 40)])
    A.release()

    expert_body(0)
    load_wd(1)

    s_gt = A.alloc(f"{tag}_gt", [128, CH, CH], BF16)
    s_lg = A.alloc(f"{tag}_lg", [128, T], F32)
    s_lgt = A.alloc(f"{tag}_lgt", [128, 16, CH], F32)
    s_m1 = A.alloc(f"{tag}_m1", [128, 16], F32)
    s_mask = A.alloc(f"{tag}_msk", [128, 16, CH], U8)
    s_mask2 = A.alloc(f"{tag}_msk2", [128, 16, CH], U8)
    s_cum = A.alloc(f"{tag}_cum", [128, 16, CH], F32)
    s_l2 = A.alloc(f"{tag}_l2", [128, 16, CH], F32)
    s_w12 = A.alloc(f"{tag}_w12", [128, 16, 2], F32)
    s_comb = A.alloc(f"{tag}_comb", [128, 16, CH], BF16)
    s_cmp = A.alloc(f"{tag}_cmp", [128, 16, CH], U8)
    s_f1 = A.alloc(f"{tag}_f1", [128, 16, CH], F32)
    s_f2 = A.alloc(f"{tag}_f2", [128, 16, CH], F32)
    load_weight(P, gT_d, s_gt, stream="wt")
    for b in range(NB):
        t0 = b * TB
        pb = psum[0]
        for h in range(CH):
            P.pe(lambda nc, h=h, t0=t0, pb=pb: nc.tensor.matmul(
                pb.ap[0:8, 0:TB], s_gt.ap[:, h, :], s_y.ap[:, h, t0:t0 + TB],
                start=(h == 0), stop=(h == CH - 1)),
                [s_gt.r(), s_y.r(b0=(h * T + t0) * 2, b1=(h * T + t0 + TB) * 2)],
                [pb.r(p1=8)])
        P.act(lambda nc, t0=t0, pb=pb: nc.scalar.copy(
            s_lg.ap[0:8, t0:t0 + TB], pb.ap[0:8, 0:TB]),
            [pb.r(p1=8)], [s_lg.r(p1=8, b0=t0 * 4, b1=(t0 + TB) * 4)])
        for tq in range(4):
            q = b * 4 + tq
            pt = psum[0 if tq % 2 == 0 else 1]
            P.pe(lambda nc, t0=t0, tq=tq, pt=pt: nc.tensor.transpose(
                pt.ap[:, 128:136], s_lg.ap[0:8, t0 + tq * 128:t0 + (tq + 1) * 128],
                s_identf.ap[0:8, 0:8]),
                [s_lg.r(p1=8), s_identf.r(p1=8)], [pt.r(b0=512, b1=544)])
            P.act(lambda nc, q=q, pt=pt: nc.scalar.copy(
                s_lgt.ap[:, q, :], pt.ap[:, 128:136]),
                [pt.r(b0=512, b1=544)], [s_lgt.r(b0=q * 32, b1=(q + 1) * 32)])

    def pick_first(mask_c, dst_f):
        for q in range(16):
            P.dve(lambda nc, q=q, mask_c=mask_c: nc.vector.tensor_tensor_scan(
                s_cum.ap[:, q, :], mask_c.ap[:, q, :], mask_c.ap[:, q, :], 0.0,
                op0=ALU.add, op1=ALU.bypass),
                [mask_c.r()], [s_cum.r()])
        P.dve(lambda nc: nc.vector.tensor_single_scalar(
            s_cmp.ap, s_cum.ap, 1.5, op=ALU.is_lt),
            [s_cum.r()], [s_cmp.r()])
        P.dve(lambda nc, mask_c=mask_c, dst_f=dst_f: nc.vector.tensor_tensor(
            dst_f.ap, mask_c.ap, s_cmp.ap, op=ALU.mult),
            [mask_c.r(), s_cmp.r()], [dst_f.r()])

    P.dve(lambda nc: nc.vector.tensor_reduce(
        s_m1.ap.unsqueeze(2), s_lgt.ap, axis=AX.X, op=ALU.max),
        [s_lgt.r()], [s_m1.r()])
    m1b = s_m1.ap.unsqueeze(2).broadcast_to([128, 16, CH])
    P.dve(lambda nc, m1b=m1b: nc.vector.tensor_tensor(
        s_mask.ap, s_lgt.ap, m1b, op=ALU.is_ge),
        [s_lgt.r(), s_m1.r()], [s_mask.r()])
    pick_first(s_mask, s_f1)
    P.dve(lambda nc: nc.vector.tensor_scalar(
        s_cum.ap, s_f1.ap, -1.0, 1.0, op0=ALU.mult, op1=ALU.add),
        [s_f1.r()], [s_cum.r()])
    P.dve(lambda nc: nc.vector.tensor_tensor(
        s_l2.ap, s_lgt.ap, s_cum.ap, op=ALU.mult),
        [s_lgt.r(), s_cum.r()], [s_l2.r()])
    P.dve(lambda nc: nc.vector.tensor_scalar(
        s_cum.ap, s_f1.ap, NEG, None, op0=ALU.mult),
        [s_f1.r()], [s_cum.r()])
    P.dve(lambda nc: nc.vector.tensor_tensor(
        s_l2.ap, s_l2.ap, s_cum.ap, op=ALU.add),
        [s_l2.r(), s_cum.r()], [s_l2.r()])
    P.dve(lambda nc: nc.vector.tensor_reduce(
        s_w12.ap[:, :, 1:2], s_l2.ap, axis=AX.X, op=ALU.max),
        [s_l2.r()], [s_w12.r()])
    m2b = s_w12.ap[:, :, 1:2].broadcast_to([128, 16, CH])
    P.dve(lambda nc, m2b=m2b: nc.vector.tensor_tensor(
        s_mask2.ap, s_l2.ap, m2b, op=ALU.is_ge),
        [s_l2.r(), s_w12.r()], [s_mask2.r()])
    pick_first(s_mask2, s_f2)
    P.dve(lambda nc: nc.vector.tensor_tensor(
        s_w12.ap[:, :, 1:2], s_w12.ap[:, :, 1:2], s_m1.ap.unsqueeze(2),
        op=ALU.subtract),
        [s_w12.r(), s_m1.r()], [s_w12.r()])
    P.act(lambda nc: nc.scalar.activation(
        s_w12.ap[:, :, 1:2], s_w12.ap[:, :, 1:2], AF.Exp),
        [s_w12.r()], [s_w12.r()])
    P.dve(lambda nc: nc.vector.tensor_scalar(
        s_w12.ap[:, :, 0:1], s_w12.ap[:, :, 1:2], 1.0, None, op0=ALU.add),
        [s_w12.r()], [s_w12.r()])
    P.dve(lambda nc: nc.vector.reciprocal(s_w12.ap[:, :, 0:1], s_w12.ap[:, :, 0:1]),
          [s_w12.r()], [s_w12.r()])
    P.dve(lambda nc: nc.vector.tensor_tensor(
        s_w12.ap[:, :, 1:2], s_w12.ap[:, :, 1:2], s_w12.ap[:, :, 0:1],
        op=ALU.mult),
        [s_w12.r()], [s_w12.r()])
    P.dve(lambda nc: nc.vector.tensor_tensor(
        s_f1.ap, s_f1.ap, s_w12.ap[:, :, 0:1].broadcast_to([128, 16, CH]),
        op=ALU.mult),
        [s_f1.r(), s_w12.r()], [s_f1.r()])
    P.dve(lambda nc: nc.vector.tensor_tensor(
        s_f2.ap, s_f2.ap, s_w12.ap[:, :, 1:2].broadcast_to([128, 16, CH]),
        op=ALU.mult),
        [s_f2.r(), s_w12.r()], [s_f2.r()])
    P.dve(lambda nc: nc.vector.tensor_tensor(
        s_comb.ap, s_f1.ap, s_f2.ap, op=ALU.add),
        [s_f1.r(), s_f2.r()], [s_comb.r()])
    for q in range(16):
        pt = psum_bfs[q % 2]
        P.pe(lambda nc, q=q, pt=pt: nc.tensor.transpose(
            pt.ap[0:8, 0:128], s_comb.ap[:, q, :], ident_bf.ap[:, 0:128]),
            [s_comb.r(), ident_bf.r()], [pt.r(p1=8, b1=256)])
        P.act(lambda nc, q=q, pt=pt: nc.scalar.copy(
            s_combT.ap[0:8, q * 128:(q + 1) * 128], pt.ap[0:8, 0:128]),
            [pt.r(p1=8, b1=256)], [s_combT.r(p1=8, b0=q * 256, b1=(q + 1) * 256)])
    A.release()

    A.mark()
    s_wg[1] = A.alloc(f"{tag}_wg1", [128, CH, HALF], BF16)
    s_wu[1] = A.alloc(f"{tag}_wu1", [128, CH, HALF], BF16)
    load_gu(1, 1)
    for e in range(1, 9):
        if e + 1 < 9:
            load_gu(e + 1, (e + 1) % 2)
        expert_body(e)
        if e + 1 < 9:
            load_wd(e + 1)

    A.release()
    s_xf = A.alloc(f"{tag}_xf", [128, CH, TB], F32)
    s_af = A.alloc(f"{tag}_af", [128, CH, TB], F32)
    for b in range(NB):
        t0 = b * TB
        P.dma('SYNC', 'xin', lambda nc, t0=t0: nc.sync.dma_start(
            out=s_xf.ap, in_=xin_v[:, :, t0:t0 + TB]),
            [dram_reg(x_in, t0, t0 + TB)], [s_xf.r()])
        P.dma('SYNC', 'xin', lambda nc, t0=t0: nc.sync.dma_start(
            out=s_af.ap, in_=acc_v[:, :, t0:t0 + TB]),
            [(acc_d.tensor.name, 0, 128, 0, 1 << 40)], [s_af.r()])
        P.dve(lambda nc: nc.vector.scalar_tensor_tensor(
            s_xf.ap, s_af.ap, 0.5, s_xf.ap, op0=ALU.mult, op1=ALU.add),
            [s_af.r(), s_xf.r()], [s_xf.r()])
        P.dma('SYNC', 'xout', lambda nc, t0=t0: nc.sync.dma_start(
            out=xout_v[:, :, t0:t0 + TB], in_=s_xf.ap),
            [s_xf.r()], [dram_reg(x_out, t0, t0 + TB)])
    A.release()


def host_prep_l1ffn(gate_w, w_up, w_down, sg_gate, sg_up, sg_down):
    E = 8
    d = {}
    d["l1f_gT"] = np.ascontiguousarray(gate_w.T)
    wg = np.ascontiguousarray(w_up[:, :HALF, :].transpose(0, 2, 1))
    wu = np.ascontiguousarray(w_up[:, HALF:, :].transpose(0, 2, 1))
    wd = np.ascontiguousarray(w_down.transpose(0, 2, 1))
    d["l1f_wg"] = wg.reshape(E * C, HALF)
    d["l1f_wu"] = wu.reshape(E * C, HALF)
    d["l1f_wd"] = wd.reshape(E * HALF, C)
    d["l1f_sgg"] = np.ascontiguousarray(sg_gate.T)
    d["l1f_sgu"] = np.ascontiguousarray(sg_up.T)
    d["l1f_sgd"] = np.ascontiguousarray(sg_down.T)
    return d


def host_consts_l1ffn():
    sel1 = np.zeros((8, C), dtype=np.float64)
    for e in range(8):
        sel1[e, e * 128:(e + 1) * 128] = 1.0
    return {"c_sel1": sel1.astype(ml_dtypes.bfloat16),
            "c_identf8": np.eye(128, 8, dtype=np.float32)[:128, :8].copy()}


def build_l1ffn_only():
    nc = bass.Bass('TRN2', target_bir_lowering=False)
    x_in = nc.dram_tensor("x_in", [C, T], F32, kind="ExternalInput").ap()
    x_out = nc.dram_tensor("x_out", [C, T], F32, kind="ExternalOutput").ap()
    acc_d = nc.dram_tensor("l1f_acc", [128, CH * T], F32, kind="Internal").ap()
    y_d = nc.dram_tensor("l1f_yd", [128, CH * T], BF16, kind="Internal").ap()
    gT_d = WeightDram(nc, "l1f_gT", C, CH)
    wg_d = WeightDram(nc, "l1f_wg", 8 * C, HALF)
    wu_d = WeightDram(nc, "l1f_wu", 8 * C, HALF)
    wd_d = WeightDram(nc, "l1f_wd", 8 * HALF, C)
    sgg_d = WeightDram(nc, "l1f_sgg", C, HALF)
    sgu_d = WeightDram(nc, "l1f_sgu", C, HALF)
    sgd_d = WeightDram(nc, "l1f_sgd", HALF, C)
    ones = nc.dram_tensor(CONST_ONES_SCALED, [128, 128], BF16, kind="ExternalInput").ap()
    ident = nc.dram_tensor("c_ident", [128, 192], BF16, kind="ExternalInput").ap()
    identf = nc.dram_tensor("c_identf8", [128, 8], F32, kind="ExternalInput").ap()
    sel1 = nc.dram_tensor("c_sel1", [8, C], BF16, kind="ExternalInput").ap()
    with contextlib.ExitStack() as stack:
        A = stack.enter_context(Arena(nc))
        psum = [stack.enter_context(PsumTile(nc, i)) for i in range(5)] + [None, None]
        psum_bfs = [stack.enter_context(PsumTileBF(nc, 5)),
                    stack.enter_context(PsumTileBF(nc, 6))]
        psum.append(stack.enter_context(PsumTile(nc, 7)))
        P = Prog(nc)
        ones_bf = A.alloc("ones", [128, 128], BF16)
        ident_bf = A.alloc("ident", [128, 192], BF16)
        P.dma('SYNC', 'xin', lambda nc: nc.sync.dma_start(out=ones_bf.ap, in_=ones),
              [reg(CONST_ONES_SCALED)], [ones_bf.r()])
        P.dma('SYNC', 'xin', lambda nc: nc.sync.dma_start(out=ident_bf.ap, in_=ident),
              [reg("c_ident")], [ident_bf.r()])
        consts = (ones_bf, identf, ident_bf, sel1, gT_d,
                  wg_d, wu_d, wd_d, sgg_d, sgu_d, sgd_d)
        emit_l1ffn(P, A, nc, x_in, x_out, consts, psum, psum_bfs, acc_d, y_d, "l1f")
        P.build()
        P.close()
    return nc



def build_fused():
    nc = bass.Bass('TRN2', target_bir_lowering=False)
    x_in = nc.dram_tensor("x_in", [C, T], F32, kind="ExternalInput").ap()
    x_out = nc.dram_tensor("x_out", [C, T], F32, kind="ExternalOutput").ap()
    xA = nc.dram_tensor("xA", [C, T], F32, kind="Internal").ap()
    xB = nc.dram_tensor("xB", [C, T], F32, kind="Internal").ap()
    xC = nc.dram_tensor("xC", [C, T], F32, kind="Internal").ap()
    xD = nc.dram_tensor("xD", [C, T], F32, kind="Internal").ap()
    xE = nc.dram_tensor("xE", [C, T], F32, kind="Internal").ap()
    acc1 = nc.dram_tensor("l1f_acc1", [128, CH * T], F32, kind="Internal").ap()
    acc2 = nc.dram_tensor("l1f_acc2", [128, CH * T], F32, kind="Internal").ap()
    yd1 = nc.dram_tensor("l1f_yd1", [128, CH * T], BF16, kind="Internal").ap()
    yd2 = nc.dram_tensor("l1f_yd2", [128, CH * T], BF16, kind="Internal").ap()
    l0f_wr = WeightDram(nc, "l0f_WrT", C, C)
    l0f_wk = WeightDram(nc, "l0f_WkT", C, C)
    l0f_wv = WeightDram(nc, "l0f_WvT", C, C)
    mu = nc.dram_tensor("l0f_mu", [128, 2, CH], F32, kind="ExternalInput").ap()
    l0a_wk = WeightDram(nc, "l0a_WkT", C, C)
    l0a_wv = WeightDram(nc, "l0a_WvT", C, C)
    l0a_wr = WeightDram(nc, "l0a_WrT", C, C)
    l0a_wo = WeightDram(nc, "l0a_WoT", C, C)
    td = nc.dram_tensor("l0a_td", [128, CH], F32, kind="ExternalInput").ap()
    tf = nc.dram_tensor("l0a_tf", [128, CH], F32, kind="ExternalInput").ap()
    tm = nc.dram_tensor("l0a_tm", [128, 3, CH], F32, kind="ExternalInput").ap()
    wq_d = WeightDram(nc, "l1a_wqT", C, C)
    wk_d = WeightDram(nc, "l1a_wkT", C, C)
    wlep_d = WeightDram(nc, "l1a_wlep", 3 * C, C)
    qb = nc.dram_tensor("l1a_qb", [128, CH], F32, kind="ExternalInput").ap()
    kb = nc.dram_tensor("l1a_kb", [128, CH], F32, kind="ExternalInput").ap()
    lepb = nc.dram_tensor("l1a_lepb", [128, CH], F32, kind="ExternalInput").ap()
    gT_d = WeightDram(nc, "l1f_gT", C, CH)
    wg_d = WeightDram(nc, "l1f_wg", 8 * C, HALF)
    wu_d = WeightDram(nc, "l1f_wu", 8 * C, HALF)
    wd_d = WeightDram(nc, "l1f_wd", 8 * HALF, C)
    sgg_d = WeightDram(nc, "l1f_sgg", C, HALF)
    sgu_d = WeightDram(nc, "l1f_sgu", C, HALF)
    sgd_d = WeightDram(nc, "l1f_sgd", HALF, C)
    ones = nc.dram_tensor(CONST_ONES_SCALED, [128, 128], BF16, kind="ExternalInput").ap()
    ident = nc.dram_tensor("c_ident", [128, 192], BF16, kind="ExternalInput").ap()
    identf = nc.dram_tensor("c_identf8", [128, 8], F32, kind="ExternalInput").ap()
    sel1 = nc.dram_tensor("c_sel1", [8, C], BF16, kind="ExternalInput").ap()
    seld = nc.dram_tensor("c_sel", [8, C], BF16, kind="ExternalInput").ap()
    cosd = nc.dram_tensor("c_cos", [128, 4, T], BF16, kind="ExternalInput").ap()
    sind = nc.dram_tensor("c_sin", [128, 4, T], BF16, kind="ExternalInput").ap()
    with contextlib.ExitStack() as stack:
        A = stack.enter_context(Arena(nc))
        psum = [stack.enter_context(PsumTile(nc, i)) for i in range(5)] + [None, None]
        psum_bfs = [stack.enter_context(PsumTileBF(nc, 5)),
                    stack.enter_context(PsumTileBF(nc, 6))]
        psum.append(stack.enter_context(PsumTile(nc, 7)))
        P = Prog(nc)
        ones_bf = A.alloc("c_ones", [128, 128], BF16)
        ident_bf = A.alloc("c_id", [128, 192], BF16)
        mu_sb = A.alloc("c_mu", [128, 2, CH], F32)
        td_sb = A.alloc("c_td", [128, CH], F32)
        tf_sb = A.alloc("c_tf", [128, CH], F32)
        tm_sb = A.alloc("c_tm", [128, 3, CH], F32)
        qb_sb = A.alloc("c_qb", [128, CH], F32)
        kb_sb = A.alloc("c_kb", [128, CH], F32)
        lepb_sb = A.alloc("c_lepb", [128, CH], F32)
        sel_bf = A.alloc("c_selb", [128, C], BF16)
        for (dn, cv, sl8) in ((CONST_ONES_SCALED, ones_bf, False), ("c_ident", ident_bf, False),
                              ("l0f_mu", mu_sb, False), ("l0a_td", td_sb, False),
                              ("l0a_tf", tf_sb, False), ("l0a_tm", tm_sb, False),
                              ("l1a_qb", qb_sb, False), ("l1a_kb", kb_sb, False),
                              ("l1a_lepb", lepb_sb, False)):
            dv = {CONST_ONES_SCALED: ones, "c_ident": ident, "l0f_mu": mu,
                  "l0a_td": td, "l0a_tf": tf, "l0a_tm": tm, "l1a_qb": qb,
                  "l1a_kb": kb, "l1a_lepb": lepb}[dn]
            P.dma('SYNC', 'xin', lambda nc, dv=dv, cv=cv: nc.sync.dma_start(
                out=cv.ap, in_=dv), [reg(dn)], [cv.r()])
        P.dma('SYNC', 'xin', lambda nc: nc.sync.dma_start(out=sel_bf.ap[0:8, :], in_=seld),
              [reg("c_sel")], [sel_bf.r()])

        def sec_l0ffn(xi, xo):
            A.mark()
            wr_bf = A.alloc("s_wrbf", [128, CH, C], BF16)
            wk_bf = A.alloc("s_wkbf", [128, CH, C], BF16)
            wv_bf = A.alloc("s_wvbf", [128, CH, C], BF16)
            load_weight(P, l0f_wr, wr_bf)
            load_weight(P, l0f_wk, wk_bf)
            load_weight(P, l0f_wv, wv_bf)
            emit_l0ffn(P, A, nc, xi, xo, wr_bf, wk_bf, wv_bf, mu_sb, ones_bf,
                       psum, True)
            A.release()

        def sec_l0attn(xi, xo):
            A.mark()
            wk_bf = A.alloc("a_wkbf", [128, CH, C], BF16)
            wv_bf = A.alloc("a_wvbf", [128, CH, C], BF16)
            wr_bf = A.alloc("a_wrbf", [128, CH, C], BF16)
            wo_bf = A.alloc("a_wobf", [128, CH, C], BF16)
            load_weight(P, l0a_wk, wk_bf)
            load_weight(P, l0a_wv, wv_bf)
            load_weight(P, l0a_wr, wr_bf)
            load_weight(P, l0a_wo, wo_bf)
            emit_l0attn(P, A, nc, xi, xo, wk_bf, wv_bf, wr_bf, wo_bf, tm_sb,
                        td_sb, tf_sb, ones_bf, psum)
            A.release()

        def sec_l1ffn(xi, xo, acc_d, y_d, tag):
            consts = (ones_bf, identf, ident_bf, sel1, gT_d,
                      wg_d, wu_d, wd_d, sgg_d, sgu_d, sgd_d)
            emit_l1ffn(P, A, nc, xi, xo, consts, psum, psum_bfs, acc_d, y_d, tag)

        def sec_l1attn(xi, xo):
            consts = (qb_sb, kb_sb, lepb_sb, ones_bf, ident_bf, cosd, sind, sel_bf)
            emit_l1attn(P, A, nc, xi, xo, wq_d, wk_d, wlep_d, consts, psum, psum_bfs)

        sec_l0ffn(x_in, xA)
        sec_l0attn(xA, xB)
        sec_l0ffn(xB, xC)
        sec_l1ffn(xC, xD, acc1, yd1, "m1")
        sec_l1attn(xD, xE)
        sec_l1ffn(xE, x_out, acc2, yd2, "m2")
        P.build()
        P.close()
    return nc


FUSED = True


def _host_all(inputs):
    wd = {}
    wd.update(host_prep_l0ffn(inputs["L0_ffn_W_r"], inputs["L0_ffn_W_k"],
                              inputs["L0_ffn_W_v"], inputs["L0_ffn_mu_r"],
                              inputs["L0_ffn_mu_k"]))
    wd.update(host_prep_l0attn(inputs["L0_attn_Wk"], inputs["L0_attn_Wv"],
                               inputs["L0_attn_Wr"], inputs["L0_attn_Wo"],
                               np.asarray(inputs["L0_attn_time_decay"]).reshape(-1),
                               np.asarray(inputs["L0_attn_time_first"]).reshape(-1),
                               np.asarray(inputs["L0_attn_time_mix_k"]).reshape(-1),
                               np.asarray(inputs["L0_attn_time_mix_v"]).reshape(-1),
                               np.asarray(inputs["L0_attn_time_mix_r"]).reshape(-1)))
    wd.update(host_prep_l1attn(inputs["L1_attn_qk_w"], inputs["L1_attn_qk_b"],
                               inputs["L1_attn_lepe_w"], inputs["L1_attn_lepe_b"]))
    wd.update(host_prep_l1ffn(inputs["L1_ffn_gate_w"], inputs["L1_ffn_w_up"],
                              inputs["L1_ffn_w_down"], inputs["L1_ffn_sg_gate"],
                              inputs["L1_ffn_sg_up"], inputs["L1_ffn_sg_down"]))
    cs = host_consts()
    cs.update(host_consts_l1attn())
    cs.update(host_consts_l1ffn())
    wd.update(cs)
    return wd


_BUILD_CACHE = {}


def _get(name, fn):
    if name not in _BUILD_CACHE:
        _BUILD_CACHE[name] = fn()
    return _BUILD_CACHE[name]


def kernel(**inputs):
    host = _host_all(inputs)
    x = np.asarray(inputs["L0_attn_x"], dtype=np.float32)
    shards = [np.ascontiguousarray(x[b].T) for b in range(B)]
    cores = list(range(B))

    def run(nc, extra):
        in_maps = []
        for b in range(B):
            m = {"x_in": shards[b]}
            for k in extra:
                m[k] = host[k]
            in_maps.append(m)
        res = run_bass_kernel_spmd(nc, in_maps, core_ids=cores)
        for b in range(B):
            shards[b] = np.ascontiguousarray(res.results[b]["x_out"])

    l0f_keys = ["l0f_WrT", "l0f_WkT", "l0f_WvT", "l0f_mu", CONST_ONES_SCALED]
    l0a_keys = ["l0a_WkT", "l0a_WvT", "l0a_WrT", "l0a_WoT", "l0a_td", "l0a_tf",
                "l0a_tm", CONST_ONES_SCALED]
    l1f_keys = ["l1f_gT", "l1f_wg", "l1f_wu", "l1f_wd", "l1f_sgg", "l1f_sgu",
                "l1f_sgd", CONST_ONES_SCALED, "c_ident", "c_identf8", "c_sel1"]
    l1a_keys = ["l1a_wqT", "l1a_wkT", "l1a_wlep", "l1a_qb", "l1a_kb", "l1a_lepb",
                CONST_ONES_SCALED, "c_ident", "c_cos", "c_sin", "c_sel"]

    if FUSED:
        all_keys = sorted(set(l0f_keys + l0a_keys + l1f_keys + l1a_keys))
        run(_get("fused", build_fused), all_keys)
    else:
        run(_get("l0ffn", build_l0ffn_only), l0f_keys)
        run(_get("l0attn", build_l0attn_only), l0a_keys)
        run(_get("l0ffn", build_l0ffn_only), l0f_keys)
        run(_get("l1ffn", build_l1ffn_only), l1f_keys)
        run(_get("l1attn", build_l1attn_only), l1a_keys)
        run(_get("l1ffn", build_l1ffn_only), l1f_keys)

    return np.stack([s.T for s in shards], 0)


# version 27
# speedup vs baseline: 1.0432x; 1.0432x over previous
import contextlib
import numpy as np
import ml_dtypes
import concourse.bass as bass
import concourse.mybir as mybir
from concourse.bass_utils import run_bass_kernel_spmd

F32 = mybir.dt.float32
BF16 = mybir.dt.bfloat16
U8 = mybir.dt.uint8
AF = mybir.ActivationFunctionType
ALU = mybir.AluOpType
AX = mybir.AxisListType

B, T, C = 8, 2048, 1024
DEBUG = False
DEBUG_MIN = False
DEBUG_NB = 1
DEBUG_NOMM = True
L1A_PHASES = 6
L1A_P4SUB = 3
L1A_P4HEADS = 8
L1A_P4TT = 16
L1A_P4KR = 1
CH = 8
NEG = -1e38


def dram_reg(ap, t0=0, t1=1 << 30):
    return (ap.tensor.name, 0, 128, t0 * 4, t1 * 4)


def reg(h, p0=0, p1=128, b0=0, b1=None):
    name = h if isinstance(h, str) else h.name
    if b1 is None:
        b1 = 1 << 40
    return (name, p0, p1, b0, b1)


def _isect(a, b):
    return (a[0] == b[0] and a[1] < b[2] and b[1] < a[2]
            and a[3] < b[4] and b[3] < a[4])


class Prog:
    COMPUTE = ('PE', 'ACT', 'DVE', 'GP')

    SELF_EDGE = True

    def __init__(self, nc, serialize=False):
        self.nc = nc
        self.serialize = serialize
        self.ops = []
        self.events = {}
        self.deps = []

    def _record(self, engine, fn, reads, writes, stream=None):
        idx = len(self.ops)
        deps = set()
        for r in reads:
            for (i, kind, rr) in self.events.get(r[0], ()):
                if kind == 'w' and _isect(r, rr):
                    deps.add(i)
        for w in writes:
            for (i, kind, rr) in self.events.get(w[0], ()):
                if i != idx and _isect(w, rr):
                    deps.add(i)
        for r in reads:
            self.events.setdefault(r[0], []).append((idx, 'r', r))
        for w in writes:
            lst = self.events.setdefault(w[0], [])
            keep = []
            for (i, kind, rr) in lst:
                covered = (rr[1] >= w[1] and rr[2] <= w[2]
                           and rr[3] >= w[3] and rr[4] <= w[4])
                if not covered:
                    keep.append((i, kind, rr))
            keep.append((idx, 'w', w))
            self.events[w[0]] = keep
        if self.serialize and idx > 0:
            deps.add(idx - 1)
        self.ops.append(dict(engine=engine, fn=fn, stream=stream))
        self.deps.append(deps)
        return idx

    def pe(self, fn, reads, writes):
        return self._record('PE', fn, reads, writes)

    def act(self, fn, reads, writes):
        return self._record('ACT', fn, reads, writes)

    def dve(self, fn, reads, writes):
        return self._record('DVE', fn, reads, writes)

    def gpc(self, fn, reads, writes):
        return self._record('GP', fn, reads, writes)

    def dump(self, nc, name, carve, shape=None):
        shape = shape or carve.shape
        d = nc.dram_tensor(name, list(shape), carve.dtype, kind="ExternalOutput").ap()
        self.dma('SYNC', 'dbg', lambda nc_, c=carve, d=d: nc_.sync.dma_start(out=d, in_=c.ap),
                 [carve.r()], [(name, 0, 128, 0, 1 << 40)])

    def dma(self, queue, stream, fn, reads, writes):
        return self._record(queue, fn, reads, writes, stream=stream)

    def build(self):
        nc = self.nc
        n = len(self.ops)
        stream_pos = {}
        dma_index = [None] * n
        stream_before = [None] * n
        running = {}
        for i, op in enumerate(self.ops):
            stream_before[i] = dict(running)
            if op['stream'] is not None:
                k = stream_pos.setdefault(op['stream'], 0)
                dma_index[i] = k
                stream_pos[op['stream']] = k + 1
                running[op['stream']] = k + 1
        eng_pos = {}
        pos_of = [None] * n
        for i, op in enumerate(self.ops):
            e = op['engine']
            k = eng_pos.setdefault(e, 0)
            pos_of[i] = k
            eng_pos[e] = k + 1
        targets = {e: set() for e in self.COMPUTE}
        for i, op in enumerate(self.ops):
            for d in self.deps[i]:
                dop = self.ops[d]
                if dop['stream'] is not None:
                    continue
                if dop['engine'] == op['engine'] and (
                        not self.SELF_EDGE or dop['engine'] == 'PE'):
                    continue
                targets[dop['engine']].add(d)
        inc_no = {}
        for e, tset in targets.items():
            for rank, d in enumerate(sorted(tset)):
                inc_no[d] = rank + 1
        self._stack = contextlib.ExitStack()
        esem = {e: self._stack.enter_context(nc.semaphore(f"sem_{e}"))
                for e in self.COMPUTE}
        ssem = {s: self._stack.enter_context(nc.semaphore(f"sem_{s}"))
                for s in stream_pos}
        eng_obj = {'PE': nc.tensor, 'ACT': nc.scalar, 'DVE': nc.vector,
                   'SYNC': nc.sync, 'GP': nc.gpsimd}
        hiwater = {}
        for i, op in enumerate(self.ops):
            e = op['engine']
            eo = eng_obj[e]
            waits = {}
            for d in self.deps[i]:
                dop = self.ops[d]
                if dop['engine'] == e and dop['stream'] is None and (
                        not self.SELF_EDGE or e == 'PE'):
                    continue
                if dop['stream'] is not None:
                    sem = ssem[dop['stream']]
                    val = 16 * (dma_index[d] + 1)
                    key = ('s', dop['stream'])
                else:
                    sem, val = esem[dop['engine']], inc_no[d]
                    key = ('e', dop['engine'])
                if key not in waits or val > waits[key][1]:
                    waits[key] = (sem, val)
            if op['stream'] is not None and dma_index[i] > 0:
                key = ('s', op['stream'])
                v = 16 * dma_index[i]
                if (key not in waits) or waits[key][1] < v:
                    waits[key] = (ssem[op['stream']], v)
            for key, (sem, val) in sorted(waits.items()):
                hk = (e, key)
                if hiwater.get(hk, 0) >= val:
                    continue
                hiwater[hk] = val
                eo.wait_ge(sem, val)
            inst = op['fn'](nc)
            if op['stream'] is not None:
                inst.then_inc(ssem[op['stream']], 16)
            elif i in inc_no:
                inst.then_inc(esem[e], 1)
        for s, cnt in stream_pos.items():
            nc.sync.wait_ge(ssem[s], 16 * cnt)
        return self

    def close(self):
        self._stack.close()


SBUF_BYTES = 195136


class Arena:
    def __init__(self, nc, name="arena", nbytes=SBUF_BYTES):
        self.cm = nc.sbuf_tensor(name, [128, nbytes], U8)
        self.t = None
        self.nbytes = nbytes
        self.free_off = 0
        self.marks = []

    def __enter__(self):
        self.t = self.cm.__enter__()
        return self

    def __exit__(self, *a):
        return self.cm.__exit__(*a)

    def alloc(self, name, shape, dtype, align=64):
        esz = {F32: 4, BF16: 2, U8: 1, mybir.dt.uint32: 4}[dtype]
        nel = int(np.prod(shape[1:]))
        nb = ((nel * esz + align - 1) // align) * align
        off = ((self.free_off + align - 1) // align) * align
        assert off + nb <= self.nbytes, f"arena overflow at {name}: {off+nb} > {self.nbytes}"
        self.free_off = off + nb
        view = self.t[:].rearrange("p (a b c) -> p a b c", a=1, b=1)[:, 0, 0, off:off + nel * esz]
        view = view.bitcast(dtype)
        if len(shape) > 2:
            inner = " ".join(f"d{i}" for i in range(len(shape) - 1))
            view = view.rearrange(f"p ({inner}) -> p {inner}",
                                  **{f"d{i}": shape[1 + i] for i in range(len(shape) - 2)})
        return Carve(name, view, off, nb, shape, dtype)

    def mark(self):
        self.marks.append(self.free_off)

    def release(self):
        self.free_off = self.marks.pop()


class Carve:
    def __init__(self, name, ap, off, nb, shape, dtype):
        self.name = name
        self.ap = ap
        self.off = off
        self.nb = nb
        self.shape = shape
        self.dtype = dtype

    def __getitem__(self, idx):
        return self.ap[idx]

    def r(self, p0=0, p1=128, b0=0, b1=None):
        if b1 is None:
            b1 = self.nb
        return ("arena", p0, p1, self.off + b0, self.off + b1)




def host_prep_l0ffn(W_r, W_k, W_v, mu_r, mu_k):
    d = {}
    d["l0f_WrT"] = np.ascontiguousarray(W_r.T)
    d["l0f_WkT"] = np.ascontiguousarray(W_k.T)
    d["l0f_WvT"] = np.ascontiguousarray(W_v.T)
    d["l0f_mu"] = np.ascontiguousarray(
        np.stack([mu_r.reshape(CH, 128).T, mu_k.reshape(CH, 128).T], 1))
    return d


CONST_ONES_SCALED = "c_ones1024"


def host_consts():
    return {CONST_ONES_SCALED: np.full((128, 128), 1.0 / 1024, dtype=ml_dtypes.bfloat16)}


class WeightDram:

    def __init__(self, nc, name, rows, cols):
        self.name = name
        self.rows, self.cols = rows, cols
        self.ap = nc.dram_tensor(name, [rows, cols], F32, kind="ExternalInput").ap()

    def view(self):
        return self.ap.rearrange("(i p) n -> p i n", p=128)


def load_weight(P, wd: WeightDram, dst: Carve, stream="wt", nsplit=1):
    ni = wd.rows // 128
    dv = dst.ap if len(dst.shape) == 3 else dst.ap.rearrange("p (i n) -> p i n", i=ni)
    sv = wd.view()
    per = max(1, ni // nsplit)
    idxs = []
    i = 0
    while i < ni:
        j = min(ni, i + per)
        i0, j0 = i, j
        idxs.append(P.dma('GP', stream,
                          lambda nc, i0=i0, j0=j0: nc.gpsimd.dma_start(
                              out=dv[:, i0:j0, :], in_=sv[:, i0:j0, :]),
                          [reg(wd.name)],
                          [dst.r(b0=i0 * wd.cols * 2, b1=j0 * wd.cols * 2)]))
        i = j
    return idxs



def emit_ln(P, src, ydst, xbf, x2, m2buf, rstd, pm, pq, ones_bf, tb, y_bf_out=None):
    CHtb2 = reg
    P.act(lambda nc: nc.scalar.copy(xbf.ap, src.ap), [src.r()], [xbf.r()])
    P.act(lambda nc: nc.scalar.activation(x2.ap, xbf.ap, AF.Square),
          [xbf.r()], [x2.r()])
    for (ps, srcb) in ((pm, xbf), (pq, x2)):
        for h in range(CH):
            P.pe(lambda nc, h=h, ps=ps, srcb=srcb: nc.tensor.matmul(
                ps.ap, ones_bf.ap, srcb.ap[:, h, :],
                start=(h == 0), stop=(h == CH - 1)),
                [srcb.r(b0=h * tb * 2, b1=(h + 1) * tb * 2), ones_bf.r()],
                [ps.r(b0=0, b1=tb * 4)])
    P.act(lambda nc: nc.scalar.activation(m2buf.ap, pm.ap[:, 0:tb], AF.Square),
          [pm.r(b0=0, b1=tb * 4)], [m2buf.r()])
    P.dve(lambda nc: nc.vector.scalar_tensor_tensor(
        rstd.ap, pq.ap[:, 0:tb], 1e-5, m2buf.ap, op0=ALU.add, op1=ALU.subtract),
        [pq.r(b0=0, b1=tb * 4), m2buf.r()], [rstd.r()])
    P.act(lambda nc: nc.scalar.activation(rstd.ap, rstd.ap, AF.Ln),
          [rstd.r()], [rstd.r()])
    P.act(lambda nc: nc.scalar.activation(rstd.ap, rstd.ap, AF.Exp, scale=-0.5),
          [rstd.r()], [rstd.r()])
    ysl = ydst.ap[:, :, 1:tb + 1]
    P.dve(lambda nc: nc.vector.tensor_tensor(
        ysl, src.ap, pm.ap[:, 0:tb].unsqueeze(1).broadcast_to([128, CH, tb]),
        op=ALU.subtract),
        [src.r(), pm.r(b0=0, b1=tb * 4)], [ydst.r(b0=4)])
    P.dve(lambda nc: nc.vector.tensor_tensor(
        ysl, ysl, rstd.ap.unsqueeze(1).broadcast_to([128, CH, tb]),
        op=ALU.mult),
        [ydst.r(b0=4), rstd.r()], [ydst.r(b0=4)])
    if y_bf_out is not None:
        P.act(lambda nc: nc.scalar.copy(y_bf_out.ap, ysl),
              [ydst.r(b0=4)], [y_bf_out.r()])



def emit_l0ffn(P, A, nc, x_in, x_out, wr_bf, wk_bf, wv_bf, mu_sb, ones_bf, psum, first_call):
    TB = 512
    NB = T // TB
    A.mark()
    s_x = [A.alloc(f"l0f_x{i}", [128, CH, TB], F32) for i in range(2)]
    s_y = [A.alloc(f"l0f_y{i}", [128, CH, TB + 2], BF16) for i in range(2)]
    s_xbf = [A.alloc(f"l0f_xbf{i}", [128, CH, TB], BF16) for i in range(2)]
    s_m2 = A.alloc("l0f_m2", [128, TB], F32)
    s_rstd = A.alloc("l0f_rstd", [128, TB], F32)
    s_mbf = A.alloc("l0f_mbf", [128, 2 * TB], BF16)
    s_xr = A.alloc("l0f_xr", [128, CH, TB], BF16)
    s_xk = A.alloc("l0f_xk", [128, CH, TB], BF16)
    s_d = A.alloc("l0f_d", [128, CH, TB], F32)
    s_sr = A.alloc("l0f_sr", [128, CH, TB], BF16)
    s_ka = A.alloc("l0f_ka", [128, CH, TB], BF16)
    s_t = A.alloc("l0f_t", [128, TB], F32)

    xin_v = x_in.rearrange("(h p) t -> p h t", p=128)
    xout_v = x_out.rearrange("(h p) t -> p h t", p=128)

    for b in range(NB):
        xb = s_x[b % 2]
        yb = s_y[b % 2]
        yprev = s_y[(b + 1) % 2]
        xbf = s_xbf[b % 2]
        t0 = b * TB
        P.dma('SYNC', 'xin', lambda nc, xb=xb, t0=t0: nc.sync.dma_start(
            out=xb.ap, in_=xin_v[:, :, t0:t0 + TB]),
            [dram_reg(x_in, t0, t0 + TB)], [xb.r()])
        P.dma('GP', 'xbf', lambda nc, xbf=xbf, t0=t0: nc.gpsimd.dma_start(
            out=xbf.ap, in_=xin_v[:, :, t0:t0 + TB]),
            [dram_reg(x_in, t0, t0 + TB)], [xbf.r()])
        if b == 0:
            P.dve(lambda nc, yb=yb: nc.vector.memset(yb.ap[:, :, 0:2], 0.0),
                  [], [yb.r()])
        else:
            P.dve(lambda nc, yb=yb, yprev=yprev: nc.vector.tensor_copy(
                yb.ap[:, :, 1:2], yprev.ap[:, :, TB + 1:TB + 2]),
                [yprev.r(b0=(TB + 1) * 2)], [yb.r()])
        emit_ln_into(P, None, yb, 2, xbf, s_m2, s_rstd, psum[0], psum[1],
                     ones_bf, TB, TB + 2, mbf=s_mbf)
        ysh = yb.ap[:, :, 1:TB + 1]
        ycur = yb.ap[:, :, 2:TB + 2]
        P.dve(lambda nc, ycur=ycur, ysh=ysh: nc.vector.tensor_tensor(
            s_d.ap, ycur, ysh, op=ALU.subtract),
            [yb.r()], [s_d.r()])
        for h in range(CH):
            P.dve(lambda nc, h=h, yb=yb: nc.vector.scalar_tensor_tensor(
                s_xr.ap[:, h, :], s_d.ap[:, h, :], mu_sb.ap[:, 0, h:h + 1],
                yb.ap[:, h, 1:TB + 1], op0=ALU.mult, op1=ALU.add),
                [s_d.r(b0=h * TB * 4, b1=(h + 1) * TB * 4), mu_sb.r(), yb.r()],
                [s_xr.r(b0=h * TB * 2, b1=(h + 1) * TB * 2)])
            P.dve(lambda nc, h=h, yb=yb: nc.vector.scalar_tensor_tensor(
                s_xk.ap[:, h, :], s_d.ap[:, h, :], mu_sb.ap[:, 1, h:h + 1],
                yb.ap[:, h, 1:TB + 1], op0=ALU.mult, op1=ALU.add),
                [s_d.r(b0=h * TB * 4, b1=(h + 1) * TB * 4), mu_sb.r(), yb.r()],
                [s_xk.r(b0=h * TB * 2, b1=(h + 1) * TB * 2)])
        for (wbf, src, dst, bank0) in ((wr_bf, s_xr, s_sr, 2), (wk_bf, s_xk, s_ka, 4)):
            for dh in range(CH):
                pb = psum[bank0 + dh % 2] if bank0 == 2 else (psum[4] if dh % 2 == 0 else psum[7])
                for h in range(CH):
                    P.pe(lambda nc, h=h, dh=dh, pb=pb, wbf=wbf, src=src: nc.tensor.matmul(
                        pb.ap, wbf.ap[:, h, dh * 128:(dh + 1) * 128], src.ap[:, h, :],
                        start=(h == 0), stop=(h == CH - 1)),
                        [wbf.r(b0=(h * 1024 + dh * 128) * 2, b1=(h * 1024 + dh * 128 + 128) * 2),
                         src.r(b0=h * TB * 2, b1=(h + 1) * TB * 2)],
                        [pb.r(b0=0, b1=TB * 4)])
                if dst is s_sr:
                    P.act(lambda nc, dh=dh, pb=pb, dst=dst: nc.scalar.activation(
                        dst.ap[:, dh, :], pb.ap[:, 0:TB], AF.Sigmoid),
                        [pb.r(b0=0, b1=TB * 4)],
                        [dst.r(b0=dh * TB * 2, b1=(dh + 1) * TB * 2)])
                else:
                    P.act(lambda nc, dh=dh, pb=pb, dst=dst: nc.scalar.activation(
                        dst.ap[:, dh, :], pb.ap[:, 0:TB], AF.Relu),
                        [pb.r(b0=0, b1=TB * 4)],
                        [dst.r(b0=dh * TB * 2, b1=(dh + 1) * TB * 2)])
        P.dve(lambda nc: nc.vector.tensor_tensor(s_ka.ap, s_ka.ap, s_ka.ap, op=ALU.mult),
              [s_ka.r()], [s_ka.r()])
        for dh in range(CH):
            pb = psum[2 + dh % 2]
            for h in range(CH):
                P.pe(lambda nc, h=h, dh=dh, pb=pb: nc.tensor.matmul(
                    pb.ap, wv_bf.ap[:, h, dh * 128:(dh + 1) * 128], s_ka.ap[:, h, :],
                    start=(h == 0), stop=(h == CH - 1)),
                    [wv_bf.r(b0=(h * 1024 + dh * 128) * 2, b1=(h * 1024 + dh * 128 + 128) * 2),
                     s_ka.r(b0=h * TB * 2, b1=(h + 1) * TB * 2)],
                    [pb.r(b0=0, b1=TB * 4)])
            P.dve(lambda nc, dh=dh, pb=pb: nc.vector.tensor_tensor(
                s_t.ap, pb.ap[:, 0:TB], s_sr.ap[:, dh, :], op=ALU.mult),
                [pb.r(b0=0, b1=TB * 4), s_sr.r(b0=dh * TB * 2, b1=(dh + 1) * TB * 2)],
                [s_t.r()])
            P.dve(lambda nc, dh=dh, xb=xb: nc.vector.scalar_tensor_tensor(
                xb.ap[:, dh, :], s_t.ap, 0.5, xb.ap[:, dh, :],
                op0=ALU.mult, op1=ALU.add),
                [s_t.r(), xb.r(b0=dh * TB * 4, b1=(dh + 1) * TB * 4)],
                [xb.r(b0=dh * TB * 4, b1=(dh + 1) * TB * 4)])
        P.dma('SYNC', 'xout', lambda nc, xb=xb, t0=t0: nc.sync.dma_start(
            out=xout_v[:, :, t0:t0 + TB], in_=xb.ap),
            [xb.r()], [dram_reg(x_out, t0, t0 + TB)])
    A.release()



def build_l0ffn_only(serialize=False):
    nc = bass.Bass('TRN2', target_bir_lowering=False)
    x_in = nc.dram_tensor("x_in", [C, T], F32, kind="ExternalInput").ap()
    x_out = nc.dram_tensor("x_out", [C, T], F32, kind="ExternalOutput").ap()
    wr = WeightDram(nc, "l0f_WrT", C, C)
    wk = WeightDram(nc, "l0f_WkT", C, C)
    wv = WeightDram(nc, "l0f_WvT", C, C)
    mu = nc.dram_tensor("l0f_mu", [128, 2, CH], F32, kind="ExternalInput").ap()
    ones = nc.dram_tensor(CONST_ONES_SCALED, [128, 128], BF16, kind="ExternalInput").ap()
    with contextlib.ExitStack() as stack:
        A = stack.enter_context(Arena(nc))
        psum = [stack.enter_context(PsumTile(nc, i)) for i in range(5)] + [None, None]
        psum.append(stack.enter_context(PsumTile(nc, 7)))
        P = Prog(nc, serialize=serialize)
        wr_bf = A.alloc("wrbf", [128, CH, C], BF16)
        wk_bf = A.alloc("wkbf", [128, CH, C], BF16)
        wv_bf = A.alloc("wvbf", [128, CH, C], BF16)
        mu_sb = A.alloc("musb", [128, 2, CH], F32)
        ones_bf = A.alloc("ones", [128, 128], BF16)
        load_weight(P, wr, wr_bf)
        load_weight(P, wk, wk_bf)
        load_weight(P, wv, wv_bf)
        P.dma('SYNC', 'xin', lambda nc: nc.sync.dma_start(out=mu_sb.ap, in_=mu),
              [reg("l0f_mu")], [mu_sb.r()])
        P.dma('SYNC', 'xin', lambda nc: nc.sync.dma_start(out=ones_bf.ap, in_=ones),
              [reg(CONST_ONES_SCALED)], [ones_bf.r()])
        emit_l0ffn(P, A, nc, x_in, x_out, wr_bf, wk_bf, wv_bf, mu_sb, ones_bf,
                   psum, True)
        P.build()
        P.close()
    return nc


class PsumTile:
    def __init__(self, nc, i):
        self.cm = nc.psum_tensor(f"pb{i}", [128, 512], F32)
        self.name = f"pb{i}"

    def __enter__(self):
        t = self.cm.__enter__()
        self.ap = t[:]
        return self

    def __exit__(self, *a):
        return self.cm.__exit__(*a)

    def r(self, p0=0, p1=128, b0=0, b1=None):
        if b1 is None:
            b1 = 2048
        return (self.name, p0, p1, b0, b1)





def emit_l0attn(P, A, nc, x_in, x_out, wk_bf, wv_bf, wr_bf, wo_bf, tm_sb,
                td_sb, tf_sb, ones_bf, psum):
    TB = 256
    NB = T // TB
    A.mark()
    s_x = [A.alloc(f"l0a_x{i}", [128, CH, TB], F32) for i in range(2)]
    s_y = A.alloc("l0a_y", [128, CH, TB + 2], BF16)
    s_mbf = A.alloc("l0a_mbf", [128, 2 * TB], BF16)
    s_ycar = A.alloc("l0a_ycar", [128, CH, 1], BF16)
    s_xbf = A.alloc("l0a_xbf", [128, CH, TB], BF16)
    s_m2 = A.alloc("l0a_m2", [128, TB], F32)
    s_rstd = A.alloc("l0a_rstd", [128, TB], F32)
    s_dm = A.alloc("l0a_dm", [128, TB], F32)
    s_mx = [A.alloc(f"l0a_mx{i}", [128, CH, TB], BF16) for i in range(3)]
    s_k = [A.alloc(f"l0a_k{i}", [128, CH, TB], BF16) for i in range(2)]
    s_v = [A.alloc(f"l0a_v{i}", [128, CH, TB], BF16) for i in range(2)]
    s_sr = [A.alloc(f"l0a_sr{i}", [128, CH, TB], BF16) for i in range(2)]
    s_sy = [A.alloc(f"l0a_sy{i}", [128, CH, TB], BF16) for i in range(2)]
    s_P = A.alloc("l0a_P", [128, CH, TB + 1], F32)
    s_A = A.alloc("l0a_A", [128, CH, TB + 1], BF16)
    s_B = A.alloc("l0a_B", [128, CH, TB + 1], BF16)
    s_F12 = A.alloc("l0a_F12", [128, CH, 2 * TB], BF16)
    s_G = A.alloc("l0a_G", [128, CH, TB], BF16)
    s_N = A.alloc("l0a_N", [128, CH, TB], BF16)
    s_T12 = A.alloc("l0a_T12", [128, CH, 2 * TB], F32)
    s_w = A.alloc("l0a_w", [128, CH], F32)

    xin_v = x_in.rearrange("(h p) t -> p h t", p=128)
    xout_v = x_out.rearrange("(h p) t -> p h t", p=128)

    P.act(lambda nc: nc.scalar.activation(s_w.ap, td_sb.ap, AF.Exp),
          [td_sb.r()], [s_w.r()])
    P.dve(lambda nc: nc.vector.tensor_scalar(s_w.ap, s_w.ap, -1.0, None, op0=ALU.mult),
          [s_w.r()], [s_w.r()])
    P.dve(lambda nc: nc.vector.memset(s_P.ap, NEG), [], [s_P.r()])
    P.dve(lambda nc: nc.vector.memset(s_A.ap, 0.0), [], [s_A.r()])
    P.dve(lambda nc: nc.vector.memset(s_B.ap, 0.0), [], [s_B.r()])
    P.dve(lambda nc: nc.vector.memset(s_ycar.ap, 0.0), [], [s_ycar.r()])

    def stage1(b):
        t0 = b * TB
        P.dma('GP', 'xbf', lambda nc, t0=t0: nc.gpsimd.dma_start(
            out=s_xbf.ap, in_=xin_v[:, :, t0:t0 + TB]),
            [dram_reg(x_in, t0, t0 + TB)], [s_xbf.r()])
        P.dve(lambda nc: nc.vector.tensor_copy(s_y.ap[:, :, 1:2], s_ycar.ap),
              [s_ycar.r()], [s_y.r()])
        emit_ln_into(P, None, s_y, 2, s_xbf, s_m2, s_rstd, psum[0], psum[1],
                     ones_bf, TB, TB + 2, mbf=s_mbf)
        P.dve(lambda nc: nc.vector.tensor_copy(s_ycar.ap, s_y.ap[:, :, TB + 1:TB + 2]),
              [s_y.r(b0=(TB + 1) * 2)], [s_ycar.r()])
        for h in range(CH):
            P.dve(lambda nc, h=h: nc.vector.tensor_tensor(
                s_dm.ap, s_y.ap[:, h, 2:TB + 2], s_y.ap[:, h, 1:TB + 1], op=ALU.subtract),
                [s_y.r(b0=h * (TB + 2) * 2, b1=(h + 1) * (TB + 2) * 2)], [s_dm.r()])
            for j in range(3):
                P.dve(lambda nc, h=h, j=j: nc.vector.scalar_tensor_tensor(
                    s_mx[j].ap[:, h, :], s_dm.ap, tm_sb.ap[:, j, h:h + 1],
                    s_y.ap[:, h, 1:TB + 1], op0=ALU.mult, op1=ALU.add),
                    [s_dm.r(), tm_sb.r(),
                     s_y.r(b0=h * (TB + 2) * 2, b1=(h + 1) * (TB + 2) * 2)],
                    [s_mx[j].r(b0=h * TB * 2, b1=(h + 1) * TB * 2)])
        for (wbf, mxi, dst, act, bank0) in ((wk_bf, 0, s_k[b % 2], AF.Identity, 2),
                                            (wv_bf, 1, s_v[b % 2], AF.Identity, 4),
                                            (wr_bf, 2, s_sr[b % 2], AF.Sigmoid, 2)):
            src = s_mx[mxi]
            for dh in range(CH):
                pb = psum[bank0 + dh % 2] if bank0 == 2 else (psum[4] if dh % 2 == 0 else psum[7])
                for h in range(CH):
                    P.pe(lambda nc, h=h, dh=dh, pb=pb, wbf=wbf, src=src: nc.tensor.matmul(
                        pb.ap[:, 0:TB], wbf.ap[:, h, dh * 128:(dh + 1) * 128],
                        src.ap[:, h, :],
                        start=(h == 0), stop=(h == CH - 1)),
                        [wbf.r(b0=(h * 1024 + dh * 128) * 2, b1=(h * 1024 + dh * 128 + 128) * 2),
                         src.r(b0=h * TB * 2, b1=(h + 1) * TB * 2)],
                        [pb.r(b0=0, b1=TB * 4)])
                P.act(lambda nc, dh=dh, pb=pb, dst=dst, act=act: nc.scalar.activation(
                    dst.ap[:, dh, :], pb.ap[:, 0:TB], act),
                    [pb.r(b0=0, b1=TB * 4)],
                    [dst.r(b0=dh * TB * 2, b1=(dh + 1) * TB * 2)])

    def stage2(b):
        kb, vb, srb = s_k[b % 2], s_v[b % 2], s_sr[b % 2]
        xb = s_x[b % 2]
        P.dma('SYNC', 'xin', lambda nc, xb=xb, t0=b * TB: nc.sync.dma_start(
            out=xb.ap, in_=xin_v[:, :, t0:t0 + TB]),
            [dram_reg(x_in, b * TB, (b + 1) * TB)], [xb.r()])
        if b > 0:
            for s in (s_P, s_A, s_B):
                P.dve(lambda nc, s=s: nc.vector.tensor_copy(
                    s.ap[:, :, 0:1], s.ap[:, :, TB:TB + 1]),
                    [s.r(b0=TB * 4)], [s.r()])
        for h in range(CH):
            P.dve(lambda nc, h=h: nc.vector.tensor_tensor_scan(
                s_P.ap[:, h, 1:TB + 1],
                s_w.ap[:, h:h + 1].broadcast_to([128, TB]),
                kb.ap[:, h, :], s_P.ap[:, h, 0:1], op0=ALU.add, op1=ALU.max),
                [s_w.r(), kb.r(b0=h * TB * 2, b1=(h + 1) * TB * 2),
                 s_P.r(b0=h * (TB + 1) * 4, b1=h * (TB + 1) * 4 + 4)],
                [s_P.r(b0=(h * (TB + 1) + 1) * 4, b1=(h + 1) * (TB + 1) * 4)])
        psh = s_P.ap[:, :, 0:TB]
        pcur = s_P.ap[:, :, 1:TB + 1]
        F1 = s_F12.ap[:, :, 0:TB]
        F2 = s_F12.ap[:, :, TB:2 * TB]
        T1 = s_T12.ap[:, :, 0:TB]
        T2 = s_T12.ap[:, :, TB:2 * TB]
        for h in range(CH):
            P.dve(lambda nc, h=h: nc.vector.scalar_tensor_tensor(
                s_T12.ap[:, h, 0:TB], s_P.ap[:, h, 0:TB], s_w.ap[:, h:h + 1],
                s_P.ap[:, h, 1:TB + 1], op0=ALU.add, op1=ALU.subtract),
                [s_P.r(b0=h * (TB + 1) * 4, b1=(h + 1) * (TB + 1) * 4), s_w.r()],
                [s_T12.r(b0=h * 2 * TB * 4, b1=(h * 2 + 1) * TB * 4)])
        P.dve(lambda nc, pcur=pcur, T2=T2: nc.vector.tensor_tensor(
            T2, kb.ap, pcur, op=ALU.subtract),
            [kb.r(), s_P.r()], [s_T12.r()])
        P.act(lambda nc: nc.scalar.activation(s_F12.ap, s_T12.ap, AF.Exp),
              [s_T12.r()], [s_F12.r()])
        P.dve(lambda nc, F2=F2: nc.vector.tensor_tensor(
            s_G.ap, F2, vb.ap, op=ALU.mult),
            [s_F12.r(), vb.r()], [s_G.r()])
        for h in range(CH):
            P.dve(lambda nc, h=h: nc.vector.tensor_tensor_scan(
                s_A.ap[:, h, 1:TB + 1], s_F12.ap[:, h, 0:TB], s_G.ap[:, h, :],
                s_A.ap[:, h, 0:1], op0=ALU.mult, op1=ALU.add),
                [s_F12.r(), s_G.r(b0=h * TB * 2, b1=(h + 1) * TB * 2),
                 s_A.r(b0=h * (TB + 1) * 2, b1=h * (TB + 1) * 2 + 2)],
                [s_A.r(b0=(h * (TB + 1) + 1) * 2, b1=(h + 1) * (TB + 1) * 2)])
            P.dve(lambda nc, h=h: nc.vector.tensor_tensor_scan(
                s_B.ap[:, h, 1:TB + 1], s_F12.ap[:, h, 0:TB], s_F12.ap[:, h, TB:2 * TB],
                s_B.ap[:, h, 0:1], op0=ALU.mult, op1=ALU.add),
                [s_F12.r(),
                 s_B.r(b0=h * (TB + 1) * 2, b1=h * (TB + 1) * 2 + 2)],
                [s_B.r(b0=(h * (TB + 1) + 1) * 2, b1=(h + 1) * (TB + 1) * 2)])
        for h in range(CH):
            P.act(lambda nc, h=h: nc.scalar.activation(
                s_T12.ap[:, h, TB:2 * TB], kb.ap[:, h, :], AF.Identity,
                bias=tf_sb.ap[:, h:h + 1]),
                [kb.r(b0=h * TB * 2, b1=(h + 1) * TB * 2), tf_sb.r()],
                [s_T12.r(b0=(h * 2 + 1) * TB * 4, b1=(h + 1) * 2 * TB * 4)])
        P.dve(lambda nc, psh=psh, T1=T1, T2=T2: nc.vector.tensor_tensor(
            T1, psh, T2, op=ALU.subtract),
            [s_P.r(), s_T12.r()], [s_T12.r()])
        P.dve(lambda nc, T1=T1, T2=T2: nc.vector.tensor_scalar(
            T2, T1, -1.0, 0.0, op0=ALU.mult, op1=ALU.min),
            [s_T12.r()], [s_T12.r()])
        P.dve(lambda nc, T1=T1: nc.vector.tensor_scalar(
            T1, T1, 0.0, None, op0=ALU.min),
            [s_T12.r()], [s_T12.r()])
        P.act(lambda nc: nc.scalar.activation(s_F12.ap, s_T12.ap, AF.Exp),
              [s_T12.r()], [s_F12.r()])
        P.dve(lambda nc, F1=F1: nc.vector.tensor_tensor(
            s_G.ap, F1, s_B.ap[:, :, 0:TB], op=ALU.mult),
            [s_F12.r(), s_B.r()], [s_G.r()])
        P.dve(lambda nc, F2=F2: nc.vector.tensor_tensor(
            s_G.ap, s_G.ap, F2, op=ALU.add),
            [s_G.r(), s_F12.r()], [s_G.r()])
        P.act(lambda nc: nc.scalar.activation(s_G.ap, s_G.ap, AF.Ln),
              [s_G.r()], [s_G.r()])
        P.act(lambda nc: nc.scalar.activation(s_G.ap, s_G.ap, AF.Exp, scale=-1.0),
              [s_G.r()], [s_G.r()])
        P.dve(lambda nc, F1=F1: nc.vector.tensor_tensor(
            s_N.ap, F1, s_A.ap[:, :, 0:TB], op=ALU.mult),
            [s_F12.r(), s_A.r()], [s_N.r()])
        P.dve(lambda nc, F1=F1, F2=F2: nc.vector.tensor_tensor(
            F1, F2, vb.ap, op=ALU.mult),
            [s_F12.r(), vb.r()], [s_F12.r()])
        P.dve(lambda nc, F1=F1: nc.vector.tensor_tensor(
            s_N.ap, s_N.ap, F1, op=ALU.add),
            [s_N.r(), s_F12.r()], [s_N.r()])
        P.dve(lambda nc: nc.vector.tensor_tensor(
            s_N.ap, s_N.ap, s_G.ap, op=ALU.mult),
            [s_N.r(), s_G.r()], [s_N.r()])
        syb = s_sy[b % 2]
        P.dve(lambda nc, syb=syb: nc.vector.tensor_tensor(
            syb.ap, s_N.ap, srb.ap, op=ALU.mult),
            [s_N.r(), srb.r()], [syb.r()])

    def stage3(b):
        xb = s_x[b % 2]
        syb = s_sy[b % 2]
        t0 = b * TB
        for dh in range(CH):
            pb = psum[4] if dh % 2 == 0 else psum[7]
            for h in range(CH):
                P.pe(lambda nc, h=h, dh=dh, pb=pb, syb=syb: nc.tensor.matmul(
                    pb.ap[:, 0:TB], wo_bf.ap[:, h, dh * 128:(dh + 1) * 128],
                    syb.ap[:, h, :],
                    start=(h == 0), stop=(h == CH - 1)),
                    [wo_bf.r(b0=(h * 1024 + dh * 128) * 2, b1=(h * 1024 + dh * 128 + 128) * 2),
                     syb.r(b0=h * TB * 2, b1=(h + 1) * TB * 2)],
                    [pb.r(b0=0, b1=TB * 4)])
            P.dve(lambda nc, dh=dh, pb=pb, xb=xb: nc.vector.tensor_tensor(
                xb.ap[:, dh, :], pb.ap[:, 0:TB], xb.ap[:, dh, :], op=ALU.add),
                [pb.r(b0=0, b1=TB * 4), xb.r(b0=dh * TB * 4, b1=(dh + 1) * TB * 4)],
                [xb.r(b0=dh * TB * 4, b1=(dh + 1) * TB * 4)])
        P.dma('SYNC', 'xout', lambda nc, xb=xb, t0=t0: nc.sync.dma_start(
            out=xout_v[:, :, t0:t0 + TB], in_=xb.ap),
            [xb.r()], [dram_reg(x_out, t0, t0 + TB)])

    stage1(0)
    for b in range(NB):
        if b + 1 < NB:
            stage1(b + 1)
        stage2(b)
        if b > 0:
            stage3(b - 1)
    stage3(NB - 1)
    A.release()


def emit_ln_bf(P, src, ydst, xbf, m2buf, rstd, dm_unused, pm, pq, ones_bf, tb):
    P.act(lambda nc: nc.scalar.copy(xbf.ap, src.ap), [src.r()], [xbf.r()])
    for h in range(CH):
        P.pe(lambda nc, h=h: nc.tensor.matmul(
            pm.ap[:, 0:tb], ones_bf.ap, xbf.ap[:, h, :],
            start=(h == 0), stop=(h == CH - 1)),
            [xbf.r(b0=h * tb * 2, b1=(h + 1) * tb * 2), ones_bf.r()],
            [pm.r(b0=0, b1=tb * 4)])
    P.act(lambda nc: nc.scalar.activation(xbf.ap, xbf.ap, AF.Square),
          [xbf.r()], [xbf.r()])
    for h in range(CH):
        P.pe(lambda nc, h=h: nc.tensor.matmul(
            pq.ap[:, 0:tb], ones_bf.ap, xbf.ap[:, h, :],
            start=(h == 0), stop=(h == CH - 1)),
            [xbf.r(b0=h * tb * 2, b1=(h + 1) * tb * 2), ones_bf.r()],
            [pq.r(b0=0, b1=tb * 4)])
    P.act(lambda nc: nc.scalar.activation(m2buf.ap[:, 0:tb], pm.ap[:, 0:tb], AF.Square),
          [pm.r(b0=0, b1=tb * 4)], [m2buf.r(b1=tb * 4)])
    P.dve(lambda nc: nc.vector.scalar_tensor_tensor(
        rstd.ap[:, 0:tb], pq.ap[:, 0:tb], 1e-5, m2buf.ap[:, 0:tb],
        op0=ALU.add, op1=ALU.subtract),
        [pq.r(b0=0, b1=tb * 4), m2buf.r(b1=tb * 4)], [rstd.r(b1=tb * 4)])
    P.act(lambda nc: nc.scalar.activation(rstd.ap[:, 0:tb], rstd.ap[:, 0:tb], AF.Ln),
          [rstd.r(b1=tb * 4)], [rstd.r(b1=tb * 4)])
    P.act(lambda nc: nc.scalar.activation(rstd.ap[:, 0:tb], rstd.ap[:, 0:tb], AF.Exp,
                                          scale=-0.5),
          [rstd.r(b1=tb * 4)], [rstd.r(b1=tb * 4)])
    ysl = ydst.ap[:, :, 1:tb + 1]
    P.dve(lambda nc, ysl=ysl: nc.vector.tensor_tensor(
        ysl, src.ap, pm.ap[:, 0:tb].unsqueeze(1).broadcast_to([128, CH, tb]),
        op=ALU.subtract),
        [src.r(), pm.r(b0=0, b1=tb * 4)], [ydst.r(b0=2)])
    P.dve(lambda nc, ysl=ysl: nc.vector.tensor_tensor(
        ysl, ysl, rstd.ap[:, 0:tb].unsqueeze(1).broadcast_to([128, CH, tb]),
        op=ALU.mult),
        [ydst.r(b0=2), rstd.r(b1=tb * 4)], [ydst.r(b0=2)])


def host_prep_l0attn(Wk, Wv, Wr, Wo, td, tf, tmk, tmv, tmr):
    d = {}
    d["l0a_WkT"] = np.ascontiguousarray(Wk.T)
    d["l0a_WvT"] = np.ascontiguousarray(Wv.T)
    d["l0a_WrT"] = np.ascontiguousarray(Wr.T)
    d["l0a_WoT"] = np.ascontiguousarray(Wo.T)
    d["l0a_td"] = np.ascontiguousarray(td.reshape(CH, 128).T)
    d["l0a_tf"] = np.ascontiguousarray(tf.reshape(CH, 128).T)
    d["l0a_tm"] = np.ascontiguousarray(
        np.stack([tmk.reshape(CH, 128).T, tmv.reshape(CH, 128).T,
                  tmr.reshape(CH, 128).T], 1))
    return d


def build_l0attn_only():
    nc = bass.Bass('TRN2', target_bir_lowering=False)
    x_in = nc.dram_tensor("x_in", [C, T], F32, kind="ExternalInput").ap()
    x_out = nc.dram_tensor("x_out", [C, T], F32, kind="ExternalOutput").ap()
    wk = WeightDram(nc, "l0a_WkT", C, C)
    wv = WeightDram(nc, "l0a_WvT", C, C)
    wr = WeightDram(nc, "l0a_WrT", C, C)
    wo = WeightDram(nc, "l0a_WoT", C, C)
    td = nc.dram_tensor("l0a_td", [128, CH], F32, kind="ExternalInput").ap()
    tf = nc.dram_tensor("l0a_tf", [128, CH], F32, kind="ExternalInput").ap()
    tm = nc.dram_tensor("l0a_tm", [128, 3, CH], F32, kind="ExternalInput").ap()
    ones = nc.dram_tensor(CONST_ONES_SCALED, [128, 128], BF16, kind="ExternalInput").ap()
    with contextlib.ExitStack() as stack:
        A = stack.enter_context(Arena(nc))
        psum = [stack.enter_context(PsumTile(nc, i)) for i in range(5)] + [None, None]
        psum.append(stack.enter_context(PsumTile(nc, 7)))
        P = Prog(nc)
        wk_bf = A.alloc("wkbf", [128, CH, C], BF16)
        wv_bf = A.alloc("wvbf", [128, CH, C], BF16)
        wr_bf = A.alloc("wrbf", [128, CH, C], BF16)
        wo_bf = A.alloc("wobf", [128, CH, C], BF16)
        td_sb = A.alloc("tdsb", [128, CH], F32)
        tf_sb = A.alloc("tfsb", [128, CH], F32)
        tm_sb = A.alloc("tmsb", [128, 3, CH], F32)
        ones_bf = A.alloc("ones", [128, 128], BF16)
        load_weight(P, wk, wk_bf)
        load_weight(P, wv, wv_bf)
        load_weight(P, wr, wr_bf)
        load_weight(P, wo, wo_bf)
        P.dma('SYNC', 'xin', lambda nc: nc.sync.dma_start(out=td_sb.ap, in_=td),
              [reg("l0a_td")], [td_sb.r()])
        P.dma('SYNC', 'xin', lambda nc: nc.sync.dma_start(out=tf_sb.ap, in_=tf),
              [reg("l0a_tf")], [tf_sb.r()])
        P.dma('SYNC', 'xin', lambda nc: nc.sync.dma_start(out=tm_sb.ap, in_=tm),
              [reg("l0a_tm")], [tm_sb.r()])
        P.dma('SYNC', 'xin', lambda nc: nc.sync.dma_start(out=ones_bf.ap, in_=ones),
              [reg(CONST_ONES_SCALED)], [ones_bf.r()])
        emit_l0attn(P, A, nc, x_in, x_out, wk_bf, wv_bf, wr_bf, wo_bf, tm_sb,
                    td_sb, tf_sb, ones_bf, psum)
        P.build()
        P.close()
    return nc






def emit_l1attn(P, A, nc, x_in, x_out, wq_d, wk_d, wlep_d, consts, psum, psum_bfs):
    TB = 512
    NB = T // TB
    TP = T + 4
    qb_sb, kb_sb, lepb_sb, ones_bf, ident_bf, cos_d, sin_d, sel_bf = consts
    A.mark()
    s_x0 = A.alloc("l1a_x0", [128, CH, TB], F32)
    s_x = [s_x0, s_x0]
    s_y = A.alloc("l1a_y", [128, CH, TP], BF16)
    s_xbf1 = A.alloc("l1a_xbf0", [128, CH, TB], BF16)
    s_xbf = [s_xbf1, s_xbf1]
    s_m2 = A.alloc("l1a_m2", [128, TB], F32)
    s_rstd = A.alloc("l1a_rstd", [128, TB], F32)
    s_mbf = A.alloc("l1a_mbf", [128, 2 * TB], BF16)
    s_q = A.alloc("l1a_q", [128, CH, T], BF16)
    s_z1 = A.alloc("l1a_z1", [128, TB], F32)
    s_z2 = A.alloc("l1a_z2", [128, TB], F32)
    s_z3 = s_m2
    s_z4 = s_rstd
    s_cs = [A.alloc(f"l1a_cs{i}", [128, TB], BF16) for i in range(4)]
    s_km = A.alloc("l1a_km", [128, CH], F32)
    s_kml = A.alloc("l1a_kml", [128, CH, CH], BF16)
    s_zbf = A.alloc("l1a_zbf", [128, T], BF16)
    s_vh = A.alloc("l1a_vh", [128, 16, 128], BF16)
    s_kr = A.alloc("l1a_kr", [128, 16, 128], BF16)
    s_kv = A.alloc("l1a_kv", [128, CH, 128], BF16)
    s_zr = [A.alloc(f"l1a_zr{i}", [128, TB], BF16) for i in range(2)]
    s_qs = [A.alloc(f"l1a_qs{i}", [128, TB], BF16) for i in range(2)]
    A.mark()
    s_k = A.alloc("l1a_k", [128, CH, T], BF16)
    wq_bf = A.alloc("l1a_wq", [128, CH, C], BF16)
    wk_bf = A.alloc("l1a_wk", [128, CH, C], BF16)
    load_weight(P, wq_d, wq_bf)
    load_weight(P, wk_d, wk_bf)

    xin_v = x_in.rearrange("(h p) t -> p h t", p=128)
    xout_v = x_out.rearrange("(h p) t -> p h t", p=128)

    P.dve(lambda nc: nc.vector.memset(s_y.ap[:, :, 0:2], 0.0), [], [s_y.r()])
    P.dve(lambda nc: nc.vector.memset(s_zbf.ap, 0.0), [], [s_zbf.r()])
    P.dve(lambda nc: nc.vector.memset(s_y.ap[:, :, TP - 2:TP], 0.0), [], [s_y.r()])

    def qk_block(wqk, bias, dst, b):
        t0 = b * TB
        for dh in range(CH):
            pb = psum[2 + dh % 2]
            for h in range(CH):
                P.pe(lambda nc, h=h, dh=dh, pb=pb, wqk=wqk, t0=t0: nc.tensor.matmul(
                    pb.ap, wqk.ap[:, h, dh * 128:(dh + 1) * 128],
                    s_y.ap[:, h, 2 + t0:2 + t0 + TB],
                    start=(h == 0), stop=(h == CH - 1)),
                    [wqk.r(b0=(h * 1024 + dh * 128) * 2, b1=(h * 1024 + dh * 128 + 128) * 2),
                     s_y.r(b0=h * TP * 2, b1=(h + 1) * TP * 2)],
                    [pb.r()])
            P.act(lambda nc, pb=pb, bias=bias, dh=dh: nc.scalar.activation(
                s_z1.ap, pb.ap, AF.Identity, bias=bias.ap[:, dh:dh + 1]),
                [pb.r(), bias.r()], [s_z1.r()])
            P.dve(lambda nc: nc.vector.tensor_scalar(
                s_z2.ap, s_z1.ap, 0.0, None, op0=ALU.min),
                [s_z1.r()], [s_z2.r()])
            P.act(lambda nc: nc.scalar.activation(s_z2.ap, s_z2.ap, AF.Exp),
                  [s_z2.r()], [s_z2.r()])
            P.dve(lambda nc: nc.vector.tensor_scalar(
                s_z1.ap, s_z1.ap, 0.0, None, op0=ALU.max),
                [s_z1.r()], [s_z1.r()])
            P.dve(lambda nc, dst=dst, dh=dh, t0=t0: nc.vector.tensor_tensor(
                dst.ap[:, dh, t0:t0 + TB], s_z1.ap, s_z2.ap, op=ALU.add),
                [s_z1.r(), s_z2.r()],
                [dst.r(b0=(dh * T + t0) * 2, b1=(dh * T + t0 + TB) * 2)])

    for b in range(NB):
        t0 = b * TB
        xbf = s_xbf[b % 2]
        P.dma('GP', 'xbf', lambda nc, xbf=xbf, t0=t0: nc.gpsimd.dma_start(
            out=xbf.ap, in_=xin_v[:, :, t0:t0 + TB]),
            [dram_reg(x_in, t0, t0 + TB)], [xbf.r()])
        emit_ln_into(P, None, s_y, 2 + t0, xbf, s_m2, s_rstd, psum[0], psum[1],
                     ones_bf, TB, TP, mbf=s_mbf)
        qk_block(wq_bf, qb_sb, s_q, b)
        qk_block(wk_bf, kb_sb, s_k, b)

    P.dve(lambda nc: nc.vector.tensor_reduce(
        s_km.ap.unsqueeze(2), s_k.ap, axis=AX.X, op=ALU.add),
        [s_k.r()], [s_km.r()])
    P.dve(lambda nc: nc.vector.tensor_scalar(
        s_km.ap, s_km.ap, 1.0 / T, None, op0=ALU.mult),
        [s_km.r()], [s_km.r()])
    P.dve(lambda nc: nc.vector.memset(s_kml.ap, 0.0), [], [s_kml.r()])
    for j in range(CH):
        hcol = 2 * (j % 4)
        P.dve(lambda nc, j=j, hcol=hcol: nc.vector.tensor_copy(
            s_kml.ap[0:64, j, hcol:hcol + 1], s_km.ap[0:64, j:j + 1]),
            [s_km.r(p1=64)], [s_kml.r(p1=64)])
        P.dve(lambda nc, j=j, hcol=hcol: nc.vector.tensor_copy(
            s_kml.ap[64:128, j, hcol + 1:hcol + 2], s_km.ap[64:128, j:j + 1]),
            [s_km.r(p0=64)], [s_kml.r(p0=64)])
    for b in range(NB):
        t0 = b * TB
        pb = psum[4]
        for j in range(CH):
            P.pe(lambda nc, j=j, t0=t0, pb=pb: nc.tensor.matmul(
                pb.ap[0:8, :], s_kml.ap[:, j, :], s_q.ap[:, j, t0:t0 + TB],
                start=(j == 0), stop=(j == CH - 1)),
                [s_kml.r(), s_q.r(b0=(j * T + t0) * 2, b1=(j * T + t0 + TB) * 2)],
                [pb.r(p1=8)])
        P.dve(lambda nc, pb=pb: nc.vector.tensor_scalar(
            s_z1.ap[0:8, :], pb.ap[0:8, :], 1e-6, None, op0=ALU.add),
            [pb.r(p1=8)], [s_z1.r(p1=8)])
        P.dve(lambda nc: nc.vector.reciprocal(s_z1.ap[0:8, :], s_z1.ap[0:8, :]),
              [s_z1.r(p1=8)], [s_z1.r(p1=8)])
        P.act(lambda nc, t0=t0: nc.scalar.copy(s_zbf.ap[0:8, t0:t0 + TB], s_z1.ap[0:8, :]),
              [s_z1.r(p1=8)], [s_zbf.r(p1=8, b0=t0 * 2, b1=(t0 + TB) * 2)])

    if L1A_PHASES < 3:
        A.release(); A.release()
        return
    if DEBUG:
        P.dump(nc, "dbg_zbf", s_zbf)
        P.dump(nc, "dbg_km", s_km)
    zb1 = s_z1.ap.bitcast(BF16)[:, 0:TB]
    zb2 = s_z1.ap.bitcast(BF16)[:, TB:2 * TB]
    zb3 = s_z2.ap.bitcast(BF16)[:, 0:TB]
    zb4 = s_z2.ap.bitcast(BF16)[:, TB:2 * TB]
    csi = [0]

    def rope_src(src_):
        for j in range(4):
            for b in range(NB):
                t0 = b * TB
                cs_c = s_cs[csi[0] % 2]
                cs_s = s_cs[2 + csi[0] % 2]
                csi[0] += 1
                P.dma('SYNC', 'xin', lambda nc, j=j, t0=t0, cs_c=cs_c: nc.sync.dma_start(
                    out=cs_c.ap, in_=cos_d[:, j, t0:t0 + TB]),
                    [reg("c_cos")], [cs_c.r()])
                P.dma('SYNC', 'xin', lambda nc, j=j, t0=t0, cs_s=cs_s: nc.sync.dma_start(
                    out=cs_s.ap, in_=sin_d[:, j, t0:t0 + TB]),
                    [reg("c_sin")], [cs_s.r()])
                P.dve(lambda nc, src_=src_, j=j, t0=t0, cs_c=cs_c: nc.vector.tensor_tensor(
                    zb1, cs_c.ap, src_.ap[:, j, t0:t0 + TB], op=ALU.mult),
                    [cs_c.r(), src_.r(b0=(j * T + t0) * 2, b1=(j * T + t0 + TB) * 2)],
                    [s_z1.r(b1=TB * 2)])
                P.dve(lambda nc, src_=src_, j=j, t0=t0, cs_s=cs_s: nc.vector.tensor_tensor(
                    zb2, cs_s.ap, src_.ap[:, 4 + j, t0:t0 + TB], op=ALU.mult),
                    [cs_s.r(), src_.r(b0=((4 + j) * T + t0) * 2, b1=((4 + j) * T + t0 + TB) * 2)],
                    [s_z1.r(b0=TB * 2)])
                P.dve(lambda nc, src_=src_, j=j, t0=t0, cs_s=cs_s: nc.vector.tensor_tensor(
                    zb3, cs_s.ap, src_.ap[:, j, t0:t0 + TB], op=ALU.mult),
                    [cs_s.r(), src_.r(b0=(j * T + t0) * 2, b1=(j * T + t0 + TB) * 2)],
                    [s_z2.r(b1=TB * 2)])
                P.dve(lambda nc, src_=src_, j=j, t0=t0, cs_c=cs_c: nc.vector.tensor_tensor(
                    zb4, cs_c.ap, src_.ap[:, 4 + j, t0:t0 + TB], op=ALU.mult),
                    [cs_c.r(), src_.r(b0=((4 + j) * T + t0) * 2, b1=((4 + j) * T + t0 + TB) * 2)],
                    [s_z2.r(b0=TB * 2)])
                P.dve(lambda nc, src_=src_, j=j, t0=t0: nc.vector.tensor_tensor(
                    src_.ap[:, 4 + j, t0:t0 + TB], zb1, zb2, op=ALU.subtract),
                    [s_z1.r()],
                    [src_.r(b0=((4 + j) * T + t0) * 2, b1=((4 + j) * T + t0 + TB) * 2)])
                P.dve(lambda nc, src_=src_, j=j, t0=t0: nc.vector.tensor_tensor(
                    src_.ap[:, j, t0:t0 + TB], zb3, zb4, op=ALU.add),
                    [s_z2.r()],
                    [src_.r(b0=(j * T + t0) * 2, b1=(j * T + t0 + TB) * 2)])

    rope_src(s_k)
    for h in range(min(CH, L1A_P4HEADS)):
        jsrc = 4 + h if h < 4 else h - 4
        for tt in range(min(16, L1A_P4TT)):
            pA, pB = psum_bfs
            P.pe(lambda nc, h=h, tt=tt, pA=pA: nc.tensor.transpose(
                pA.ap[:, 0:128], s_y.ap[:, h, 2 + tt * 128:2 + (tt + 1) * 128],
                ident_bf.ap[:, 0:128]),
                [s_y.r(b0=h * TP * 2, b1=(h + 1) * TP * 2), ident_bf.r()],
                [pA.r(b1=256)])
            P.act(lambda nc, tt=tt, pA=pA: nc.scalar.copy(
                s_vh.ap[:, tt, :], pA.ap[:, 0:128]),
                [pA.r(b1=256)], [s_vh.r(b0=tt * 256, b1=(tt + 1) * 256)])
            if not L1A_P4KR:
                continue
            P.pe(lambda nc, jsrc=jsrc, tt=tt, pB=pB: nc.tensor.transpose(
                pB.ap[:, 0:128],
                s_k.ap[:, jsrc, tt * 128:(tt + 1) * 128],
                ident_bf.ap[:, 0:128]),
                [s_k.r(b0=jsrc * T * 2, b1=(jsrc + 1) * T * 2), ident_bf.r()],
                [pB.r(b1=256)])
            P.act(lambda nc, tt=tt, pB=pB: nc.scalar.copy(
                s_kr.ap[:, tt, :], pB.ap[:, 0:128]),
                [pB.r(b1=256)], [s_kr.r(b0=tt * 256, b1=(tt + 1) * 256)])
        if L1A_P4SUB >= 2:
            pk = psum[7]
            for tt in range(16):
                P.pe(lambda nc, tt=tt, pk=pk: nc.tensor.matmul(
                    pk.ap[:, 0:128], s_kr.ap[:, tt, :], s_vh.ap[:, tt, :],
                    start=(tt == 0), stop=(tt == 15)),
                    [s_kr.r(b0=tt * 256, b1=(tt + 1) * 256),
                     s_vh.r(b0=tt * 256, b1=(tt + 1) * 256)],
                    [pk.r(b1=512)])
            P.act(lambda nc, h=h, pk=pk: nc.scalar.copy(s_kv.ap[:, h, :], pk.ap[:, 0:128]),
                  [pk.r(b1=512)], [s_kv.r(b0=h * 256, b1=(h + 1) * 256)])

    if L1A_PHASES < 5:
        A.release(); A.release()
        return
    rope_src(s_q)

    A.release()
    wlep_bf = A.alloc("l1a_wlep", [128, 3 * CH, C], BF16)
    load_weight(P, wlep_d, wlep_bf, nsplit=3)
    for b in range(NB):
        t0 = b * TB
        xb = s_x[b % 2]
        P.dma('SYNC', 'xin', lambda nc, xb=xb, t0=t0: nc.sync.dma_start(
            out=xb.ap, in_=xin_v[:, :, t0:t0 + TB]),
            [dram_reg(x_in, t0, t0 + TB)], [xb.r()])
        for dh in range(CH):
            pb = psum[2 + dh % 2]
            pz = psum[4]
            P.pe(lambda nc, dh=dh, t0=t0, pz=pz: nc.tensor.matmul(
                pz.ap, sel_bf.ap[0:8, dh * 128:(dh + 1) * 128],
                s_zbf.ap[0:8, t0:t0 + TB], start=True, stop=True),
                [sel_bf.r(p1=8), s_zbf.r(p1=8, b0=t0 * 2, b1=(t0 + TB) * 2)],
                [pz.r()])
            zr = s_zr[dh % 2]
            qs = s_qs[dh % 2]
            P.act(lambda nc, pz=pz, zr=zr: nc.scalar.copy(zr.ap, pz.ap),
                  [pz.r()], [zr.r()])
            jsrc = 4 + dh if dh < 4 else dh - 4
            P.dve(lambda nc, jsrc=jsrc, t0=t0, zr=zr, qs=qs: nc.vector.tensor_tensor(
                qs.ap, s_q.ap[:, jsrc, t0:t0 + TB],
                zr.ap, op=ALU.mult),
                [s_q.r(b0=jsrc * T * 2, b1=(jsrc + 1) * T * 2), zr.r()],
                [qs.r()])
            for dt in range(3):
                for h in range(CH):
                    P.pe(lambda nc, dt=dt, h=h, dh=dh, t0=t0, pb=pb: nc.tensor.matmul(
                        pb.ap, wlep_bf.ap[:, dt * CH + h, dh * 128:(dh + 1) * 128],
                        s_y.ap[:, h, t0 + dt + 1:t0 + dt + 1 + TB],
                        start=(dt == 0 and h == 0), stop=False),
                        [wlep_bf.r(b0=((dt * CH + h) * 1024 + dh * 128) * 2,
                                   b1=((dt * CH + h) * 1024 + dh * 128 + 128) * 2),
                         s_y.r(b0=h * TP * 2, b1=(h + 1) * TP * 2)],
                        [pb.r()])
            P.pe(lambda nc, dh=dh, pb=pb, qs=qs: nc.tensor.matmul(
                pb.ap, s_kv.ap[:, dh, :], qs.ap,
                start=False, stop=True),
                [s_kv.r(b0=dh * 256, b1=(dh + 1) * 256), qs.r()],
                [pb.r()])
            P.dve(lambda nc, dh=dh, pb=pb, xb=xb: nc.vector.scalar_tensor_tensor(
                xb.ap[:, dh, :], pb.ap, lepb_sb.ap[:, dh:dh + 1], xb.ap[:, dh, :],
                op0=ALU.add, op1=ALU.add),
                [pb.r(), lepb_sb.r(), xb.r(b0=dh * TB * 4, b1=(dh + 1) * TB * 4)],
                [xb.r(b0=dh * TB * 4, b1=(dh + 1) * TB * 4)])
        P.dma('SYNC', 'xout', lambda nc, xb=xb, t0=t0: nc.sync.dma_start(
            out=xout_v[:, :, t0:t0 + TB], in_=xb.ap),
            [xb.r()], [dram_reg(x_out, t0, t0 + TB)])
    A.release()


def emit_ln_into(P, src, ydst, col0, xbf, m2buf, rstd, pm, pq, ones_bf, tb, tot,
                 mbf=None):
    for h in range(CH):
        P.pe(lambda nc, h=h: nc.tensor.matmul(
            pm.ap[:, 0:tb], ones_bf.ap, xbf.ap[:, h, :],
            start=(h == 0), stop=(h == CH - 1)),
            [xbf.r(b0=h * tb * 2, b1=(h + 1) * tb * 2), ones_bf.r()],
            [pm.r(b0=0, b1=tb * 4)])
    P.act(lambda nc: nc.scalar.activation(m2buf.ap[:, 0:tb], pm.ap[:, 0:tb], AF.Square),
          [pm.r(b0=0, b1=tb * 4)], [m2buf.r(b1=tb * 4)])
    if mbf is not None:
        P.act(lambda nc: nc.scalar.copy(mbf.ap[:, 0:tb], pm.ap[:, 0:tb]),
              [pm.r(b0=0, b1=tb * 4)], [mbf.r(b1=tb * 2)])
        for h in range(CH):
            ysl = ydst.ap[:, h, col0:col0 + tb]
            P.dve(lambda nc, h=h, ysl=ysl: nc.vector.tensor_tensor(
                ysl, xbf.ap[:, h, :], mbf.ap[:, 0:tb], op=ALU.subtract),
                [xbf.r(b0=h * tb * 2, b1=(h + 1) * tb * 2), mbf.r(b1=tb * 2)],
                [ydst.r()])
    P.act(lambda nc: nc.scalar.activation(xbf.ap, xbf.ap, AF.Square),
          [xbf.r()], [xbf.r()])
    for h in range(CH):
        P.pe(lambda nc, h=h: nc.tensor.matmul(
            pq.ap[:, 0:tb], ones_bf.ap, xbf.ap[:, h, :],
            start=(h == 0), stop=(h == CH - 1)),
            [xbf.r(b0=h * tb * 2, b1=(h + 1) * tb * 2), ones_bf.r()],
            [pq.r(b0=0, b1=tb * 4)])
    P.dve(lambda nc: nc.vector.scalar_tensor_tensor(
        rstd.ap[:, 0:tb], pq.ap[:, 0:tb], 1e-5, m2buf.ap[:, 0:tb],
        op0=ALU.add, op1=ALU.subtract),
        [pq.r(b0=0, b1=tb * 4), m2buf.r(b1=tb * 4)], [rstd.r(b1=tb * 4)])
    P.act(lambda nc: nc.scalar.activation(rstd.ap[:, 0:tb], rstd.ap[:, 0:tb], AF.Ln),
          [rstd.r(b1=tb * 4)], [rstd.r(b1=tb * 4)])
    if mbf is not None:
        P.act(lambda nc: nc.scalar.activation(
            mbf.ap[:, tb:2 * tb], rstd.ap[:, 0:tb], AF.Exp, scale=-0.5),
            [rstd.r(b1=tb * 4)], [mbf.r(b0=tb * 2)])
        for h in range(CH):
            ysl = ydst.ap[:, h, col0:col0 + tb]
            P.dve(lambda nc, h=h, ysl=ysl: nc.vector.tensor_tensor(
                ysl, ysl, mbf.ap[:, tb:2 * tb], op=ALU.mult),
                [ydst.r(), mbf.r(b0=tb * 2)], [ydst.r()])
    else:
        P.act(lambda nc: nc.scalar.activation(rstd.ap[:, 0:tb], rstd.ap[:, 0:tb], AF.Exp,
                                              scale=-0.5),
              [rstd.r(b1=tb * 4)], [rstd.r(b1=tb * 4)])
        ysl = ydst.ap[:, :, col0:col0 + tb]
        P.dve(lambda nc, ysl=ysl: nc.vector.tensor_tensor(
            ysl, src.ap, pm.ap[:, 0:tb].unsqueeze(1).broadcast_to([128, CH, tb]),
            op=ALU.subtract),
            [src.r(), pm.r(b0=0, b1=tb * 4)], [ydst.r()])
        P.dve(lambda nc, ysl=ysl: nc.vector.tensor_tensor(
            ysl, ysl, rstd.ap[:, 0:tb].unsqueeze(1).broadcast_to([128, CH, tb]),
            op=ALU.mult),
            [ydst.r(), rstd.r(b1=tb * 4)], [ydst.r()])


def host_prep_l1attn(qk_w, qk_b, lepe_w, lepe_b):
    perm = np.concatenate([np.arange(0, C, 2), np.arange(1, C, 2)])
    qw = qk_w[:, 0:C][:, perm]
    kw = qk_w[:, C:2 * C][:, perm]
    d = {}
    d["l1a_wqT"] = np.ascontiguousarray(qw)
    d["l1a_wkT"] = np.ascontiguousarray(kw)
    d["l1a_qb"] = np.ascontiguousarray(qk_b[0:C][perm].reshape(CH, 128).T)
    d["l1a_kb"] = np.ascontiguousarray(qk_b[C:2 * C][perm].reshape(CH, 128).T)
    d["l1a_wlep"] = np.ascontiguousarray(lepe_w.reshape(3 * C, C))
    d["l1a_lepb"] = np.ascontiguousarray(lepe_b.reshape(CH, 128).T)
    return d


def host_consts_l1attn():
    kmax = C // 2
    theta = 1.0 / (10000.0 ** (np.arange(kmax, dtype=np.float64) / kmax))
    ang = np.arange(T, dtype=np.float64)[:, None] * theta[None, :]
    cos = np.cos(ang).T.astype(np.float32)
    sin = np.sin(ang).T.astype(np.float32)
    def to_ct(a):
        return np.ascontiguousarray(a.reshape(4, 128, T).transpose(1, 0, 2)).astype(ml_dtypes.bfloat16)
    sel = np.zeros((8, C), dtype=np.float64)
    for e in range(8):
        sel[e, e * 128:(e + 1) * 128] = 1.0 / np.sqrt(T)
    return {"c_cos": to_ct(cos), "c_sin": to_ct(sin),
            "c_ident": np.concatenate([np.eye(128), np.vstack([np.eye(64), np.eye(64)])], 1).astype(ml_dtypes.bfloat16),
            "c_sel": sel.astype(ml_dtypes.bfloat16)}


def build_l1attn_only():
    nc = bass.Bass('TRN2', target_bir_lowering=False)
    x_in = nc.dram_tensor("x_in", [C, T], F32, kind="ExternalInput").ap()
    x_out = nc.dram_tensor("x_out", [C, T], F32, kind="ExternalOutput").ap()
    wq_d = WeightDram(nc, "l1a_wqT", C, C)
    wk_d = WeightDram(nc, "l1a_wkT", C, C)
    wlep_d = WeightDram(nc, "l1a_wlep", 3 * C, C)
    qb = nc.dram_tensor("l1a_qb", [128, CH], F32, kind="ExternalInput").ap()
    kb = nc.dram_tensor("l1a_kb", [128, CH], F32, kind="ExternalInput").ap()
    lepb = nc.dram_tensor("l1a_lepb", [128, CH], F32, kind="ExternalInput").ap()
    ones = nc.dram_tensor(CONST_ONES_SCALED, [128, 128], BF16, kind="ExternalInput").ap()
    ident = nc.dram_tensor("c_ident", [128, 192], BF16, kind="ExternalInput").ap()
    cosd = nc.dram_tensor("c_cos", [128, 4, T], BF16, kind="ExternalInput").ap()
    sind = nc.dram_tensor("c_sin", [128, 4, T], BF16, kind="ExternalInput").ap()
    seld = nc.dram_tensor("c_sel", [8, C], BF16, kind="ExternalInput").ap()
    with contextlib.ExitStack() as stack:
        A = stack.enter_context(Arena(nc))
        psum = [stack.enter_context(PsumTile(nc, i)) for i in range(5)] + [None, None]
        psum_bfs = [stack.enter_context(PsumTileBF(nc, 5)),
                    stack.enter_context(PsumTileBF(nc, 6))]
        psum.append(stack.enter_context(PsumTile(nc, 7)))
        P = Prog(nc)
        qb_sb = A.alloc("qbsb", [128, CH], F32)
        kb_sb = A.alloc("kbsb", [128, CH], F32)
        lepb_sb = A.alloc("lepbsb", [128, CH], F32)
        ones_bf = A.alloc("ones", [128, 128], BF16)
        ident_bf = A.alloc("ident", [128, 192], BF16)
        sel_bf = A.alloc("sel", [128, C], BF16)
        P.dma('SYNC', 'xin', lambda nc: nc.sync.dma_start(out=qb_sb.ap, in_=qb),
              [reg("l1a_qb")], [qb_sb.r()])
        P.dma('SYNC', 'xin', lambda nc: nc.sync.dma_start(out=kb_sb.ap, in_=kb),
              [reg("l1a_kb")], [kb_sb.r()])
        P.dma('SYNC', 'xin', lambda nc: nc.sync.dma_start(out=lepb_sb.ap, in_=lepb),
              [reg("l1a_lepb")], [lepb_sb.r()])
        P.dma('SYNC', 'xin', lambda nc: nc.sync.dma_start(out=ones_bf.ap, in_=ones),
              [reg(CONST_ONES_SCALED)], [ones_bf.r()])
        P.dma('SYNC', 'xin', lambda nc: nc.sync.dma_start(out=ident_bf.ap, in_=ident),
              [reg("c_ident")], [ident_bf.r()])
        P.dma('SYNC', 'xin', lambda nc: nc.sync.dma_start(out=sel_bf.ap[0:8, :], in_=seld),
              [reg("c_sel")], [sel_bf.r()])
        consts = (qb_sb, kb_sb, lepb_sb, ones_bf, ident_bf, cosd, sind, sel_bf)
        emit_l1attn(P, A, nc, x_in, x_out, wq_d, wk_d, wlep_d, consts, psum, psum_bfs)
        P.build()
        P.close()
    return nc


class PsumTileBF:
    def __init__(self, nc, i):
        self.cm = nc.psum_tensor(f"pbf{i}", [128, 1024], BF16)
        self.name = f"pbf{i}"

    def __enter__(self):
        t = self.cm.__enter__()
        self.ap = t[:]
        return self

    def __exit__(self, *a):
        return self.cm.__exit__(*a)

    def r(self, p0=0, p1=128, b0=0, b1=None):
        if b1 is None:
            b1 = 2048
        return (self.name, p0, p1, b0, b1)



HALF = 1536
HK = HALF // 128


def emit_l1ffn(P, A, nc, x_in, x_out, consts, psum, psum_bfs, acc_d, y_d, tag):
    TB = 512
    NB = T // TB
    (ones_bf, identf_d, ident_bf, sel1_d, gT_d,
     wg_d, wu_d, wd_d, sgg_d, sgu_d, sgd_d) = consts
    acc_v = acc_d.rearrange("p (h t) -> p h t", h=CH)
    y_v = y_d.rearrange("p (h t) -> p h t", h=CH)
    xin_v = x_in.rearrange("(h p) t -> p h t", p=128)
    xout_v = x_out.rearrange("(h p) t -> p h t", p=128)
    A.mark()
    s_identf = A.alloc(f"{tag}_idf", [128, 8], F32)
    s_sel1 = A.alloc(f"{tag}_sel1", [8, C], BF16)
    s_combT = A.alloc(f"{tag}_combT", [128, T], BF16)
    s_ys = [A.alloc(f"{tag}_ys{i}", [128, CH, TB], BF16) for i in range(3)]
    s_wg = [None, None]
    s_wu = [None, None]
    s_wg[0] = A.alloc(f"{tag}_wg0", [128, CH, HALF], BF16)
    s_wu[0] = A.alloc(f"{tag}_wu0", [128, CH, HALF], BF16)
    s_wd = A.alloc(f"{tag}_wd", [128, HK, C], BF16)
    s_h = A.alloc(f"{tag}_h", [128, HK, TB], BF16)
    s_sg = [A.alloc(f"{tag}_sg{i}", [128, TB], BF16) for i in range(4)]
    s_su = [A.alloc(f"{tag}_su{i}", [128, TB], BF16) for i in range(4)]
    s_cr = [A.alloc(f"{tag}_cr{i}", [128, TB], BF16) for i in range(2)]
    s_ab = [A.alloc(f"{tag}_ab{i}", [128, TB], F32) for i in range(4)]
    P.dma('SYNC', 'xin', lambda nc: nc.sync.dma_start(out=s_identf.ap, in_=identf_d),
          [reg("c_identf8")], [s_identf.r()])
    P.dma('SYNC', 'xin', lambda nc: nc.sync.dma_start(out=s_sel1.ap[0:8, :], in_=sel1_d),
          [reg("c_sel1")], [s_sel1.r()])

    def load_gu(e, par):
        if e == 0:
            load_weight(P, sgg_d, s_wg[par], stream="wt")
            load_weight(P, sgu_d, s_wu[par], stream="wtu")
        else:
            ex = e - 1
            for (wd_, dst, strm) in ((wg_d, s_wg[par], "wt"), (wu_d, s_wu[par], "wtu")):
                sv = wd_.view()
                P.dma('GP', strm, lambda nc, sv=sv, dst=dst, ex=ex: nc.gpsimd.dma_start(
                    out=dst.ap, in_=sv[:, ex * CH:(ex + 1) * CH, :]),
                    [reg(wd_.name)], [dst.r()])

    def load_wd(e):
        if e == 0:
            load_weight(P, sgd_d, s_wd, stream="wtd")
        else:
            ex = e - 1
            sv = wd_d.view()
            P.dma('GP', 'wtd', lambda nc, sv=sv, ex=ex: nc.gpsimd.dma_start(
                out=s_wd.ap, in_=sv[:, ex * HK:(ex + 1) * HK, :]),
                [reg(wd_d.name)], [s_wd.r()])

    def expert_body(e):
        par = e % 2
        wg, wu = s_wg[par], s_wu[par]
        for b in range(NB):
            t0 = b * TB
            ys = s_ys[(e * NB + b) % 3]
            P.dma('SYNC', 'xin', lambda nc, t0=t0, ys=ys: nc.sync.dma_start(
                out=ys.ap, in_=y_v[:, :, t0:t0 + TB]),
                [(y_d.tensor.name, 0, 128, 0, 1 << 40)], [ys.r()])
            cr = s_cr[e % 2]
            if e > 0:
                pz = psum[4]
                P.pe(lambda nc, e=e, t0=t0, pz=pz: nc.tensor.matmul(
                    pz.ap, s_sel1.ap[0:8, (e - 1) * 128:e * 128],
                    s_combT.ap[0:8, t0:t0 + TB], start=True, stop=True),
                    [s_sel1.r(p1=8), s_combT.r(p1=8, b0=t0 * 2, b1=(t0 + TB) * 2)],
                    [pz.r()])
                P.act(lambda nc, pz=pz, cr=cr: nc.scalar.copy(cr.ap, pz.ap),
                      [pz.r()], [cr.r()])
            for mt in range(HK):
                pg, pu = psum[2], psum[3]
                for h in range(CH):
                    P.pe(lambda nc, h=h, mt=mt, pg=pg, wg=wg, ys=ys: nc.tensor.matmul(
                        pg.ap, wg.ap[:, h, mt * 128:(mt + 1) * 128],
                        ys.ap[:, h, :],
                        start=(h == 0), stop=(h == CH - 1)),
                        [wg.r(b0=(h * HALF + mt * 128) * 2, b1=(h * HALF + mt * 128 + 128) * 2),
                         ys.r(b0=h * TB * 2, b1=(h + 1) * TB * 2)],
                        [pg.r()])
                for h in range(CH):
                    P.pe(lambda nc, h=h, mt=mt, pu=pu, wu=wu, ys=ys: nc.tensor.matmul(
                        pu.ap, wu.ap[:, h, mt * 128:(mt + 1) * 128],
                        ys.ap[:, h, :],
                        start=(h == 0), stop=(h == CH - 1)),
                        [wu.r(b0=(h * HALF + mt * 128) * 2, b1=(h * HALF + mt * 128 + 128) * 2),
                         ys.r(b0=h * TB * 2, b1=(h + 1) * TB * 2)],
                        [pu.r()])
                sg = s_sg[mt % 4]
                su = s_su[mt % 4]
                P.act(lambda nc, pg=pg, sg=sg: nc.scalar.activation(sg.ap, pg.ap, AF.Silu),
                      [pg.r()], [sg.r()])
                if e > 0:
                    P.dve(lambda nc, pu=pu, su=su, cr=cr: nc.vector.tensor_tensor(
                        su.ap, pu.ap, cr.ap, op=ALU.mult),
                        [pu.r(), cr.r()], [su.r()])
                else:
                    P.act(lambda nc, pu=pu, su=su: nc.scalar.copy(su.ap, pu.ap),
                          [pu.r()], [su.r()])
                P.dve(lambda nc, mt=mt, sg=sg, su=su: nc.vector.tensor_tensor(
                    s_h.ap[:, mt, :], sg.ap, su.ap, op=ALU.mult),
                    [sg.r(), su.r()],
                    [s_h.r(b0=mt * TB * 2, b1=(mt + 1) * TB * 2)])
            for dg in range(4):
                for (di, pd) in ((0, psum[4]), (1, psum[7])):
                    dh = dg * 2 + di
                    for hk in range(HK):
                        P.pe(lambda nc, hk=hk, dh=dh, pd=pd: nc.tensor.matmul(
                            pd.ap, s_wd.ap[:, hk, dh * 128:(dh + 1) * 128],
                            s_h.ap[:, hk, :],
                            start=(hk == 0), stop=(hk == HK - 1)),
                            [s_wd.r(b0=(hk * C + dh * 128) * 2, b1=(hk * C + dh * 128 + 128) * 2),
                             s_h.r(b0=hk * TB * 2, b1=(hk + 1) * TB * 2)],
                            [pd.r()])
                    ab = s_ab[dh % 4]
                    P.act(lambda nc, pd=pd, ab=ab: nc.scalar.copy(ab.ap, pd.ap),
                          [pd.r()], [ab.r()])
                    P.dma('GP', 'acc', lambda nc, e=e, dh=dh, t0=t0, ab=ab: nc.gpsimd.dma_start(
                        out=acc_v[:, dh, t0:t0 + TB], in_=ab.ap,
                        accum_op=(ALU.bypass if e == 0 else ALU.add)),
                        [ab.r()],
                        [(acc_d.tensor.name, 0, 128, (dh * T + t0) * 4, (dh * T + t0 + TB) * 4)])

    load_gu(0, 0)
    load_wd(0)
    A.mark()
    s_y = A.alloc(f"{tag}_y", [128, CH, T], BF16)
    A.mark()
    s_xbf = [A.alloc(f"{tag}_xbf{i}", [128, CH, TB], BF16) for i in range(2)]
    s_m2 = A.alloc(f"{tag}_m2", [128, TB], F32)
    s_rstd = A.alloc(f"{tag}_rstd", [128, TB], F32)
    s_mbf = A.alloc(f"{tag}_mbf", [128, 2 * TB], BF16)
    for b in range(NB):
        t0 = b * TB
        xbf = s_xbf[b % 2]
        P.dma('GP', 'xbf', lambda nc, t0=t0, xbf=xbf: nc.gpsimd.dma_start(
            out=xbf.ap, in_=xin_v[:, :, t0:t0 + TB]),
            [dram_reg(x_in, t0, t0 + TB)], [xbf.r()])
        emit_ln_into(P, None, s_y, t0, xbf, s_m2, s_rstd, psum[0], psum[1],
                     ones_bf, TB, T, mbf=s_mbf)
        P.dma('SYNC', 'yd', lambda nc, t0=t0: nc.sync.dma_start(
            out=y_v[:, :, t0:t0 + TB], in_=s_y.ap[:, :, t0:t0 + TB]),
            [s_y.r(b0=t0 * 2)], [(y_d.tensor.name, 0, 128, 0, 1 << 40)])
    A.release()

    expert_body(0)
    load_wd(1)

    s_gt = A.alloc(f"{tag}_gt", [128, CH, CH], BF16)
    s_lg = A.alloc(f"{tag}_lg", [128, T], F32)
    s_lgt = A.alloc(f"{tag}_lgt", [128, 16, CH], F32)
    s_m1 = A.alloc(f"{tag}_m1", [128, 16], F32)
    s_mask = A.alloc(f"{tag}_msk", [128, 16, CH], U8)
    s_mask2 = A.alloc(f"{tag}_msk2", [128, 16, CH], U8)
    s_cum = A.alloc(f"{tag}_cum", [128, 16, CH], F32)
    s_l2 = A.alloc(f"{tag}_l2", [128, 16, CH], F32)
    s_w12 = A.alloc(f"{tag}_w12", [128, 16, 2], F32)
    s_comb = A.alloc(f"{tag}_comb", [128, 16, CH], BF16)
    s_cmp = A.alloc(f"{tag}_cmp", [128, 16, CH], U8)
    s_f1 = A.alloc(f"{tag}_f1", [128, 16, CH], F32)
    s_f2 = A.alloc(f"{tag}_f2", [128, 16, CH], F32)
    load_weight(P, gT_d, s_gt, stream="wt")
    for b in range(NB):
        t0 = b * TB
        pb = psum[0]
        for h in range(CH):
            P.pe(lambda nc, h=h, t0=t0, pb=pb: nc.tensor.matmul(
                pb.ap[0:8, 0:TB], s_gt.ap[:, h, :], s_y.ap[:, h, t0:t0 + TB],
                start=(h == 0), stop=(h == CH - 1)),
                [s_gt.r(), s_y.r(b0=(h * T + t0) * 2, b1=(h * T + t0 + TB) * 2)],
                [pb.r(p1=8)])
        P.act(lambda nc, t0=t0, pb=pb: nc.scalar.copy(
            s_lg.ap[0:8, t0:t0 + TB], pb.ap[0:8, 0:TB]),
            [pb.r(p1=8)], [s_lg.r(p1=8, b0=t0 * 4, b1=(t0 + TB) * 4)])
        for tq in range(4):
            q = b * 4 + tq
            pt = psum[0 if tq % 2 == 0 else 1]
            P.pe(lambda nc, t0=t0, tq=tq, pt=pt: nc.tensor.transpose(
                pt.ap[:, 128:136], s_lg.ap[0:8, t0 + tq * 128:t0 + (tq + 1) * 128],
                s_identf.ap[0:8, 0:8]),
                [s_lg.r(p1=8), s_identf.r(p1=8)], [pt.r(b0=512, b1=544)])
            P.act(lambda nc, q=q, pt=pt: nc.scalar.copy(
                s_lgt.ap[:, q, :], pt.ap[:, 128:136]),
                [pt.r(b0=512, b1=544)], [s_lgt.r(b0=q * 32, b1=(q + 1) * 32)])

    def pick_first(mask_c, dst_f):
        for q in range(16):
            P.dve(lambda nc, q=q, mask_c=mask_c: nc.vector.tensor_tensor_scan(
                s_cum.ap[:, q, :], mask_c.ap[:, q, :], mask_c.ap[:, q, :], 0.0,
                op0=ALU.add, op1=ALU.bypass),
                [mask_c.r()], [s_cum.r()])
        P.dve(lambda nc: nc.vector.tensor_single_scalar(
            s_cmp.ap, s_cum.ap, 1.5, op=ALU.is_lt),
            [s_cum.r()], [s_cmp.r()])
        P.dve(lambda nc, mask_c=mask_c, dst_f=dst_f: nc.vector.tensor_tensor(
            dst_f.ap, mask_c.ap, s_cmp.ap, op=ALU.mult),
            [mask_c.r(), s_cmp.r()], [dst_f.r()])

    P.dve(lambda nc: nc.vector.tensor_reduce(
        s_m1.ap.unsqueeze(2), s_lgt.ap, axis=AX.X, op=ALU.max),
        [s_lgt.r()], [s_m1.r()])
    m1b = s_m1.ap.unsqueeze(2).broadcast_to([128, 16, CH])
    P.dve(lambda nc, m1b=m1b: nc.vector.tensor_tensor(
        s_mask.ap, s_lgt.ap, m1b, op=ALU.is_ge),
        [s_lgt.r(), s_m1.r()], [s_mask.r()])
    pick_first(s_mask, s_f1)
    P.dve(lambda nc: nc.vector.tensor_scalar(
        s_cum.ap, s_f1.ap, -1.0, 1.0, op0=ALU.mult, op1=ALU.add),
        [s_f1.r()], [s_cum.r()])
    P.dve(lambda nc: nc.vector.tensor_tensor(
        s_l2.ap, s_lgt.ap, s_cum.ap, op=ALU.mult),
        [s_lgt.r(), s_cum.r()], [s_l2.r()])
    P.dve(lambda nc: nc.vector.tensor_scalar(
        s_cum.ap, s_f1.ap, NEG, None, op0=ALU.mult),
        [s_f1.r()], [s_cum.r()])
    P.dve(lambda nc: nc.vector.tensor_tensor(
        s_l2.ap, s_l2.ap, s_cum.ap, op=ALU.add),
        [s_l2.r(), s_cum.r()], [s_l2.r()])
    P.dve(lambda nc: nc.vector.tensor_reduce(
        s_w12.ap[:, :, 1:2], s_l2.ap, axis=AX.X, op=ALU.max),
        [s_l2.r()], [s_w12.r()])
    m2b = s_w12.ap[:, :, 1:2].broadcast_to([128, 16, CH])
    P.dve(lambda nc, m2b=m2b: nc.vector.tensor_tensor(
        s_mask2.ap, s_l2.ap, m2b, op=ALU.is_ge),
        [s_l2.r(), s_w12.r()], [s_mask2.r()])
    pick_first(s_mask2, s_f2)
    P.dve(lambda nc: nc.vector.tensor_tensor(
        s_w12.ap[:, :, 1:2], s_w12.ap[:, :, 1:2], s_m1.ap.unsqueeze(2),
        op=ALU.subtract),
        [s_w12.r(), s_m1.r()], [s_w12.r()])
    P.act(lambda nc: nc.scalar.activation(
        s_w12.ap[:, :, 1:2], s_w12.ap[:, :, 1:2], AF.Exp),
        [s_w12.r()], [s_w12.r()])
    P.dve(lambda nc: nc.vector.tensor_scalar(
        s_w12.ap[:, :, 0:1], s_w12.ap[:, :, 1:2], 1.0, None, op0=ALU.add),
        [s_w12.r()], [s_w12.r()])
    P.dve(lambda nc: nc.vector.reciprocal(s_w12.ap[:, :, 0:1], s_w12.ap[:, :, 0:1]),
          [s_w12.r()], [s_w12.r()])
    P.dve(lambda nc: nc.vector.tensor_tensor(
        s_w12.ap[:, :, 1:2], s_w12.ap[:, :, 1:2], s_w12.ap[:, :, 0:1],
        op=ALU.mult),
        [s_w12.r()], [s_w12.r()])
    P.dve(lambda nc: nc.vector.tensor_tensor(
        s_f1.ap, s_f1.ap, s_w12.ap[:, :, 0:1].broadcast_to([128, 16, CH]),
        op=ALU.mult),
        [s_f1.r(), s_w12.r()], [s_f1.r()])
    P.dve(lambda nc: nc.vector.tensor_tensor(
        s_f2.ap, s_f2.ap, s_w12.ap[:, :, 1:2].broadcast_to([128, 16, CH]),
        op=ALU.mult),
        [s_f2.r(), s_w12.r()], [s_f2.r()])
    P.dve(lambda nc: nc.vector.tensor_tensor(
        s_comb.ap, s_f1.ap, s_f2.ap, op=ALU.add),
        [s_f1.r(), s_f2.r()], [s_comb.r()])
    for q in range(16):
        pt = psum_bfs[q % 2]
        P.pe(lambda nc, q=q, pt=pt: nc.tensor.transpose(
            pt.ap[0:8, 0:128], s_comb.ap[:, q, :], ident_bf.ap[:, 0:128]),
            [s_comb.r(), ident_bf.r()], [pt.r(p1=8, b1=256)])
        P.act(lambda nc, q=q, pt=pt: nc.scalar.copy(
            s_combT.ap[0:8, q * 128:(q + 1) * 128], pt.ap[0:8, 0:128]),
            [pt.r(p1=8, b1=256)], [s_combT.r(p1=8, b0=q * 256, b1=(q + 1) * 256)])
    A.release()

    A.mark()
    s_wg[1] = A.alloc(f"{tag}_wg1", [128, CH, HALF], BF16)
    s_wu[1] = A.alloc(f"{tag}_wu1", [128, CH, HALF], BF16)
    load_gu(1, 1)
    for e in range(1, 9):
        if e + 1 < 9:
            load_gu(e + 1, (e + 1) % 2)
        expert_body(e)
        if e + 1 < 9:
            load_wd(e + 1)

    A.release()
    s_xf = A.alloc(f"{tag}_xf", [128, CH, TB], F32)
    s_af = A.alloc(f"{tag}_af", [128, CH, TB], F32)
    for b in range(NB):
        t0 = b * TB
        P.dma('SYNC', 'xin', lambda nc, t0=t0: nc.sync.dma_start(
            out=s_xf.ap, in_=xin_v[:, :, t0:t0 + TB]),
            [dram_reg(x_in, t0, t0 + TB)], [s_xf.r()])
        P.dma('SYNC', 'xin', lambda nc, t0=t0: nc.sync.dma_start(
            out=s_af.ap, in_=acc_v[:, :, t0:t0 + TB]),
            [(acc_d.tensor.name, 0, 128, 0, 1 << 40)], [s_af.r()])
        P.dve(lambda nc: nc.vector.scalar_tensor_tensor(
            s_xf.ap, s_af.ap, 0.5, s_xf.ap, op0=ALU.mult, op1=ALU.add),
            [s_af.r(), s_xf.r()], [s_xf.r()])
        P.dma('SYNC', 'xout', lambda nc, t0=t0: nc.sync.dma_start(
            out=xout_v[:, :, t0:t0 + TB], in_=s_xf.ap),
            [s_xf.r()], [dram_reg(x_out, t0, t0 + TB)])
    A.release()


def host_prep_l1ffn(gate_w, w_up, w_down, sg_gate, sg_up, sg_down):
    E = 8
    d = {}
    d["l1f_gT"] = np.ascontiguousarray(gate_w.T)
    wg = np.ascontiguousarray(w_up[:, :HALF, :].transpose(0, 2, 1))
    wu = np.ascontiguousarray(w_up[:, HALF:, :].transpose(0, 2, 1))
    wd = np.ascontiguousarray(w_down.transpose(0, 2, 1))
    d["l1f_wg"] = wg.reshape(E * C, HALF)
    d["l1f_wu"] = wu.reshape(E * C, HALF)
    d["l1f_wd"] = wd.reshape(E * HALF, C)
    d["l1f_sgg"] = np.ascontiguousarray(sg_gate.T)
    d["l1f_sgu"] = np.ascontiguousarray(sg_up.T)
    d["l1f_sgd"] = np.ascontiguousarray(sg_down.T)
    return d


def host_consts_l1ffn():
    sel1 = np.zeros((8, C), dtype=np.float64)
    for e in range(8):
        sel1[e, e * 128:(e + 1) * 128] = 1.0
    return {"c_sel1": sel1.astype(ml_dtypes.bfloat16),
            "c_identf8": np.eye(128, 8, dtype=np.float32)[:128, :8].copy()}


def build_l1ffn_only():
    nc = bass.Bass('TRN2', target_bir_lowering=False)
    x_in = nc.dram_tensor("x_in", [C, T], F32, kind="ExternalInput").ap()
    x_out = nc.dram_tensor("x_out", [C, T], F32, kind="ExternalOutput").ap()
    acc_d = nc.dram_tensor("l1f_acc", [128, CH * T], F32, kind="Internal").ap()
    y_d = nc.dram_tensor("l1f_yd", [128, CH * T], BF16, kind="Internal").ap()
    gT_d = WeightDram(nc, "l1f_gT", C, CH)
    wg_d = WeightDram(nc, "l1f_wg", 8 * C, HALF)
    wu_d = WeightDram(nc, "l1f_wu", 8 * C, HALF)
    wd_d = WeightDram(nc, "l1f_wd", 8 * HALF, C)
    sgg_d = WeightDram(nc, "l1f_sgg", C, HALF)
    sgu_d = WeightDram(nc, "l1f_sgu", C, HALF)
    sgd_d = WeightDram(nc, "l1f_sgd", HALF, C)
    ones = nc.dram_tensor(CONST_ONES_SCALED, [128, 128], BF16, kind="ExternalInput").ap()
    ident = nc.dram_tensor("c_ident", [128, 192], BF16, kind="ExternalInput").ap()
    identf = nc.dram_tensor("c_identf8", [128, 8], F32, kind="ExternalInput").ap()
    sel1 = nc.dram_tensor("c_sel1", [8, C], BF16, kind="ExternalInput").ap()
    with contextlib.ExitStack() as stack:
        A = stack.enter_context(Arena(nc))
        psum = [stack.enter_context(PsumTile(nc, i)) for i in range(5)] + [None, None]
        psum_bfs = [stack.enter_context(PsumTileBF(nc, 5)),
                    stack.enter_context(PsumTileBF(nc, 6))]
        psum.append(stack.enter_context(PsumTile(nc, 7)))
        P = Prog(nc)
        ones_bf = A.alloc("ones", [128, 128], BF16)
        ident_bf = A.alloc("ident", [128, 192], BF16)
        P.dma('SYNC', 'xin', lambda nc: nc.sync.dma_start(out=ones_bf.ap, in_=ones),
              [reg(CONST_ONES_SCALED)], [ones_bf.r()])
        P.dma('SYNC', 'xin', lambda nc: nc.sync.dma_start(out=ident_bf.ap, in_=ident),
              [reg("c_ident")], [ident_bf.r()])
        consts = (ones_bf, identf, ident_bf, sel1, gT_d,
                  wg_d, wu_d, wd_d, sgg_d, sgu_d, sgd_d)
        emit_l1ffn(P, A, nc, x_in, x_out, consts, psum, psum_bfs, acc_d, y_d, "l1f")
        P.build()
        P.close()
    return nc



def build_fused():
    nc = bass.Bass('TRN2', target_bir_lowering=False)
    x_in = nc.dram_tensor("x_in", [C, T], F32, kind="ExternalInput").ap()
    x_out = nc.dram_tensor("x_out", [C, T], F32, kind="ExternalOutput").ap()
    xA = nc.dram_tensor("xA", [C, T], F32, kind="Internal").ap()
    xB = nc.dram_tensor("xB", [C, T], F32, kind="Internal").ap()
    xC = nc.dram_tensor("xC", [C, T], F32, kind="Internal").ap()
    xD = nc.dram_tensor("xD", [C, T], F32, kind="Internal").ap()
    xE = nc.dram_tensor("xE", [C, T], F32, kind="Internal").ap()
    acc1 = nc.dram_tensor("l1f_acc1", [128, CH * T], F32, kind="Internal").ap()
    acc2 = nc.dram_tensor("l1f_acc2", [128, CH * T], F32, kind="Internal").ap()
    yd1 = nc.dram_tensor("l1f_yd1", [128, CH * T], BF16, kind="Internal").ap()
    yd2 = nc.dram_tensor("l1f_yd2", [128, CH * T], BF16, kind="Internal").ap()
    l0f_wr = WeightDram(nc, "l0f_WrT", C, C)
    l0f_wk = WeightDram(nc, "l0f_WkT", C, C)
    l0f_wv = WeightDram(nc, "l0f_WvT", C, C)
    mu = nc.dram_tensor("l0f_mu", [128, 2, CH], F32, kind="ExternalInput").ap()
    l0a_wk = WeightDram(nc, "l0a_WkT", C, C)
    l0a_wv = WeightDram(nc, "l0a_WvT", C, C)
    l0a_wr = WeightDram(nc, "l0a_WrT", C, C)
    l0a_wo = WeightDram(nc, "l0a_WoT", C, C)
    td = nc.dram_tensor("l0a_td", [128, CH], F32, kind="ExternalInput").ap()
    tf = nc.dram_tensor("l0a_tf", [128, CH], F32, kind="ExternalInput").ap()
    tm = nc.dram_tensor("l0a_tm", [128, 3, CH], F32, kind="ExternalInput").ap()
    wq_d = WeightDram(nc, "l1a_wqT", C, C)
    wk_d = WeightDram(nc, "l1a_wkT", C, C)
    wlep_d = WeightDram(nc, "l1a_wlep", 3 * C, C)
    qb = nc.dram_tensor("l1a_qb", [128, CH], F32, kind="ExternalInput").ap()
    kb = nc.dram_tensor("l1a_kb", [128, CH], F32, kind="ExternalInput").ap()
    lepb = nc.dram_tensor("l1a_lepb", [128, CH], F32, kind="ExternalInput").ap()
    gT_d = WeightDram(nc, "l1f_gT", C, CH)
    wg_d = WeightDram(nc, "l1f_wg", 8 * C, HALF)
    wu_d = WeightDram(nc, "l1f_wu", 8 * C, HALF)
    wd_d = WeightDram(nc, "l1f_wd", 8 * HALF, C)
    sgg_d = WeightDram(nc, "l1f_sgg", C, HALF)
    sgu_d = WeightDram(nc, "l1f_sgu", C, HALF)
    sgd_d = WeightDram(nc, "l1f_sgd", HALF, C)
    ones = nc.dram_tensor(CONST_ONES_SCALED, [128, 128], BF16, kind="ExternalInput").ap()
    ident = nc.dram_tensor("c_ident", [128, 192], BF16, kind="ExternalInput").ap()
    identf = nc.dram_tensor("c_identf8", [128, 8], F32, kind="ExternalInput").ap()
    sel1 = nc.dram_tensor("c_sel1", [8, C], BF16, kind="ExternalInput").ap()
    seld = nc.dram_tensor("c_sel", [8, C], BF16, kind="ExternalInput").ap()
    cosd = nc.dram_tensor("c_cos", [128, 4, T], BF16, kind="ExternalInput").ap()
    sind = nc.dram_tensor("c_sin", [128, 4, T], BF16, kind="ExternalInput").ap()
    with contextlib.ExitStack() as stack:
        A = stack.enter_context(Arena(nc))
        psum = [stack.enter_context(PsumTile(nc, i)) for i in range(5)] + [None, None]
        psum_bfs = [stack.enter_context(PsumTileBF(nc, 5)),
                    stack.enter_context(PsumTileBF(nc, 6))]
        psum.append(stack.enter_context(PsumTile(nc, 7)))
        P = Prog(nc)
        ones_bf = A.alloc("c_ones", [128, 128], BF16)
        ident_bf = A.alloc("c_id", [128, 192], BF16)
        mu_sb = A.alloc("c_mu", [128, 2, CH], F32)
        td_sb = A.alloc("c_td", [128, CH], F32)
        tf_sb = A.alloc("c_tf", [128, CH], F32)
        tm_sb = A.alloc("c_tm", [128, 3, CH], F32)
        qb_sb = A.alloc("c_qb", [128, CH], F32)
        kb_sb = A.alloc("c_kb", [128, CH], F32)
        lepb_sb = A.alloc("c_lepb", [128, CH], F32)
        sel_bf = A.alloc("c_selb", [128, C], BF16)
        for (dn, cv, sl8) in ((CONST_ONES_SCALED, ones_bf, False), ("c_ident", ident_bf, False),
                              ("l0f_mu", mu_sb, False), ("l0a_td", td_sb, False),
                              ("l0a_tf", tf_sb, False), ("l0a_tm", tm_sb, False),
                              ("l1a_qb", qb_sb, False), ("l1a_kb", kb_sb, False),
                              ("l1a_lepb", lepb_sb, False)):
            dv = {CONST_ONES_SCALED: ones, "c_ident": ident, "l0f_mu": mu,
                  "l0a_td": td, "l0a_tf": tf, "l0a_tm": tm, "l1a_qb": qb,
                  "l1a_kb": kb, "l1a_lepb": lepb}[dn]
            P.dma('SYNC', 'xin', lambda nc, dv=dv, cv=cv: nc.sync.dma_start(
                out=cv.ap, in_=dv), [reg(dn)], [cv.r()])
        P.dma('SYNC', 'xin', lambda nc: nc.sync.dma_start(out=sel_bf.ap[0:8, :], in_=seld),
              [reg("c_sel")], [sel_bf.r()])

        def sec_l0ffn(xi, xo):
            A.mark()
            wr_bf = A.alloc("s_wrbf", [128, CH, C], BF16)
            wk_bf = A.alloc("s_wkbf", [128, CH, C], BF16)
            wv_bf = A.alloc("s_wvbf", [128, CH, C], BF16)
            load_weight(P, l0f_wr, wr_bf)
            load_weight(P, l0f_wk, wk_bf)
            load_weight(P, l0f_wv, wv_bf)
            emit_l0ffn(P, A, nc, xi, xo, wr_bf, wk_bf, wv_bf, mu_sb, ones_bf,
                       psum, True)
            A.release()

        def sec_l0attn(xi, xo):
            A.mark()
            wk_bf = A.alloc("a_wkbf", [128, CH, C], BF16)
            wv_bf = A.alloc("a_wvbf", [128, CH, C], BF16)
            wr_bf = A.alloc("a_wrbf", [128, CH, C], BF16)
            wo_bf = A.alloc("a_wobf", [128, CH, C], BF16)
            load_weight(P, l0a_wk, wk_bf)
            load_weight(P, l0a_wv, wv_bf)
            load_weight(P, l0a_wr, wr_bf)
            load_weight(P, l0a_wo, wo_bf)
            emit_l0attn(P, A, nc, xi, xo, wk_bf, wv_bf, wr_bf, wo_bf, tm_sb,
                        td_sb, tf_sb, ones_bf, psum)
            A.release()

        def sec_l1ffn(xi, xo, acc_d, y_d, tag):
            consts = (ones_bf, identf, ident_bf, sel1, gT_d,
                      wg_d, wu_d, wd_d, sgg_d, sgu_d, sgd_d)
            emit_l1ffn(P, A, nc, xi, xo, consts, psum, psum_bfs, acc_d, y_d, tag)

        def sec_l1attn(xi, xo):
            consts = (qb_sb, kb_sb, lepb_sb, ones_bf, ident_bf, cosd, sind, sel_bf)
            emit_l1attn(P, A, nc, xi, xo, wq_d, wk_d, wlep_d, consts, psum, psum_bfs)

        sec_l0ffn(x_in, xA)
        sec_l0attn(xA, xB)
        sec_l0ffn(xB, xC)
        sec_l1ffn(xC, xD, acc1, yd1, "m1")
        sec_l1attn(xD, xE)
        sec_l1ffn(xE, x_out, acc2, yd2, "m2")
        P.build()
        P.close()
    return nc


FUSED = True


def _host_all(inputs):
    wd = {}
    wd.update(host_prep_l0ffn(inputs["L0_ffn_W_r"], inputs["L0_ffn_W_k"],
                              inputs["L0_ffn_W_v"], inputs["L0_ffn_mu_r"],
                              inputs["L0_ffn_mu_k"]))
    wd.update(host_prep_l0attn(inputs["L0_attn_Wk"], inputs["L0_attn_Wv"],
                               inputs["L0_attn_Wr"], inputs["L0_attn_Wo"],
                               np.asarray(inputs["L0_attn_time_decay"]).reshape(-1),
                               np.asarray(inputs["L0_attn_time_first"]).reshape(-1),
                               np.asarray(inputs["L0_attn_time_mix_k"]).reshape(-1),
                               np.asarray(inputs["L0_attn_time_mix_v"]).reshape(-1),
                               np.asarray(inputs["L0_attn_time_mix_r"]).reshape(-1)))
    wd.update(host_prep_l1attn(inputs["L1_attn_qk_w"], inputs["L1_attn_qk_b"],
                               inputs["L1_attn_lepe_w"], inputs["L1_attn_lepe_b"]))
    wd.update(host_prep_l1ffn(inputs["L1_ffn_gate_w"], inputs["L1_ffn_w_up"],
                              inputs["L1_ffn_w_down"], inputs["L1_ffn_sg_gate"],
                              inputs["L1_ffn_sg_up"], inputs["L1_ffn_sg_down"]))
    cs = host_consts()
    cs.update(host_consts_l1attn())
    cs.update(host_consts_l1ffn())
    wd.update(cs)
    return wd


_BUILD_CACHE = {}


def _get(name, fn):
    if name not in _BUILD_CACHE:
        _BUILD_CACHE[name] = fn()
    return _BUILD_CACHE[name]


def kernel(**inputs):
    host = _host_all(inputs)
    x = np.asarray(inputs["L0_attn_x"], dtype=np.float32)
    shards = [np.ascontiguousarray(x[b].T) for b in range(B)]
    cores = list(range(B))

    def run(nc, extra):
        in_maps = []
        for b in range(B):
            m = {"x_in": shards[b]}
            for k in extra:
                m[k] = host[k]
            in_maps.append(m)
        res = run_bass_kernel_spmd(nc, in_maps, core_ids=cores)
        for b in range(B):
            shards[b] = np.ascontiguousarray(res.results[b]["x_out"])

    l0f_keys = ["l0f_WrT", "l0f_WkT", "l0f_WvT", "l0f_mu", CONST_ONES_SCALED]
    l0a_keys = ["l0a_WkT", "l0a_WvT", "l0a_WrT", "l0a_WoT", "l0a_td", "l0a_tf",
                "l0a_tm", CONST_ONES_SCALED]
    l1f_keys = ["l1f_gT", "l1f_wg", "l1f_wu", "l1f_wd", "l1f_sgg", "l1f_sgu",
                "l1f_sgd", CONST_ONES_SCALED, "c_ident", "c_identf8", "c_sel1"]
    l1a_keys = ["l1a_wqT", "l1a_wkT", "l1a_wlep", "l1a_qb", "l1a_kb", "l1a_lepb",
                CONST_ONES_SCALED, "c_ident", "c_cos", "c_sin", "c_sel"]

    if FUSED:
        all_keys = sorted(set(l0f_keys + l0a_keys + l1f_keys + l1a_keys))
        run(_get("fused", build_fused), all_keys)
    else:
        run(_get("l0ffn", build_l0ffn_only), l0f_keys)
        run(_get("l0attn", build_l0attn_only), l0a_keys)
        run(_get("l0ffn", build_l0ffn_only), l0f_keys)
        run(_get("l1ffn", build_l1ffn_only), l1f_keys)
        run(_get("l1attn", build_l1attn_only), l1a_keys)
        run(_get("l1ffn", build_l1ffn_only), l1f_keys)

    return np.stack([s.T for s in shards], 0)
